# version 11
# baseline (speedup 1.0000x reference)
import contextlib
import numpy as np
import ml_dtypes
import concourse.bass as bass
import concourse.mybir as mybir
from concourse.bass_utils import run_bass_kernel_spmd

F32 = mybir.dt.float32
BF16 = mybir.dt.bfloat16
AF = mybir.ActivationFunctionType
ALU = mybir.AluOpType
AX = mybir.AxisListType.X
bf16 = ml_dtypes.bfloat16

NCORES = 8
B, L, D = 2, 8192, 1024
TOK = 2048
HALO = 648
XT = HALO + TOK
OWN0 = HALO
NH, DS, DH, DI = 16, 64, 128, 2048
ZD = 4112
NCH = 16
HEADS, DHEAD, INNER = 8, 64, 512
WIN = 512
HID, HIDP = 2730, 2816
NQS = 17
NKB = 21
KCOL0 = HALO - 640
RMS_EPS = 1.1920928955078125e-07
CONST_VALS = (1e-5, 1.0, -1.0, RMS_EPS, 1e-12, 1e-30, 0.5)
DEV_DEBUG = False


class Sem:
    def __init__(self, handle, dma=False):
        self.h = handle
        self.n = 0
        self.dma = dma


class Prog:
    ENGS = ("sync", "tensor", "vector", "scalar", "gpsimd")

    def __init__(self, nc):
        self.nc = nc
        self.steps = {e: [] for e in self.ENGS}
        self.last_wait = {e: {} for e in self.ENGS}

    def on(self, eng, fn):
        self.steps[eng].append(fn)

    def wait(self, eng, sem, val=None):
        if val is None or sem.dma:
            val = sem.n
        if val <= 0:
            return
        lw = self.last_wait[eng]
        if lw.get(id(sem), -1) >= val:
            return
        lw[id(sem)] = val
        h = sem.h
        self.on(eng, lambda e, h=h, v=val: e.wait_ge(h, v))

    def emit(self, eng, emitfn, waits=(), inc=None, inc_amt=1):
        for w in waits:
            s, v = w if isinstance(w, tuple) else (w, None)
            self.wait(eng, s, v)
        if inc is not None and inc.dma:
            self.wait(eng, inc, inc.n)
        if inc is not None:
            inc.n += inc_amt
            h, amt = inc.h, inc_amt
            self.on(eng, lambda e, f=emitfn, h=h, a=amt: f(e).then_inc(h, a))
            return inc.n
        self.on(eng, lambda e, f=emitfn: f(e))
        return None

    def drain(self, eng):
        self.on(eng, lambda e: e.drain())

    def run(self, block):
        for eng in self.ENGS:
            steps = self.steps[eng]
            getattr(block, eng)(lambda e, steps=steps: [f(e) for f in steps])

    def n_steps(self):
        return {e: len(self.steps[e]) for e in self.ENGS}


def _rotary_tables(t0, n):
    half = np.arange(0, DHEAD, 2, dtype=np.float64)
    inv_freq = 1.0 / 10000.0 ** (half / DHEAD)
    t = np.arange(t0, t0 + n, dtype=np.float64)[:, None]
    fr = t * inv_freq[None, :]
    fr = np.concatenate([fr, fr], axis=1)
    base = (half + 0.4 * DHEAD) / (1.4 * DHEAD)
    power = (t - L // 2) / float(WIN // 2)
    sc = np.concatenate([base[None, :] ** power] * 2, axis=1)
    cos, sin = np.cos(fr), np.sin(fr)
    sign = np.where(np.arange(DHEAD) < 32, -1.0, 1.0)[None, :]
    return np.stack([
        (cos * sc * 8.0), (sin * sc * sign * 8.0),
        (cos / sc), (sin / sc * sign)]).astype(np.float32)


def _attn_mask_tiles(p):
    out = np.full((6, 128, 640), -1.0e30, np.float32)
    koffs = np.array((0, -384, -256, -128, -512))
    k = np.arange(128)
    qr = np.arange(128)
    for ti in range(6):
        i = 16 if ti == 0 else ti - 1
        q0 = 2048 * p - 128 + 128 * i
        tk = q0 + koffs[None, :, None] + k[:, None, None]
        tq = q0 + qr[None, None, :]
        kept = (tk >= 0) & (tk <= tq) & (tq <= tk + 512)
        out[ti] = np.where(kept, 0.0, -1.0e30).transpose(
            0, 1, 2).reshape(128, 640)
    return out


def _pad_w1(w1, b1):
    w1p = np.zeros((D, 2 * HIDP), np.float32)
    b1p = np.zeros((2 * HIDP,), np.float32)
    for j in range(HIDP // 128):
        lo, hi = 128 * j, min(128 * (j + 1), HID)
        n = hi - lo
        if n > 0:
            w1p[:, 256 * j: 256 * j + n] = w1[:, lo:hi]
            b1p[256 * j: 256 * j + n] = b1[lo:hi]
            w1p[:, 256 * j + 128: 256 * j + 128 + n] = w1[:, HID + lo: HID + hi]
            b1p[256 * j + 128: 256 * j + 128 + n] = b1[HID + lo: HID + hi]
    return w1p, b1p


def _pad_w2(w2):
    w2p = np.zeros((HIDP, D), np.float32)
    w2p[:HID] = w2
    return w2p


def _cols(vec, pdim=128):
    return np.ascontiguousarray(np.asarray(vec, np.float32).reshape(-1, pdim).T)


def _tslice(ap_dram, c0, w):
    return ap_dram.rearrange("(a p) t -> p a t", p=128)[:, :, c0:c0 + w]


POOL_COLS = 105472


def pv(pool, byte_off, prt, cols, dt):
    assert byte_off % 4 == 0
    nb = cols * (4 if dt == F32 else 2)
    assert byte_off + nb <= POOL_COLS * 2, (byte_off, nb)
    sl = pool[0:prt, byte_off // 2: (byte_off + nb) // 2]
    if dt == BF16:
        return sl
    return sl.bitcast(F32)


class Layout:
    def __init__(self, pool):
        self.pool = pool
        self.cur = 0
        self.map = {}

    def add(self, name, prt, cols, dt, at=None):
        nb = cols * (4 if dt == F32 else 2)
        off = self.cur if at is None else at
        off = (off + 255) // 256 * 256
        if at is None:
            self.cur = off + nb
        self.map[name] = (off, prt, cols, dt)
        assert off + nb <= POOL_COLS * 2, (name, off, nb)
        return pv(self.pool, off, prt, cols, dt)

    def mark(self):
        return self.cur

    def reset(self, pos):
        self.cur = pos


def chv(buf, stride, j, c0, w):
    return buf[:, j * stride + c0: j * stride + c0 + w]


def emit_rows_ln(prog, eng_rows, PS_row, srow, srow2, n, eps, sPE, sDV, sSC,
                 wait_pe):
    prog.wait("vector", sPE, wait_pe)
    prog.emit("vector", lambda v: v.tensor_scalar_mul(
        srow[0:1, :n], PS_row[0:1, :n], 1.0 / D))
    prog.drain("vector")
    prog.emit("vector", lambda v: v.tensor_mul(
        srow2[0:1, :n], srow[0:1, :n], srow[0:1, :n]))
    prog.drain("vector")
    dv = prog.emit("vector", lambda v: v.scalar_tensor_tensor(
        srow2[0:1, :n], PS_row[64:65, :n], 1.0 / D, srow2[0:1, :n],
        ALU.mult, ALU.subtract), inc=sDV)
    sc = prog.emit("scalar", lambda s: s.activation(
        srow2[0:1, :n], srow2[0:1, :n], AF.Sqrt, bias=eps),
        waits=[(sDV, dv)], inc=sSC)
    prog.wait("vector", sSC, sc)
    prog.emit("vector", lambda v: v.reciprocal(
        srow2[0:1, :n], srow2[0:1, :n]))
    prog.drain("vector")
    dv = prog.emit("vector", lambda v: v.scalar_tensor_tensor(
        srow[0:1, :n], srow[0:1, :n], -1.0, srow2[0:1, :n],
        ALU.mult, ALU.mult), inc=sDV)
    return dv





def emit_accum(prog, sGP, out_ap, in_ap, waits, sOut=None):
    prog.emit("gpsimd", lambda g, o=out_ap, i=in_ap: g.dma_start(
        out=o, in_=i, accum_op=ALU.add), waits=waits,
        inc=sOut, inc_amt=16)
    prog.drain("gpsimd")
    return prog.emit("gpsimd", lambda g: g.nop(), inc=sGP)

def emit_attn(P):
    prog, nc = P["prog"], P["nc"]
    sPE, sDV, sSC, sGP, sW, sOut = (P[k] for k in
                                    ("sPE", "sDV", "sSC", "sGP", "sW", "sOut"))
    PS_A, PS_B, PS_S, PS_T, PS_U = (P[k] for k in
                                    ("PS_A", "PS_B", "PS_S", "PS_T", "PS_U"))
    src, xout = P["src"], P["xout"]
    wqkv_sb, wo_sb = P["wqkv_sb"], P["wo_sb"]
    tb, masks = P["tb"], P["masks"]
    kfm, Vb, Oring, qsl = P["kfm"], P["Vb"], P["Oring"], P["qsl"]
    kprep, qprep, rexp, rot1, sqb, nrm = (P[k] for k in
        ("kprep", "qprep", "rexp", "rot1", "sqb", "nrm"))
    csrow, cslab, out_d = P["csrow"], P["cslab"], P["out_d"]
    oneb, onef, idb = P["oneb"], P["onef"], P["idb"]
    KT = TOK + 640
    NRING = 6
    KCOL0 = HALO - 640

    fq = prog.emit("tensor", lambda t: t.matmul(
        PS_U[:, 0:64], onef[0:1, :], P["qsrow"][0:1, 0:64],
        start=True, stop=True),
        waits=[(sW, None), (P["sU"], None), (sDV, None), (sSC, None)])
    prog.emit("tensor", lambda t: t.matmul(
        PS_U[:, 64:96], onef[0:1, :], P["qsrow"][0:1, 32:64],
        start=True, stop=True))
    prog.emit("tensor", lambda t: t.matmul(
        PS_U[:, 96:128], onef[0:1, :], P["qsrow"][0:1, 0:32],
        start=True, stop=True))
    prog.emit("tensor", lambda t: t.matmul(
        PS_U[:, 128:192], onef[0:1, :], P["ksrow"][0:1, 0:64],
        start=True, stop=True))
    prog.emit("tensor", lambda t: t.matmul(
        PS_U[:, 192:224], onef[0:1, :], P["ksrow"][0:1, 32:64],
        start=True, stop=True))
    fq = prog.emit("tensor", lambda t: t.matmul(
        PS_U[:, 224:256], onef[0:1, :], P["ksrow"][0:1, 0:32],
        start=True, stop=True), inc=sPE)
    prog.wait("vector", sPE, fq)
    prog.wait("vector", sW, None)
    for (ti, pc) in ((0, 0), (1, 64), (2, 128), (3, 192)):
        prog.emit("vector", lambda v, ti=ti, pc=pc: v.tensor_tensor(
            tb[ti].rearrange("p (a b) -> p a b", a=NKB),
            tb[ti].rearrange("p (a b) -> p a b", a=NKB),
            PS_U[:, pc:pc + 64][:, None, :].broadcast_to([128, NKB, 64]),
            ALU.mult))
    prog.drain("vector")
    fold_dv = prog.emit("vector", lambda v: v.nop(), inc=sDV)
    prog.wait("vector", sW, None)
    P["_v_ev"] = P["_k_ev"] = P["_q_ev"] = 0
    P["_psA_ev"] = P["_psB_ev"] = 0
    P["_apfree"] = [0, 0]
    P["_vfree"] = [0] * 6

    sc_done = [0, 0]
    dv_done = [fold_dv, fold_dv]
    slot_free = [0] * NRING
    kq_dv = fold_dv
    wo_done_dv = fold_dv

    def prep(j, which):
        nonlocal kq_dv
        pbuf = kprep if which == "k" else qprep
        t_cs = tb[2] if which == "k" else tb[0]
        t_ss = tb[3] if which == "k" else tb[1]
        prog.drain("scalar")
        sq = prog.emit("scalar", lambda s: s.activation(
            sqb[:, 0:512], pbuf[:, 0:512], AF.Square),
            waits=[(sPE, None)], inc=sSC)
        rd = prog.emit("vector", lambda v: v.reduce_sum(
            nrm[:, 0:8][:, :, None],
            sqb.rearrange("p (a b) -> p a b", a=8), axis=AX),
            waits=[(sSC, sq)], inc=sDV)
        s2 = prog.emit("scalar", lambda s: s.activation(
            nrm[:, 0:8], nrm[:, 0:8], AF.Sqrt), waits=[(sDV, rd)], inc=sSC)
        prog.wait("vector", sSC, s2)
        prog.emit("vector", lambda v: v.tensor_scalar_max(
            nrm[:, 0:8], nrm[:, 0:8], 1e-12))
        prog.drain("vector")
        prog.emit("vector", lambda v: v.reciprocal(nrm[:, 0:8], nrm[:, 0:8]))
        prog.drain("vector")
        prog.emit("vector", lambda v: v.tensor_copy(
            rexp.rearrange("p (a b) -> p a b", a=8),
            nrm[:, 0:8][:, :, None].broadcast_to([128, 8, 64])))
        prog.drain("vector")
        prog.emit("vector", lambda v: v.tensor_mul(
            pbuf[:, 0:512], pbuf[:, 0:512], rexp[:, 0:512]))
        prog.drain("vector")
        prog.emit("vector", lambda v, j=j: v.tensor_tensor(
            rot1[:, 0:512], pbuf[:, 0:512],
            t_cs[:, 64 * j:64 * j + 64][:, None, :].broadcast_to([128, 8, 64])
            .rearrange("p a b -> p (a b)") if False else
            t_cs[:, 64 * j:64 * j + 64][:, None, :].broadcast_to([128, 8, 64]),
            ALU.mult))
        prog.drain("vector")
        shuf = bass.AP(pbuf.tensor, pbuf.offset + 32,
                       [list(pbuf.ap)[0], [64, 8], [-32, 2], [1, 32]])
        prog.emit("vector", lambda v, j=j, shuf=shuf: v.tensor_tensor(
            rexp.rearrange("p (a b c) -> p a b c", a=8, b=2), shuf,
            t_ss[:, 64 * j:64 * j + 64][:, None, :].broadcast_to([128, 8, 64])
            .rearrange("p a (b c) -> p a b c", b=2), ALU.mult))
        prog.drain("vector")
        rot_dv = prog.emit("vector", lambda v: v.tensor_tensor(
            sqb[:, 0:512], rot1[:, 0:512], rexp[:, 0:512], ALU.add), inc=sDV)
        base = 0 if which == "k" else 4
        tpl = None
        for m in range(4):
            tpl = prog.emit("tensor", lambda t, m=m, base=base: t.transpose(
                PS_T[:, 128 * (base + m): 128 * (base + m) + 128],
                sqb[:, 128 * m: 128 * m + 128], idb[:, :]),
                waits=([(sDV, rot_dv), (sDV, kq_dv)] if m == 0 else ()),
                inc=sPE if m == 3 else None)
        prog.wait("vector", sPE, tpl)
        for m in range(4):
            if which == "k":
                prog.emit("vector", lambda v, m=m, j=j: v.tensor_copy(
                    kfm[:, KT * m + 128 * j: KT * m + 128 * j + 128],
                    PS_T[:, 128 * m: 128 * m + 128]))
            else:
                prog.emit("vector", lambda v, m=m, j=j: v.tensor_copy(
                    qsl[:, 512 * (j % 2) + 128 * m:
                        512 * (j % 2) + 128 * m + 128],
                    PS_T[:, 128 * (4 + m): 128 * (4 + m) + 128]))
        prog.drain("vector")
        kq_dv = prog.emit("vector", lambda v: v.nop(), inc=sDV)

    def attn_unit(i, h):
        pr = (i * 8 + h) % 2
        sb = 1024 * pr
        hp, oh = h // 2, h % 2
        ti = 0 if i >= 5 else 1 + i
        t1 = prog.emit("vector", lambda v, ti=ti: v.tensor_tensor(
            PS_S[:, sb: sb + 640], PS_S[:, sb: sb + 640],
            masks[:, 640 * ti: 640 * ti + 640], ALU.add),
            waits=[(sPE, P["_spair_pe"][pr])], inc=sDV)
        if P.get("stage") == "mask":
            sc_done[pr] = 0
            dv_done[pr] = t1
            return t1
        ex = prog.emit("scalar", lambda s, pr=pr: s.activation(
            P["Apar"][:, 640 * pr: 640 * pr + 640], PS_S[:, sb: sb + 640],
            AF.Exp), waits=[(sDV, t1), (sPE, P["_apfree"][pr])], inc=sSC)
        if P.get("stage") == "exp":
            sc_done[pr] = ex
            dv_done[pr] = t1
            return t1
        om = None
        cm = None
        for c, jj in enumerate((i + 4, i + 1, i + 2, i + 3, i)):
            om = prog.emit("tensor", lambda t, c=c, jj=jj % 6, pr=pr, h=h:
                           t.matmul(
                PS_S[:, sb + 640: sb + 768][0:64, :],
                Vb[:, 512 * jj + 64 * h: 512 * jj + 64 * h + 64],
                P["Apar"][:, 640 * pr + 128 * c: 640 * pr + 128 * c + 128],
                start=(c == 0), stop=(c == 4)),
                waits=([(sSC, ex), (sDV, dv_done[pr])] if c == 0 else ()),
                inc=sPE if c == 4 else None)
        oev = prog.emit("scalar", lambda s, pr=pr: s.activation(
            P["osb"][0:64, 128 * pr: 128 * pr + 128],
            PS_S[0:64, sb + 640: sb + 768], AF.Identity),
            waits=[(sPE, om)], inc=sSC)
        for c in range(5):
            cm = prog.emit("tensor", lambda t, c=c, pr=pr: t.matmul(
                PS_S[:, sb + 768: sb + 896][0:1, :], oneb[:, 0:1],
                P["Apar"][:, 640 * pr + 128 * c: 640 * pr + 128 * c + 128],
                start=(c == 0), stop=(c == 4)),
                waits=([(sSC, oev)] if c == 0 else ()),
                inc=sPE if c == 4 else None)
        if P.get("stage") == "omm":
            sc_done[pr] = ex
            dv_done[pr] = t1
            P["_apfree"][pr] = cm
            return t1
        prog.wait("vector", sPE, cm)
        prog.emit("vector", lambda v: v.tensor_scalar_max(
            PS_S[0:1, sb + 768: sb + 896], PS_S[0:1, sb + 768: sb + 896],
            1e-30))
        prog.drain("vector")
        rc = prog.emit("vector", lambda v, pr=pr: v.reciprocal(
            csrow[0:1, 128 * pr: 128 * pr + 128],
            PS_S[0:1, sb + 768: sb + 896]), inc=sDV)
        if P.get("stage") == "norm1":
            sc_done[pr] = ex
            dv_done[pr] = rc
            P["_apfree"][pr] = cm
            return rc
        bc = prog.emit("tensor", lambda t, pr=pr: t.matmul(
            PS_S[:, sb + 896: sb + 1024][0:64, :], onef[0:1, 0:64],
            csrow[0:1, 128 * pr: 128 * pr + 128], start=True, stop=True),
            waits=[(sDV, rc)], inc=sPE)
        if P.get("stage") == "norm2":
            sc_done[pr] = ex
            dv_done[pr] = rc
            P["_apfree"][pr] = cm
            return rc
        slot = i % NRING
        if P.get("stage") == "norm3":
            sc_done[pr] = oev
            dv_done[pr] = rc
            P["_apfree"][pr] = cm
            return rc
        prog.wait("vector", sPE, bc)
        prog.wait("vector", sPE, slot_free[slot])
        prog.wait("vector", sSC, oev)
        mu = prog.emit("vector", lambda v, slot=slot, hp=hp, oh=oh, pr=pr:
                       v.tensor_tensor(
            Oring[64 * oh: 64 * oh + 64,
                  512 * slot + 128 * hp: 512 * slot + 128 * hp + 128],
            P["osb"][0:64, 128 * pr: 128 * pr + 128],
            PS_S[0:64, sb + 896: sb + 1024], ALU.mult), inc=sDV)
        sc_done[pr] = oev
        dv_done[pr] = mu
        P["_apfree"][pr] = bc
        return mu

    def spair(i, hp):
        nonlocal kq_dv
        pe = [None, None]
        for c, jj in enumerate((i + 4, i + 1, i + 2, i + 3, i)):
            for oh in range(2):
                h = 2 * hp + oh
                pr = (i * 8 + h) % 2
                sb = 1024 * pr
                pe[oh] = prog.emit(
                    "tensor", lambda t, c=c, jj=jj, hp=hp, oh=oh, sb=sb, i=i:
                    t.matmul(
                        PS_S[:, sb + 128 * [0, 1, 2, 3, 4][c]:
                             sb + 128 * [0, 1, 2, 3, 4][c] + 128],
                        kfm[64 * oh: 64 * oh + 64,
                            KT * hp + 128 * jj: KT * hp + 128 * jj + 128],
                        qsl[64 * oh: 64 * oh + 64,
                            512 * (i % 2): 512 * (i % 2) + 128]
                        if False else
                        qsl[64 * oh: 64 * oh + 64,
                            512 * ((i + 4) % 2) + 128 * hp:
                            512 * ((i + 4) % 2) + 128 * hp + 128],
                        start=True, stop=True,
                        tile_position=(64 * oh, 0)),
                    waits=([(sSC, sc_done[(i * 8 + 2 * hp + oh) % 2]),
                            (sDV, dv_done[(i * 8 + 2 * hp + oh) % 2]),
                            (sDV, kq_dv)]
                           if c == 0 else ()),
                    inc=sPE)
        P["_spair_pe"][(i * 8 + 2 * hp) % 2] = pe[0]
        P["_spair_pe"][(i * 8 + 2 * hp + 1) % 2] = pe[1]

    P["_spair_pe"] = [0, 0]

    wo_slab_done = []
    for j in range(NKB):
        kc0 = KCOL0 + 128 * j
        for (tgt, ps, ofs) in (("v", PS_A, 1024), ("k", PS_B, 512)) + (
                (("q", PS_A, 0),) if j >= 4 else ()):
            mm_last = None
            for kk in range(8):
                sh = 1 if kk >= 4 else 0
                wl = []
                if kk == 0 and tgt == "v":
                    wl = [(sW, None), (sSC, P["_psA_ev"])]
                if kk == 0 and tgt == "k":
                    wl = [(sSC, P["_psB_ev"])]
                if kk == 0 and tgt == "q":
                    wl = [(sSC, P["_v_ev"])]
                mm_last = prog.emit(
                    "tensor", lambda t, kk=kk, kc0=kc0, ps=ps, ofs=ofs, sh=sh:
                    t.matmul(
                        ps[:, 0:512],
                        src[:, kk * XT + kc0 - sh: kk * XT + kc0 + 128 - sh],
                        wqkv_sb[:, kk * 1536 + ofs: kk * 1536 + ofs + 512],
                        start=(kk == 0), stop=(kk == 7)),
                    waits=wl, inc=sPE if kk == 7 else None)
            if tgt == "v":
                P["_v_ev"] = prog.emit(
                    "scalar", lambda s, j=j: s.activation(
                        Vb[:, 512 * (j % 6): 512 * (j % 6) + 512],
                        PS_A[:, 0:512], AF.Identity),
                    waits=[(sPE, mm_last), (sPE, P["_vfree"][j % 6])],
                    inc=sSC)
                P["_psA_ev"] = P["_v_ev"]
            elif tgt == "k":
                P["_k_ev"] = prog.emit(
                    "scalar", lambda s: s.activation(
                        kprep[:, 0:512], PS_B[:, 0:512], AF.Identity),
                    waits=[(sPE, mm_last)], inc=sSC)
                P["_psB_ev"] = P["_k_ev"]
            else:
                P["_q_ev"] = prog.emit(
                    "scalar", lambda s: s.activation(
                        qprep[:, 0:512], PS_A[:, 0:512], AF.Identity),
                    waits=[(sPE, mm_last)], inc=sSC)
                P["_psA_ev"] = P["_q_ev"]
        if P.get("stage") == "qkv0":
            continue
        prog.wait("vector", sSC, P["_k_ev"])
        prep(j, "k")
        if j >= 4:
            prog.wait("vector", sSC, P["_q_ev"])
            prep(j, "q")
        if P.get("stage") == "qkv":
            continue
        if j >= 4:
            i = j - 4
            for hp in range(4):
                spair(i, hp)
                if P.get("stage") in ("spair",):
                    sc_done[0] = sc_done[1] = 0
                    continue
                attn_unit(i, 2 * hp)
                attn_unit(i, 2 * hp + 1)
            P["_vfree"][i % 6] = sPE.n
            if i == 0 or (i % 4 == 0 and i > 0) or i == NQS - 1:
                pass
        def emit_wo(slab_qsubs, x1c0, own_c0, width):
            nonlocal wo_done_dv
            runs = []
            for qq in slab_qsubs:
                sl = qq % NRING
                if runs and runs[-1][-1] == sl - 1:
                    runs[-1].append(sl)
                else:
                    runs.append([sl])
            ev_sc = []
            prev_ev = P.get("_wo_ev", 0)
            for m in range(8):
                pu_last = None
                ncol = 0
                for run in runs:
                    nr = len(run) * 128
                    for hp in range(4):
                        pu_last = prog.emit(
                            "tensor",
                            lambda t, m=m, hp=hp, run=run, ncol=ncol, nr=nr:
                            t.matmul(
                                PS_U[:, ncol:ncol + nr],
                                wo_sb[:, 1024 * hp + 128 * m:
                                      1024 * hp + 128 * m + 128],
                                Oring[:, 512 * run[0] + 128 * hp:
                                      512 * run[0] + 128 * hp + 128]
                                .broadcast_to([128, nr]) if False else
                                bass.AP(Oring.tensor,
                                        Oring.offset + 512 * run[0] + 128 * hp,
                                        [list(Oring.ap)[0], [512, len(run)],
                                         [1, 128]]),
                                start=(hp == 0), stop=(hp == 3)),
                            waits=([(sDV, dv_done[0]), (sDV, dv_done[1]),
                                    (sSC, prev_ev)]
                                   if (hp == 0 and ncol == 0) else ()),
                            inc=sPE if (hp == 3) else None)
                    ncol += nr
                ev = prog.emit("scalar", lambda s, m=m, width=width:
                               s.activation(
                    cslab[:, 512 * m: 512 * m + width], PS_U[:, 0:width],
                    AF.Identity),
                    waits=[(sPE, pu_last), (sOut, None),
                           (sGP, P.get("_acc_gp", 0))],
                    inc=sSC)
                ev_sc.append(ev)
                prev_ev = ev
                xsrc0 = HALO - 128 + x1c0
                x1w = TOK + 128
                xu = prog.emit("vector", lambda v, m=m, x1c0=x1c0, x1w=x1w,
                               xsrc0=xsrc0, width=width: v.tensor_tensor(
                    xout[:, m * x1w + x1c0: m * x1w + x1c0 + width],
                    src[:, m * XT + xsrc0: m * XT + xsrc0 + width],
                    cslab[:, 512 * m: 512 * m + width], ALU.add),
                    waits=[(sSC, ev)], inc=sDV)
            P["_wo_ev"] = ev_sc[-1]
            prog.drain("vector")
            wo_done_dv = prog.emit("vector", lambda v: v.nop(), inc=sDV)
            for qq in slab_qsubs:
                slot_free[qq % NRING] = sPE.n
            if own_c0 >= 0:
                P["_acc_gp"] = emit_accum(
                    prog, sGP, _tslice(out_d, own_c0, width),
                    cslab.rearrange("p (a b) -> p a b", a=8)[:, :, :width],
                    [(sSC, ev_sc[-1]), (sDV, wo_done_dv),
                     (sGP, P.get("_acc_gp", 0))], sOut=sOut)
        if P.get("stage") == "attn":
            continue
        if j == 4:
            emit_wo([0], 0, -1, 128)
        elif j >= 8 and (j - 4) % 4 == 0:
            s4 = (j - 8) // 4
            emit_wo([4 * s4 + 1 + t for t in range(4)], 128 + 512 * s4,
                    512 * s4, 512)
    return wo_done_dv



def emit_mlp(P):
    prog, nc = P["prog"], P["nc"]
    sPE, sDV, sSC, sGP, sW, sSp, sOut = (P[k] for k in
        ("sPE", "sDV", "sSC", "sGP", "sW", "sSp", "sOut"))
    sU = P["sU"]
    PS_A, PS_B, PS_S, PS_T, PS_U = (P[k] for k in
        ("PS_A", "PS_B", "PS_S", "PS_T", "PS_U"))
    xio, w1p, w2p = P["xio"], P["w1p"], P["w2p"]
    b1c, b2c, lngc, lnbc = P["b1c"], P["b2c"], P["lngc"], P["lnbc"]
    hln, sq2, geluS, mrow, mrow2 = (P[k] for k in
        ("hln", "sq2", "geluS", "mrow", "mrow2"))
    act_d, out_d, cslab = P["act_d"], P["out_d"], P["cslab"]
    oneb, onef = P["oneb"], P["onef"]
    x1w = TOK + 128
    NSL = TOK // 512
    xs = lambda kk, c0, w: xio[:, kk * x1w + c0 - (1 if kk >= 4 else 0):
                               kk * x1w + c0 + w - (1 if kk >= 4 else 0)]
    act_sp = []
    hln_pe_last = 0
    for s4 in range(NSL):
        c0 = 128 + 512 * s4
        sq = prog.emit("scalar", lambda s, c0=c0: [s.activation(
            sq2[:, kk * 512: kk * 512 + 512], xs(kk, c0, 512), AF.Square)
            for kk in range(8)][-1], waits=[(sDV, None), (sPE, None)],
            inc=sSC)
        mm1 = None
        for kk in range(8):
            mm1 = prog.emit("tensor", lambda t, kk=kk, c0=c0: t.matmul(
                PS_U[0:1, 0:512], oneb[:, 0:1], xs(kk, c0, 512),
                start=(kk == 0), stop=(kk == 7)),
                waits=([(sDV, None), (sW, None)] if kk == 0 else ()),
                inc=sPE if kk == 7 else None)
        mm2 = None
        for kk in range(8):
            mm2 = prog.emit("tensor", lambda t, kk=kk: t.matmul(
                PS_U[64:65, 0:512], oneb[:, 0:1],
                sq2[:, kk * 512: kk * 512 + 512],
                start=(kk == 0), stop=(kk == 7)),
                waits=([(sSC, sq)] if kk == 0 else ()),
                inc=sPE if kk == 7 else None)
        dvr = emit_rows_ln(prog, "vector", PS_U, mrow, mrow2, 512, 1e-5,
                           sPE, sDV, sSC, mm2)
        if DEV_DEBUG and s4 == 0:
            dbg_r = P["nc"].dram_tensor(
                "dbg_rows" + P.get("tag", ""), [2, 512], F32,
                kind="ExternalOutput").ap()
            prog.emit("sync", lambda s: s.dma_start(
                out=dbg_r[0:1, :], in_=mrow[0:1, 0:512]),
                waits=[(sDV, dvr)], inc=sSp, inc_amt=16)
            prog.emit("sync", lambda s: s.dma_start(
                out=dbg_r[1:2, :], in_=mrow2[0:1, 0:512]),
                inc=sSp, inc_amt=16)
        bc1 = prog.emit("tensor", lambda t: t.matmul(
            PS_S[:, 0:512], onef[0:1, :], mrow2[0:1, 0:512],
            start=True, stop=True), waits=[(sDV, dvr)])
        bc2 = prog.emit("tensor", lambda t: t.matmul(
            PS_S[:, 512:1024], onef[0:1, :], mrow[0:1, 0:512],
            start=True, stop=True), inc=sPE)
        prog.wait("vector", sPE, bc2)
        prog.wait("vector", sU)
        hl_dv = None
        for kk in range(8):
            prog.emit("vector", lambda v, kk=kk, c0=c0: v.tensor_tensor(
                geluS[:, 0:512], xs(kk, c0, 512), PS_S[:, 0:512], ALU.mult))
            prog.drain("vector")
            prog.emit("vector", lambda v: v.tensor_add(
                geluS[:, 0:512], geluS[:, 0:512], PS_S[:, 512:1024]))
            prog.drain("vector")
            hl_dv = prog.emit("vector", lambda v, kk=kk: v.tensor_scalar(
                hln[:, kk * 512: kk * 512 + 512], geluS[:, 0:512],
                lngc[:, kk: kk + 1], lnbc[:, kk: kk + 1],
                ALU.mult, ALU.add), inc=sDV)
            prog.drain("vector")
        a_rd, g_rd = 0, 0
        for jm in range(22):
            am = None
            for kk in range(8):
                am = prog.emit("tensor", lambda t, kk=kk, jm=jm: t.matmul(
                    PS_A[:, 0:512],
                    w1p[:, kk * 5632 + 256 * jm: kk * 5632 + 256 * jm + 128],
                    hln[:, kk * 512: kk * 512 + 512],
                    start=(kk == 0), stop=(kk == 7)),
                    waits=([(sDV, hl_dv), (sDV, a_rd), (sW, None),
                            (P["sU"], None)] if kk == 0 else ()),
                    inc=sPE if kk == 7 else None)
            gm = None
            for kk in range(8):
                gm = prog.emit("tensor", lambda t, kk=kk, jm=jm: t.matmul(
                    PS_B[:, 0:512],
                    w1p[:, kk * 5632 + 256 * jm + 128:
                        kk * 5632 + 256 * jm + 256],
                    hln[:, kk * 512: kk * 512 + 512],
                    start=(kk == 0), stop=(kk == 7)),
                    waits=([(sSC, g_rd)] if kk == 0 else ()),
                    inc=sPE if kk == 7 else None)
            g_rd = prog.emit("scalar", lambda s, jm=jm: s.activation(
                geluS[:, 0:512], PS_B[:, 0:512], AF.Gelu,
                bias=b1c[:, 2 * jm + 1: 2 * jm + 2]),
                waits=[(sPE, gm), (sDV, a_rd)], inc=sSC)
            a_rd = prog.emit("vector", lambda v, jm=jm: v.scalar_tensor_tensor(
                P["act"][:, 512 * jm: 512 * jm + 512], PS_A[:, 0:512],
                b1c[:, 2 * jm: 2 * jm + 1], geluS[:, 0:512],
                ALU.add, ALU.mult),
                waits=[(sSC, g_rd), (sPE, am), (sSp, None)], inc=sDV)
        act_sp.append(prog.emit("sync", lambda s, s4=s4: s.dma_start(
            out=act_d[s4], in_=P["act"][:, :]),
            waits=[(sDV, a_rd)], inc=sSp, inc_amt=16))
        if DEV_DEBUG and s4 == 0:
            dbg_h = P["nc"].dram_tensor(
                "dbg_hln" + P.get("tag", ""), [128, 8 * 512], BF16,
                kind="ExternalOutput").ap()
            dbg_a = P["nc"].dram_tensor(
                "dbg_act" + P.get("tag", ""), [128, 22 * 512], BF16,
                kind="ExternalOutput").ap()
            prog.emit("sync", lambda s: s.dma_start(out=dbg_h, in_=hln[:, :]),
                      inc=sSp, inc_amt=16)
            prog.emit("sync", lambda s: s.dma_start(
                out=dbg_a, in_=P["act"][:, :]), inc=sSp, inc_amt=16)
        hln_pe_last = sPE.n
    w2_ld = None
    for kk in range(22):
        w2_ld = prog.emit("gpsimd", lambda g, kk=kk: g.dma_start(
            out=w2p[:, kk * 1024:(kk + 1) * 1024],
            in_=P["w2_w"][128 * kk:128 * (kk + 1), :]),
            waits=([(sPE, hln_pe_last), (sSp, None)] if kk == 0 else ()),
            inc=sW, inc_amt=16)
    for s4 in range(NSL):
        c0 = 128 + 512 * s4
        ar = prog.emit("sync", lambda s, s4=s4: s.dma_start(
            out=P["act"][:, :], in_=act_d[s4]),
            waits=[(sDV, None), (sW, None), (sSC, None)],
            inc=sSp, inc_amt=16)
        ev_sc = None
        xu_dv = None
        for m in range(8):
            pb = PS_A if m % 2 == 0 else PS_B
            wm = None
            for kk in range(22):
                wm = prog.emit("tensor", lambda t, kk=kk, m=m, pb=pb:
                               t.matmul(
                    pb[:, 0:512],
                    w2p[:, kk * 1024 + 128 * m: kk * 1024 + 128 * m + 128],
                    P["act"][:, 512 * kk: 512 * kk + 512],
                    start=(kk == 0), stop=(kk == 21)),
                    waits=([(sSp, None), (sW, None), (sSC, None)]
                           if kk == 0 else ()),
                    inc=sPE if kk == 21 else None)
            ev_sc = prog.emit("scalar", lambda s, m=m, pb=pb: s.activation(
                cslab[:, 512 * m: 512 * m + 512], pb[:, 0:512],
                AF.Identity, bias=b2c[:, m: m + 1]),
                waits=[(sPE, wm), (sOut, None),
                       (sGP, P.get("_acc_gp", 0))], inc=sSC)
            if not P.get("skip_x"):
                xu_dv = prog.emit("vector", lambda v, m=m, c0=c0:
                                  v.tensor_tensor(
                    xio[:, m * x1w + c0: m * x1w + c0 + 512],
                    xio[:, m * x1w + c0: m * x1w + c0 + 512],
                    cslab[:, 512 * m: 512 * m + 512], ALU.add),
                    waits=[(sSC, ev_sc)], inc=sDV)
        P["_acc_gp"] = emit_accum(
            prog, sGP, _tslice(out_d, 512 * s4, 512),
            cslab.rearrange("p (a b) -> p a b", a=8),
            [(sSC, ev_sc), (sGP, P.get("_acc_gp", 0))], sOut=sOut)
        if DEV_DEBUG and P.get("tag"):
            dbg_ml = P["nc"].dram_tensor(
                f"dbg_mlp{s4}", [128, 8 * 512], F32,
                kind="ExternalOutput").ap()
            prog.emit("sync", lambda s, dbg_ml=dbg_ml: s.dma_start(
                out=dbg_ml, in_=cslab[:, :]),
                waits=[(sSC, ev_sc)], inc=sSp, inc_amt=16)
            prog.wait("scalar", sSp)
    prog.drain("vector")
    return prog.emit("vector", lambda v: v.nop(), inc=sDV)


def build_launch1(upto="all"):
    nc = bass.Bass("TRN2", target_bir_lowering=False)
    dt_in = lambda n, s, d=F32: nc.dram_tensor(n, s, d, kind="ExternalInput").ap()
    dt_out = lambda n, s, d=F32: nc.dram_tensor(n, s, d, kind="ExternalOutput").ap()
    dt_int = lambda n, s, d=F32: nc.dram_tensor(n, s, d).ap()

    u_t = dt_in("u_t", [D, XT])
    W_in = dt_in("W_in", [D, ZD])
    conv_w = dt_in("conv_w", [128, 16 * 4])
    conv_b = dt_in("conv_b", [128, 16])
    delta_p = dt_in("delta_p", [16, 1])
    wqkv_w = dt_in("wqkv", [D, 3 * INNER])
    wo_w = dt_in("wo_w", [INNER, D])
    w1_w = dt_in("w1_w", [D, 2 * HIDP])
    b1_w = dt_in("b1_w", [128, 2 * HIDP // 128])
    w2_w = dt_in("w2_w", [HIDP, D])
    b2_w = dt_in("b2_w", [128, D // 128])
    lng_w = dt_in("lng", [128, 8])
    lnb_w = dt_in("lnb", [128, 8])
    qs_in = dt_in("qs", [1, DHEAD])
    ks_in = dt_in("ks", [1, DHEAD])
    rotab = dt_in("rotab", [4, NKB * 128, DHEAD])
    masks_in = dt_in("masks", [6, 128, 640])
    ident_f = dt_in("ident_f", [128, 128])
    out_d = dt_out("out_part", [D, TOK])
    xn_d = dt_out("xn_sp", [D, XT], BF16)
    x1_d = dt_out("x1_sp", [D, TOK], BF16)
    lam_d = dt_out("lam_sp", [16, NCH * 128 + 1])
    G_d = dt_out("G_sp", [DS, NH * DH])
    Xtm_d = dt_out("Xtm_sp", [NCH, 128, NH * DH], BF16)
    Btm_d = dt_out("Btm_sp", [NCH, 128, NH * DS], BF16)
    Bfm_d = dt_out("Bfm_sp", [8, 128, TOK], BF16)
    Cfm_d = dt_out("Cfm_sp", [8, 128, TOK], BF16)
    act_d = dt_int("act_sc", [4, 128, 22 * 512], BF16)

    ctx = contextlib.ExitStack()
    pool = ctx.enter_context(nc.sbuf_tensor("pool", [128, POOL_COLS], BF16))
    ps = lambda n, s, d: ctx.enter_context(nc.psum_tensor(n, s, d))
    PS_A = ps("PS_A", [128, 512], F32)
    PS_B = ps("PS_B", [128, 512], F32)
    PS_S = ps("PS_S", [128, 2048], F32)
    PS_T = ps("PS_T", [128, 1024], BF16)
    PS_U = ps("PS_U", [128, 512], F32)
    mksem = lambda n, dma=False: Sem(ctx.enter_context(nc.semaphore(n)), dma)
    sW = mksem("sW", True); sU = mksem("sU", True); sSp = mksem("sSp", True)
    sPE = mksem("sPE"); sDV = mksem("sDV"); sSC = mksem("sSC")
    sGP = mksem("sGP"); sOut = mksem("sOut", True)
    prog = Prog(nc)

    HDR = 2048
    XN_AT = 161024 + HDR
    X1_AT = 70144 + HDR
    RB_AT = 104960 + HDR
    assert XN_AT + 8 * XT * 2 <= POOL_COLS * 2, (XN_AT, XT)

    LM = Layout(pool)
    idf = LM.add("idf", 128, 128, F32)
    idb = LM.add("idb", 128, 128, BF16)
    oneb = LM.add("oneb", 128, 128, BF16)
    onef = LM.add("onef", 128, 128, F32)
    cap_c = LM.add("cap_c", 128, 8, F32)
    assert LM.cur <= HDR, LM.cur
    LM.cur = HDR

    Wsb = LM.add("Wsb", 128, 8 * ZD, BF16)
    u_sl = LM.add("u_sl", 128, 8 * 512, F32)
    ubf = LM.add("ubf", 128, 8 * 512, BF16)
    xr = LM.add("xr", 128, 16 * 520, BF16)
    sil = LM.add("sil", 128, 16 * 128, BF16)
    bst = LM.add("bst", 128, 8 * 512, BF16)
    cst = LM.add("cst", 128, 2 * 512, BF16)
    bds = LM.add("bds", 128, 16 * 64, BF16)
    xts = LM.add("xts", 128, 16 * 128, BF16)
    dtb = LM.add("dtb", 16, 2048, F32)
    lam = LM.add("lam", 16, 2056, F32)
    ebd = LM.add("ebd", 128, NCH * 16, F32)
    ela = LM.add("ela", 64, NCH * 16, F32)
    Gb = LM.add("Gb", 64, NH * DH, F32)
    cwc = LM.add("cwc", 128, 64, F32)
    cbc = LM.add("cbc", 128, 16, F32)
    dpc = LM.add("dpc", 16, 1, F32)
    srow = LM.add("srow", 1, 512, F32)
    srow2 = LM.add("srow2", 1, 512, F32)
    prebuf = LM.add("prebuf", 16, 128, F32)
    cacc = LM.add("cacc", 128, 128, F32)
    row1 = LM.add("row1", 1, 16, F32)
    assert LM.cur <= XN_AT, LM.cur
    xn = pv(pool, XN_AT, 128, 8 * XT, BF16)
    osl = pv(pool, LM.map["xr"][0], 128, 8 * 512, F32)
    dts = pv(pool, LM.map["u_sl"][0], 16, 2048, F32)

    for k in range(8):
        prog.emit("gpsimd", lambda g, k=k: g.dma_start(
            out=Wsb[:, k * ZD:(k + 1) * ZD],
            in_=W_in[128 * k:128 * (k + 1), :]), inc=sW, inc_amt=16)
    for (dst, src) in ((cwc, conv_w), (cbc, conv_b), (dpc, delta_p),
                      (idf, ident_f)):
        prog.emit("sync", lambda s, d=dst, r=src: s.dma_start(
            out=d[:, :], in_=r), inc=sU, inc_amt=16)
    prog.emit("gpsimd", lambda g: g.dma_start(out=idb[:, :], in_=ident_f),
              inc=sW, inc_amt=16)
    w_all = sW.n
    c_all = sU.n
    prog.emit("vector", lambda v: v.memset(oneb[:, :], 1.0))
    prog.emit("vector", lambda v: v.memset(onef[:, :], 1.0))
    prog.emit("vector", lambda v: v.memset(lam[:, 0:1], 0.0))
    for i, vv in enumerate(CONST_VALS):
        nc.const_aps.register(F32, vv, cap_c[:, i:i + 1])
        prog.emit("vector", lambda v, i=i, vv=vv: v.memset(
            cap_c[:, i:i + 1], vv))
    prog.drain("vector")
    ones_dv = prog.emit("vector", lambda v: v.nop(), inc=sDV)
    for e in ("scalar", "tensor", "gpsimd", "sync"):
        prog.wait(e, sDV, ones_dv)

    slabs = [(c, min(512, XT - c)) for c in range(0, XT, 512)]
    prev_done_dv = ones_dv
    prev_out = 0
    prev_mm2 = 0
    prev_rows_dv = 0
    for (c0, w) in slabs:
        ud = prog.emit("sync", lambda s, c0=c0, w=w: s.dma_start(
            out=u_sl.rearrange("p (a b) -> p a b", a=8)[:, :, :w],
            in_=_tslice(u_t, c0, w)),
            waits=[(sDV, prev_done_dv), (sOut, prev_out)],
            inc=sU, inc_amt=16)
        vb = prog.emit("vector", lambda v, w=w: v.tensor_copy(
            ubf.rearrange("p (a b) -> p a b", a=8)[:, :, :w],
            u_sl.rearrange("p (a b) -> p a b", a=8)[:, :, :w]),
            waits=[(sU, ud), (sPE, prev_mm2)], inc=sDV)
        mm1 = None
        for kk in range(8):
            mm1 = prog.emit("tensor", lambda t, kk=kk, w=w: t.matmul(
                PS_U[0:1, :w], oneb[:, 0:1], ubf[:, kk * 512: kk * 512 + w],
                start=(kk == 0), stop=(kk == 7)),
                waits=[(sDV, vb), (sDV, prev_rows_dv)] if kk == 0 else (),
                inc=sPE if kk == 7 else None)
        sq = prog.emit("scalar", lambda s, w=w: s.activation(
            ubf.rearrange("p (a b) -> p a b", a=8)[:, :, :w],
            ubf.rearrange("p (a b) -> p a b", a=8)[:, :, :w], AF.Square),
            waits=[(sPE, mm1)], inc=sSC)
        mm2 = None
        for kk in range(8):
            mm2 = prog.emit("tensor", lambda t, kk=kk, w=w: t.matmul(
                PS_U[64:65, :w], oneb[:, 0:1], ubf[:, kk * 512: kk * 512 + w],
                start=(kk == 0), stop=(kk == 7)),
                waits=[(sSC, sq)] if kk == 0 else (),
                inc=sPE if kk == 7 else None)
        dvr = emit_rows_ln(prog, "vector", PS_U, srow, srow2, w, 1e-5,
                           sPE, sDV, sSC, mm2)
        own0 = max(c0, OWN0)
        w2 = c0 + w - own0
        prog.emit("tensor", lambda t, w=w: t.matmul(
            PS_S[:, 0:w], onef[0:1, :], srow2[0:1, :w], start=True, stop=True),
            waits=[(sDV, dvr)])
        bc2 = prog.emit("tensor", lambda t, w=w: t.matmul(
            PS_S[:, 512:512 + w], onef[0:1, :], srow[0:1, :w],
            start=True, stop=True), inc=sPE)
        prog.wait("vector", sPE, bc2)
        for kk in range(8):
            prog.emit("vector", lambda v, kk=kk, w=w: v.tensor_mul(
                u_sl[:, kk * 512: kk * 512 + w],
                u_sl[:, kk * 512: kk * 512 + w], PS_S[:, 0:w]))
        prog.drain("vector")
        for kk in range(8):
            prog.emit("vector", lambda v, kk=kk, w=w: v.tensor_add(
                u_sl[:, kk * 512: kk * 512 + w],
                u_sl[:, kk * 512: kk * 512 + w], PS_S[:, 512:512 + w]))
        prog.drain("vector")
        xn_dv = prog.emit("vector", lambda v, c0=c0, w=w: v.tensor_copy(
            xn.rearrange("p (a b) -> p a b", a=8)[:, :, c0:c0 + w],
            u_sl.rearrange("p (a b) -> p a b", a=8)[:, :, :w]), inc=sDV)
        prev_done_dv = xn_dv
        if w2 > 0:
            o_ = own0 - c0
            ud2 = prog.emit("sync", lambda s, o=own0, w2=w2: s.dma_start(
                out=osl.rearrange("p (a b) -> p a b", a=8)[:, :, :w2],
                in_=_tslice(u_t, o, w2)), inc=sU, inc_amt=16)
            prog.wait("vector", sU, ud2)
            for kk in range(8):
                prog.emit("vector", lambda v, kk=kk, o_=o_, w2=w2:
                          v.tensor_tensor(
                              osl[:, kk * 512: kk * 512 + w2],
                              osl[:, kk * 512: kk * 512 + w2],
                              u_sl[:, kk * 512 + o_: kk * 512 + o_ + w2],
                              ALU.add))
            prog.drain("vector")
            odv = prog.emit("vector", lambda v: v.nop(), inc=sDV)
            prog.emit("gpsimd", lambda g, o=own0 - OWN0, w2=w2: g.dma_start(
                out=_tslice(out_d, o, w2),
                in_=osl.rearrange("p (a b) -> p a b", a=8)[:, :, :w2]),
                waits=[(sDV, odv)], inc=sOut, inc_amt=16)
            prev_done_dv = odv
        prev_out = sOut.n
        prev_mm2 = mm2
        prev_rows_dv = dvr
    ln_done_dv = sDV.n
    ln_done_out = sOut.n

    NSL = TOK // 512
    prev_dt_sc = 0
    for s4 in range(NSL):
        c0 = OWN0 + 512 * s4
        dt_pe = None
        for kk in range(8):
            dt_pe = prog.emit("tensor", lambda t, kk=kk, c0=c0: t.matmul(
                PS_U[0:16, 0:512], Wsb[:, kk * ZD + 4096: kk * ZD + 4112],
                xn[:, kk * XT + c0: kk * XT + c0 + 512],
                start=(kk == 0), stop=(kk == 7)),
                waits=([(sW, w_all), (sDV, ln_done_dv), (sSC, prev_dt_sc)]
                       if kk == 0 else ()), inc=sPE if kk == 7 else None)
        prev_dt_sc = prog.emit("scalar", lambda s, s4=s4: s.activation(
            dtb[:, 512 * s4: 512 * (s4 + 1)], PS_U[0:16, 0:512], AF.Identity),
            waits=[(sPE, dt_pe)], inc=sSC)
    prog.wait("scalar", sU, c_all)
    prog.wait("scalar", sOut)
    prog.drain("scalar")
    prog.emit("scalar", lambda s: s.activation(
        dts[:, 0:TOK], dtb[:, 0:TOK], AF.Exp, bias=dpc[:, 0:1]))
    prog.drain("scalar")
    prog.emit("scalar", lambda s: s.activation(
        dts[:, 0:TOK], dts[:, 0:TOK], AF.Ln, bias=1.0))
    prog.drain("scalar")
    sp_sc = prog.emit("scalar", lambda s: s.activation(
        dtb[:, 0:TOK], dtb[:, 0:TOK], AF.Abs), inc=sSC)
    prog.wait("vector", sSC, sp_sc)
    prog.emit("vector", lambda v: v.scalar_tensor_tensor(
        dtb[:, 0:TOK], dts[:, 0:TOK], -1.0, dtb[:, 0:TOK],
        ALU.mult, ALU.mult))
    prog.drain("vector")
    lam_dv = prog.emit("vector", lambda v: v.tensor_tensor_scan(
        lam[:, 1:TOK + 1], dtb[:, 0:TOK], dtb[:, 0:TOK], lam[:, 0:1],
        ALU.add, ALU.min), inc=sDV)
    prog.emit("sync", lambda s: s.dma_start(
        out=lam_d, in_=lam[:, 0:TOK + 1]), waits=[(sDV, lam_dv)],
        inc=sSp, inc_amt=16)

    pre_sc = 0
    pre_dv = 0
    for c in range(NCH):
        e0, s0 = 128 * c + 128, 128 * c
        pd = prog.emit("vector", lambda v, c=c, e0=e0: v.tensor_scalar(
            prebuf[:, :], lam[:, 128 * c + 1: 128 * c + 129],
            lam[:, e0:e0 + 1], None, ALU.subtract),
            waits=[(sDV, lam_dv), (sSC, pre_sc)], inc=sDV)
        tp = prog.emit("tensor", lambda t: t.transpose(
            PS_S[:, 512:528], prebuf[:, :], idf[0:16, 0:16]),
            waits=[(sDV, pd), (sU, c_all), (sSC, pre_sc)], inc=sPE)
        ebd_sc = prog.emit("scalar", lambda s, c=c: s.activation(
            ebd[:, 16 * c: 16 * c + 16], PS_S[:, 512:528], AF.Exp,
            scale=-1.0), waits=[(sPE, tp)], inc=sSC)
        dcol = prog.emit("vector", lambda v, e0=e0, s0=s0: v.tensor_tensor(
            prebuf[:, 0:1], lam[:, e0:e0 + 1], lam[:, s0:s0 + 1],
            ALU.subtract), waits=[(sPE, tp)], inc=sDV)
        ec = prog.emit("scalar", lambda s: s.activation(
            prebuf[:, 1:2], prebuf[:, 0:1], AF.Exp),
            waits=[(sDV, dcol)], inc=sSC)
        tp2 = prog.emit("tensor", lambda t: t.transpose(
            PS_U[0:1, 0:16], prebuf[:, 1:2], idf[0:16, 0:16]),
            waits=[(sSC, ec), (sSC, pre_sc)], inc=sPE)
        rc = prog.emit("vector", lambda v: v.tensor_copy(
            row1[0:1, 0:16], PS_U[0:1, 0:16]), waits=[(sPE, tp2)], inc=sDV)
        bc = prog.emit("tensor", lambda t: t.matmul(
            PS_U[0:64, 32:48], onef[0:1, 0:64], row1[0:1, 0:16],
            start=True, stop=True), waits=[(sDV, rc)], inc=sPE)
        pre_sc = prog.emit("scalar", lambda s, c=c: s.activation(
            ela[:, 16 * c: 16 * c + 16], PS_U[0:64, 32:48], AF.Identity),
            waits=[(sPE, bc)], inc=sSC)
    epre_sc = pre_sc

    prog.emit("vector", lambda v: v.memset(Gb[:, :], 0.0))
    prog.drain("vector")
    g_dv = prog.emit("vector", lambda v: v.nop(), inc=sDV)
    conv_gp = 0
    sil_pe = 0
    xts_sp = 0
    bds_sp = 0
    bst_pe = 0
    xt_dv = g_dv
    bd_dv = g_dv
    ctr_dv = g_dv
    sil_sc = 0
    for s4 in range(NSL):
        c0 = OWN0 + 512 * s4
        if s4 == 0:
            for mj in range(16):
                hp = None
                for kk in range(8):
                    hp = prog.emit("tensor", lambda t, mj=mj, kk=kk: t.matmul(
                        PS_A[0:128, 0:8],
                        Wsb[:, kk * ZD + 128 * mj: kk * ZD + 128 * mj + 128],
                        xn[:, kk * XT + OWN0 - 8: kk * XT + OWN0],
                        start=(kk == 0), stop=(kk == 7)),
                        waits=([(sW, w_all), (sSC, epre_sc)]
                               if (mj == 0 and kk == 0) else ()),
                        inc=sPE if kk == 7 else None)
                hsc = prog.emit("scalar", lambda s, mj=mj: s.activation(
                    xr[:, 520 * mj: 520 * mj + 8], PS_A[0:128, 0:8],
                    AF.Identity), waits=[(sPE, hp)], inc=sSC)
                prog.wait("tensor", sSC, hsc)
        else:
            prog.wait("vector", sGP, conv_gp)
            prog.wait("vector", sSC, None)
            prog.emit("vector", lambda v: v.tensor_copy(
                xr.rearrange("p (a b) -> p a b", a=16)[:, :, 0:8],
                xr.rearrange("p (a b) -> p a b", a=16)[:, :, 512:520]))
            prog.drain("vector")
        for mj in range(16):
            pb = PS_A if mj % 2 == 0 else PS_B
            for kk in range(8):
                prog.emit("tensor", lambda t, mj=mj, kk=kk, c0=c0, pb=pb:
                          t.matmul(
                              pb[:, 0:512],
                              Wsb[:, kk * ZD + 128 * mj:
                                  kk * ZD + 128 * mj + 128],
                              xn[:, kk * XT + c0: kk * XT + c0 + 512],
                              start=(kk == 0), stop=(kk == 7)),
                          waits=([(sSC, None), (sGP, conv_gp)]
                                 if kk == 0 else ()),
                          inc=sPE if kk == 7 else None)
            zp = sPE.n
            prog.emit("scalar", lambda s, mj=mj, pb=pb: s.activation(
                xr[:, 520 * mj + 8: 520 * mj + 520], pb[:, 0:512],
                AF.Identity), waits=[(sPE, zp)], inc=sSC)
        zx_sc = sSC.n
        for mj in range(8):
            pb = PS_A if mj % 2 == 0 else PS_B
            for kk in range(8):
                prog.emit("tensor", lambda t, mj=mj, kk=kk, c0=c0, pb=pb:
                          t.matmul(
                              pb[:, 0:512],
                              Wsb[:, kk * ZD + 2048 + 128 * mj:
                                  kk * ZD + 2048 + 128 * mj + 128],
                              xn[:, kk * XT + c0: kk * XT + c0 + 512],
                              start=(kk == 0), stop=(kk == 7)),
                          waits=([(sSC, None), (sSp, bds_sp), (sPE, bst_pe)]
                                 if kk == 0 else ()),
                          inc=sPE if kk == 7 else None)
            zp = sPE.n
            bev = prog.emit("scalar", lambda s, mj=mj, pb=pb: s.activation(
                bst[:, 512 * mj: 512 * mj + 512], pb[:, 0:512],
                AF.Identity), waits=[(sPE, zp)], inc=sSC)
            prog.emit("sync", lambda s, mj=mj, s4=s4: s.dma_start(
                out=Bfm_d[mj, :, 512 * s4: 512 * s4 + 512],
                in_=bst[:, 512 * mj: 512 * mj + 512]),
                waits=[(sSC, bev)], inc=sSp, inc_amt=16)
        cst_sp = [0, 0]
        for mj in range(8):
            pb = PS_A if mj % 2 == 0 else PS_B
            buf = mj % 2
            for kk in range(8):
                prog.emit("tensor", lambda t, mj=mj, kk=kk, c0=c0, pb=pb:
                          t.matmul(
                              pb[:, 0:512],
                              Wsb[:, kk * ZD + 3072 + 128 * mj:
                                  kk * ZD + 3072 + 128 * mj + 128],
                              xn[:, kk * XT + c0: kk * XT + c0 + 512],
                              start=(kk == 0), stop=(kk == 7)),
                          waits=([(sSC, None)] if kk == 0 else ()),
                          inc=sPE if kk == 7 else None)
            zp = sPE.n
            cev = prog.emit("scalar", lambda s, buf=buf, pb=pb: s.activation(
                cst[:, 512 * buf: 512 * buf + 512], pb[:, 0:512],
                AF.Identity), waits=[(sPE, zp), (sSp, cst_sp[buf])], inc=sSC)
            cst_sp[buf] = prog.emit(
                "sync", lambda s, mj=mj, s4=s4, buf=buf: s.dma_start(
                    out=Cfm_d[mj, :, 512 * s4: 512 * s4 + 512],
                    in_=cst[:, 512 * buf: 512 * buf + 512]),
                waits=[(sSC, cev)], inc=sSp, inc_amt=16)
        for cc in range(4):
            c = 4 * s4 + cc
            tc0 = 8 + 128 * cc
            prog.wait("vector", sSC, zx_sc)
            last_sil = sil_sc
            for fc in range(16):
                prog.emit("vector", lambda v, fc=fc, tc0=tc0: v.tensor_scalar(
                    cacc[:, :], xr[:, 520 * fc + tc0 - 3:
                                   520 * fc + tc0 + 125],
                    cwc[:, 4 * fc: 4 * fc + 1], None, ALU.mult),
                    waits=[(sSC, last_sil), (sPE, sil_pe)])
                for tap in range(1, 4):
                    prog.drain("vector")
                    prog.emit("vector", lambda v, fc=fc, tc0=tc0, tap=tap:
                              v.scalar_tensor_tensor(
                                  cacc[:, :],
                                  xr[:, 520 * fc + tc0 - 3 + tap:
                                     520 * fc + tc0 + 125 + tap],
                                  cwc[:, 4 * fc + tap: 4 * fc + tap + 1],
                                  cacc[:, :], ALU.mult, ALU.add))
                prog.drain("vector")
                cg = prog.emit("vector", lambda v: v.nop(), inc=sGP)
                last_sil = prog.emit("scalar", lambda s, fc=fc: s.activation(
                    sil[:, 128 * fc: 128 * fc + 128], cacc[:, :], AF.Silu,
                    bias=cbc[:, fc: fc + 1]), waits=[(sGP, cg)], inc=sSC)
            conv_gp = sGP.n
            sil_sc = last_sil
            for wv in range(2):
                tpw = None
                for sl in range(8):
                    fc = 8 * wv + sl
                    tpw = prog.emit("tensor", lambda t, fc=fc, sl=sl:
                                    t.transpose(
                        PS_T[:, 128 * sl: 128 * sl + 128],
                        sil[:, 128 * fc: 128 * fc + 128], idb[:, :]),
                        waits=([(sSC, sil_sc), (sDV, xt_dv), (sDV, bd_dv)]
                               if sl == 0 else ()),
                        inc=sPE if sl == 7 else None)
                prog.wait("vector", sPE, tpw)
                prog.wait("vector", sSp, xts_sp)
                for sl in range(8):
                    fc = 8 * wv + sl
                    prog.emit("vector", lambda v, fc=fc, sl=sl: v.tensor_copy(
                        xts[:, 128 * fc: 128 * fc + 128],
                        PS_T[:, 128 * sl: 128 * sl + 128]))
                prog.drain("vector")
                xt_dv = prog.emit("vector", lambda v: v.nop(), inc=sDV)
            sil_pe = sPE.n
            xts_sp = prog.emit("sync", lambda s, c=c: s.dma_start(
                out=Xtm_d[c, :, :], in_=xts[:, :]),
                waits=[(sDV, xt_dv)], inc=sSp, inc_amt=16)
            tpb = None
            for mj in range(8):
                tpb = prog.emit("tensor", lambda t, mj=mj, cc=cc: t.transpose(
                    PS_T[:, 128 * mj: 128 * mj + 128],
                    bst[:, 512 * mj + 128 * cc: 512 * mj + 128 * cc + 128],
                    idb[:, :]),
                    waits=([(sDV, xt_dv), (sDV, bd_dv)] if mj == 0 else ()),
                    inc=sPE if mj == 7 else None)
            bst_pe = tpb
            prog.wait("vector", sPE, tpb)
            prog.wait("vector", sSp, bds_sp)
            for mj in range(8):
                for hh in range(2):
                    h = 2 * mj + hh
                    prog.emit("vector", lambda v, mj=mj, hh=hh, h=h, c=c:
                              v.tensor_scalar(
                                  bds[:, 64 * h: 64 * h + 64],
                                  PS_T[:, 128 * mj + 64 * hh:
                                       128 * mj + 64 * hh + 64],
                                  ebd[:, 16 * c + h: 16 * c + h + 1],
                                  None, ALU.mult))
            prog.drain("vector")
            bd_dv = prog.emit("vector", lambda v: v.nop(), inc=sDV)
            bds_sp = prog.emit("sync", lambda s, c=c: s.dma_start(
                out=Btm_d[c, :, :], in_=bds[:, :]),
                waits=[(sDV, bd_dv)], inc=sSp, inc_amt=16)
            stt_dv = {}
            for h in range(NH):
                pg = PS_U if h % 2 == 0 else PS_S
                cm = prog.emit("tensor", lambda t, h=h, pg=pg: t.matmul(
                    pg[0:64, 0:128], bds[:, 64 * h: 64 * h + 64],
                    xts[:, 128 * h: 128 * h + 128], start=True, stop=True),
                    waits=([(sDV, bd_dv), (sDV, ctr_dv)] if h <= 1 else
                           [(sDV, stt_dv[h - 2])]), inc=sPE)
                stt_dv[h] = prog.emit(
                    "vector", lambda v, h=h, c=c, pg=pg:
                    v.scalar_tensor_tensor(
                        Gb[:, 128 * h: 128 * h + 128],
                        Gb[:, 128 * h: 128 * h + 128],
                        ela[:, 16 * c + h: 16 * c + h + 1],
                        pg[0:64, 0:128], ALU.mult, ALU.add),
                    waits=[(sPE, cm)], inc=sDV)
                prog.drain("vector")
            ctr_dv = stt_dv[NH - 1]
    prog.emit("sync", lambda s: s.dma_start(out=G_d, in_=Gb[:, :]),
              waits=[(sDV, ctr_dv)], inc=sSp, inc_amt=16)
    prog.emit("sync", lambda s: s.dma_start(
        out=xn_d.rearrange("(a p) t -> p a t", p=128),
        in_=xn.rearrange("p (a b) -> p a b", a=8)), inc=sSp, inc_amt=16)
    m1_done = dict(sp=sSp.n)

    if upto == "m1":
        for sm in (sW, sU, sSp, sOut):
            prog.wait("sync", sm)
        with nc.Block() as blk:
            prog.run(blk)
        ctx.close()
        return nc
    KT = TOK + 640
    LA = Layout(pool)
    LA.cur = HDR
    cslabF = LA.add("cslabF", 128, 8 * 512, F32)
    wqkv_sb = LA.add("wqkv_sb", 128, 8 * 1536, BF16)
    wo_sb = LA.add("wo_sb", 128, 4 * 1024, BF16)
    tbs = [LA.add(f"tb{i}", 128, NKB * 64, BF16) for i in range(4)]
    masksF = LA.add("masksF", 128, 6 * 640, BF16)
    qsrow = LA.add("qsrow", 1, 64, F32)
    ksrow = LA.add("ksrow", 1, 64, F32)
    csrowF = LA.add("csrowF", 1, 256, F32)
    assert LA.cur <= X1_AT, LA.cur
    x1 = pv(pool, X1_AT, 128, 8 * (TOK + 128), BF16)
    LB = Layout(pool)
    LB.cur = RB_AT
    kfm = LB.add("kfm", 128, 4 * KT, BF16)
    Vb = LB.add("Vb", 128, 6 * 512, BF16)
    Oring = LB.add("Oring", 128, 6 * 512, BF16)
    qsl = LB.add("qsl", 128, 2 * 512, BF16)
    kprep = LB.add("kprep", 128, 512, F32)
    qprep = LB.add("qprep", 128, 512, F32)
    rexpF = LB.add("rexpF", 128, 512, F32)
    rot1F = LB.add("rot1F", 128, 512, F32)
    sqbF = LB.add("sqbF", 128, 512, BF16)
    nrmF = LB.add("nrmF", 128, 16, F32)
    AparF = LB.add("AparF", 128, 2 * 640, BF16)
    osbF = LB.add("osbF", 128, 256, F32)
    assert LB.cur <= XN_AT, LB.cur

    for e in ("gpsimd", "sync"):
        for sm in (sPE, sSC, sDV, sGP):
            prog.wait(e, sm)
    for kk in range(8):
        prog.emit("gpsimd", lambda g, kk=kk: g.dma_start(
            out=wqkv_sb[:, kk * 1536:(kk + 1) * 1536],
            in_=wqkv_w[128 * kk:128 * (kk + 1), :]), inc=sW, inc_amt=16)
    for kk in range(4):
        prog.emit("gpsimd", lambda g, kk=kk: g.dma_start(
            out=wo_sb[:, kk * 1024:(kk + 1) * 1024],
            in_=wo_w[128 * kk:128 * (kk + 1), :]), inc=sW, inc_amt=16)
    for ti in range(4):
        prog.emit("gpsimd", lambda g, ti=ti: g.dma_start(
            out=tbs[ti].rearrange("p (a b) -> p a b", a=NKB),
            in_=rotab.rearrange("q (a p) b -> q p a b", p=128)[ti]),
            inc=sW, inc_amt=16)
    prog.emit("gpsimd", lambda g: g.dma_start(
        out=masksF.rearrange("p (a b) -> p a b", a=6),
        in_=masks_in.transpose([1, 0, 2])), inc=sW, inc_amt=16)
    prog.emit("sync", lambda s: s.dma_start(out=qsrow[0:1, :], in_=qs_in),
              inc=sU, inc_amt=16)
    prog.emit("sync", lambda s: s.dma_start(out=ksrow[0:1, :], in_=ks_in),
              inc=sU, inc_amt=16)

    prog.wait("gpsimd", sOut)
    prog.drain("gpsimd")
    ln_out_gp = prog.emit("gpsimd", lambda g: g.nop(), inc=sGP)
    P = dict(nc=nc, prog=prog, sPE=sPE, sDV=sDV, sSC=sSC, sGP=sGP, sW=sW,
             sU=sU, sSp=sSp, sOut=sOut, PS_A=PS_A, PS_B=PS_B, PS_S=PS_S, PS_T=PS_T,
             PS_U=PS_U, src=xn, xout=x1, wqkv_sb=wqkv_sb, wo_sb=wo_sb,
             tb=tbs, masks=masksF, qsrow=qsrow, ksrow=ksrow, kfm=kfm, Vb=Vb,
             Oring=Oring, qsl=qsl, kprep=kprep, qprep=qprep, rexp=rexpF,
             rot1=rot1F, sqb=sqbF, nrm=nrmF, Apar=AparF, csrow=csrowF,
             cslab=cslabF, out_d=out_d, oneb=oneb, onef=onef,
             idb=idb, osb=osbF)
    P["_acc_gp"] = ln_out_gp
    import os as _os
    P["stage"] = _os.environ.get("F1A_STAGE", "all")
    emit_attn(P)
    if upto == "f1a":
        for sm in (sW, sU, sSp, sOut):
            prog.wait("sync", sm)
        with nc.Block() as blk:
            prog.run(blk)
        ctx.close()
        return nc
    if DEV_DEBUG:
        dbg_x1 = dt_out("dbg_x1", [D, TOK + 128], BF16)
        dbg_kfm = dt_out("dbg_kfm", [128, 4 * KT], BF16)
        dbg_q = dt_out("dbg_q", [128, 1024], BF16)
        prog.emit("sync", lambda s: s.dma_start(
            out=dbg_x1.rearrange("(a p) t -> p a t", p=128),
            in_=x1.rearrange("p (a b) -> p a b", a=8)),
            waits=[(sDV, None)], inc=sSp, inc_amt=16)
        prog.emit("sync", lambda s: s.dma_start(out=dbg_kfm, in_=kfm[:, :]),
                  inc=sSp, inc_amt=16)
        prog.emit("sync", lambda s: s.dma_start(out=dbg_q, in_=qsl[:, :]),
                  inc=sSp, inc_amt=16)

    LM2 = Layout(pool)
    LM2.cur = LA.map["wqkv_sb"][0]
    hlnF = LM2.add("hlnF", 128, 8 * 512, BF16)
    sq2F = LM2.add("sq2F", 128, 8 * 512, BF16)
    geluF = LM2.add("geluF", 128, 512, F32)
    mrowF = LM2.add("mrowF", 1, 512, F32)
    mrow2F = LM2.add("mrow2F", 1, 512, F32)
    b1cF = LM2.add("b1cF", 128, 44, F32)
    b2cF = LM2.add("b2cF", 128, 8, F32)
    lngF = LM2.add("lngF", 128, 8, F32)
    lnbF = LM2.add("lnbF", 128, 8, F32)
    actF = LM2.add("actF", 128, 22 * 512, BF16)
    assert LM2.cur <= X1_AT, LM2.cur
    w1p = pv(pool, RB_AT, 128, 8 * 5632, BF16)
    w2p = pv(pool, RB_AT, 128, 22 * 1024, BF16)

    for kk in range(8):
        prog.emit("gpsimd", lambda g, kk=kk: g.dma_start(
            out=w1p[:, kk * 5632:(kk + 1) * 5632],
            in_=w1_w[128 * kk:128 * (kk + 1), :]),
            waits=([(sPE, None), (sSC, None), (sDV, None), (sSp, None)]
                   if kk == 0 else ()), inc=sW, inc_amt=16)
    for (dst, srcd) in ((b1cF, b1_w), (b2cF, b2_w), (lngF, lng_w),
                        (lnbF, lnb_w)):
        prog.emit("sync", lambda s, d=dst, r=srcd: s.dma_start(
            out=d[:, :], in_=r),
            waits=[(sPE, None), (sSC, None), (sDV, None)],
            inc=sU, inc_amt=16)
    P2 = dict(P)
    P2.update(xio=x1, w1p=w1p, w2p=w2p, b1c=b1cF, b2c=b2cF, lngc=lngF,
              lnbc=lnbF, hln=hlnF, sq2=sq2F, geluS=geluF, mrow=mrowF,
              mrow2=mrow2F, act=actF, act_d=act_d, cslab=cslabF,
              w2_w=w2_w, skip_x=False)
    emit_mlp(P2)

    prog.emit("sync", lambda s: s.dma_start(
        out=x1_d.rearrange("(a p) t -> p a t", p=128),
        in_=x1.rearrange("p (a b) -> p a b", a=8)[:, :, 128:128 + TOK]),
        waits=[(sDV, None)], inc=sSp, inc_amt=16)

    prog.drain("gpsimd")
    fin_gp = prog.emit("gpsimd", lambda g: g.nop(), inc=sGP)
    prog.wait("sync", sGP, fin_gp)
    for sm in (sW, sU, sSp, sOut):
        prog.wait("sync", sm)
    with nc.Block() as blk:
        prog.run(blk)
    ctx.close()
    return nc


def set_tok(tok):
    global TOK, XT, NCH, NQS, NKB
    TOK = tok
    XT = HALO + TOK
    NCH = TOK // 128
    NQS = TOK // 128 + 1
    NKB = (TOK + 640) // 128


def build_launch2(upto="all"):
    nc = bass.Bass("TRN2", target_bir_lowering=False)
    dt_in = lambda n, s, d=F32: nc.dram_tensor(n, s, d, kind="ExternalInput").ap()
    dt_out = lambda n, s, d=F32: nc.dram_tensor(n, s, d, kind="ExternalOutput").ap()
    dt_int = lambda n, s, d=F32: nc.dram_tensor(n, s, d).ap()

    out_init = dt_in("out_init", [D, TOK])
    xn_sp = dt_in("xn_sp", [D, XT], BF16)
    x1_sp = dt_in("x1_sp", [D, TOK], BF16)
    x1_halo = dt_in("x1_halo", [D, HALO], BF16)
    lam_sp = dt_in("lam_sp", [16, NCH * 128 + 1])
    Xtm_d = dt_in("Xtm_sp", [NCH, 128, NH * DH], BF16)
    Btm_d = dt_in("Btm_sp", [NCH, 128, NH * DS], BF16)
    Bfm_d = dt_in("Bfm_sp", [8, 128, TOK], BF16)
    Cfm_d = dt_in("Cfm_sp", [8, 128, TOK], BF16)
    Gin_d = dt_in("Gin", [3, DS, NH * DH])
    lamf_d = dt_in("lamf", [1, 48])
    Wg_w = dt_in("W_gate", [D, DI])
    Wo2_w = dt_in("W_out", [DI, D])
    nw_w = dt_in("norm_w", [128, 16])
    wqkv_w = dt_in("wqkv", [D, 3 * INNER])
    wo_w = dt_in("wo_w", [INNER, D])
    w1_w = dt_in("w1_w", [D, 2 * HIDP])
    b1_w = dt_in("b1_w", [128, 2 * HIDP // 128])
    w2_w = dt_in("w2_w", [HIDP, D])
    b2_w = dt_in("b2_w", [128, D // 128])
    lng_w = dt_in("lng", [128, 8])
    lnb_w = dt_in("lnb", [128, 8])
    qs_in = dt_in("qs", [1, DHEAD])
    ks_in = dt_in("ks", [1, DHEAD])
    rotab = dt_in("rotab", [4, NKB * 128, DHEAD])
    masks_in = dt_in("masks", [6, 128, 640])
    ident_f = dt_in("ident_f", [128, 128])
    iota2_in = dt_in("iota2", [128, 2])
    dmask_in = dt_in("dmask", [128, 128])
    out_d = dt_out("out", [D, TOK])
    act_d = dt_int("act_sc", [4, 128, 22 * 512], BF16)
    d_lr = dt_int("d_lr", [16, 129])
    d3 = dt_int("d3", [3, 2048])

    ctx = contextlib.ExitStack()
    pool = ctx.enter_context(nc.sbuf_tensor("pool", [128, POOL_COLS], BF16))
    ps = lambda n, s, d: ctx.enter_context(nc.psum_tensor(n, s, d))
    PS_A = ps("PS_A", [128, 512], F32)
    PS_B = ps("PS_B", [128, 512], F32)
    PS_S = ps("PS_S", [128, 2048], F32)
    PS_T = ps("PS_T", [128, 1024], BF16)
    PS_U = ps("PS_U", [128, 512], F32)
    mksem = lambda n, dma=False: Sem(ctx.enter_context(nc.semaphore(n)), dma)
    sW = mksem("sW", True); sU = mksem("sU", True); sSp = mksem("sSp", True)
    sPE = mksem("sPE"); sDV = mksem("sDV"); sSC = mksem("sSC")
    sGP = mksem("sGP"); sOut = mksem("sOut", True)
    prog = Prog(nc)

    HDR = 2048
    XN_AT = 161024 + HDR
    X1_AT = 70144 + HDR
    RB_AT = 104960 + HDR

    LM = Layout(pool)
    idf = LM.add("idf", 128, 128, F32)
    idb = LM.add("idb", 128, 128, BF16)
    oneb = LM.add("oneb", 128, 128, BF16)
    onef = LM.add("onef", 128, 128, F32)
    cap_c = LM.add("cap_c", 128, 8, F32)
    assert LM.cur <= HDR, LM.cur
    LM.cur = HDR
    Wg = LM.add("Wg", 128, 8 * DI, BF16)
    Wo2 = LM.add("Wo2", 128, 16 * 1024, BF16)
    Yb = LM.add("Yb", 128, 16 * 512, F32)
    gate = LM.add("gate", 128, 16 * 512, BF16)
    xnsl = LM.add("xnsl", 128, 8 * 512, BF16)
    lam = LM.add("lam", 16, 2056, F32)
    lamrow = LM.add("lamrow", 1, 2064, F32)
    relrow = LM.add("relrow", 1, 2048, F32)
    stagA = LM.add("stagA", 2, 2048, F32)
    stagB = LM.add("stagB", 2, 2048, F32)
    Xc = LM.add("Xc", 128, 2 * 2048, BF16)
    Bc = LM.add("Bc", 128, 2 * 1024, BF16)
    Cc = LM.add("Cc", 128, 2 * 1024, BF16)
    Bdc = LM.add("Bdc", 128, 2 * 1024, BF16)
    Hb = LM.add("Hb", 64, NH * DH, F32)
    Hbf = LM.add("Hbf", 64, NH * DH, BF16)
    ela2 = LM.add("ela2", 64, 16, F32)
    ecol = LM.add("ecol", 64, 48, F32)
    Dm = LM.add("Dm", 128, 128, F32)
    Sdb = LM.add("Sdb", 128, 128, BF16)
    Cdb = LM.add("Cdb", 64, 128, BF16)
    iota2 = LM.add("iota2", 128, 2, F32)
    dmask = LM.add("dmask", 128, 128, BF16)
    lamf = LM.add("lamf", 1, 48, F32)
    nwc = LM.add("nwc", 128, 16, F32)
    rrow = LM.add("rrow", 1, 512, F32)
    cslab2 = pv(pool, LM.map["stagA"][0], 128, 8 * 512, F32)
    Gsl = pv(pool, LM.map["stagA"][0], 64, NH * DH, F32)
    assert LM.cur <= POOL_COLS * 2, LM.cur

    for i, vv in enumerate(CONST_VALS):
        nc.const_aps.register(F32, vv, cap_c[:, i:i + 1])
        prog.emit("vector", lambda v, i=i, vv=vv: v.memset(
            cap_c[:, i:i + 1], vv))
    prog.emit("vector", lambda v: v.memset(oneb[:, :], 1.0))
    prog.emit("vector", lambda v: v.memset(onef[:, :], 1.0))
    prog.emit("vector", lambda v: v.memset(relrow[0:1, :], 1.0))
    prog.drain("vector")
    ones_dv = prog.emit("vector", lambda v: v.nop(), inc=sDV)
    for e in ("scalar", "tensor", "gpsimd", "sync"):
        prog.wait(e, sDV, ones_dv)
    prog.emit("sync", lambda s: s.dma_start(out=d3[0:1, :],
              in_=relrow[0:1, 0:2048]), inc=sSp, inc_amt=16)
    prog.emit("sync", lambda s: s.dma_start(out=d3[2:3, :],
              in_=relrow[0:1, 0:2048]), inc=sSp, inc_amt=16)
    prog.emit("gpsimd", lambda g: g.dma_start(out=out_d, in_=out_init),
              inc=sOut, inc_amt=16)
    prog.drain("gpsimd")
    prog.emit("gpsimd", lambda g: g.nop(), inc=sGP)
    prog.emit("gpsimd", lambda g: g.dma_start(out=dmask[:, :], in_=dmask_in),
              inc=sW, inc_amt=16)
    for (dst, srcd) in ((idf, ident_f), (iota2, iota2_in), (nwc, nw_w),
                        (lamf, lamf_d)):
        prog.emit("sync", lambda s, d=dst, r=srcd: s.dma_start(
            out=d[:, :], in_=r), inc=sU, inc_amt=16)
    prog.emit("sync", lambda s: s.dma_start(
        out=lam[:, 0:NCH * 128 + 1], in_=lam_sp), inc=sU, inc_amt=16)
    prog.emit("gpsimd", lambda g: g.dma_start(out=idb[:, :], in_=ident_f),
              inc=sW, inc_amt=16)
    for kk in range(8):
        prog.emit("gpsimd", lambda g, kk=kk: g.dma_start(
            out=Wg[:, kk * DI:(kk + 1) * DI],
            in_=Wg_w[128 * kk:128 * (kk + 1), :]), inc=sW, inc_amt=16)
    for kk in range(16):
        prog.emit("gpsimd", lambda g, kk=kk: g.dma_start(
            out=Wo2[:, kk * 1024:(kk + 1) * 1024],
            in_=Wo2_w[128 * kk:128 * (kk + 1), :]), inc=sW, inc_amt=16)

    el = prog.emit("scalar", lambda s: s.activation(
        lamf[0:1, :], lamf[0:1, :], AF.Exp), waits=[(sU, None)], inc=sSC)
    eb = prog.emit("tensor", lambda t: t.matmul(
        PS_U[0:64, 0:48], onef[0:1, 0:64], lamf[0:1, 0:48],
        start=True, stop=True), waits=[(sSC, el)], inc=sPE)
    ec_sc = prog.emit("scalar", lambda s: s.activation(
        ecol[:, :], PS_U[0:64, 0:48], AF.Identity),
        waits=[(sPE, eb)], inc=sSC)
    prog.emit("vector", lambda v: v.memset(Hb[:, :], 0.0),
              waits=[(sSC, ec_sc)])
    prog.drain("vector")
    hdv = prog.emit("vector", lambda v: v.nop(), inc=sDV)
    for j in range(3):
        gl = prog.emit("sync", lambda s, j=j: s.dma_start(
            out=Gsl[:, :], in_=Gin_d[j]), waits=[(sDV, hdv)],
            inc=sU, inc_amt=16)
        prog.wait("vector", sU, gl)
        for h in range(NH):
            prog.emit("vector", lambda v, j=j, h=h: v.scalar_tensor_tensor(
                Hb[:, 128 * h: 128 * h + 128], Hb[:, 128 * h: 128 * h + 128],
                ecol[:, 16 * j + h: 16 * j + h + 1],
                Gsl[:, 128 * h: 128 * h + 128], ALU.mult, ALU.add))
            prog.drain("vector")
        hdv = prog.emit("vector", lambda v: v.nop(), inc=sDV)
    prog.wait("vector", sDV, hdv)
    prog.emit("vector", lambda v: v.tensor_copy(Hbf[:, :], Hb[:, :]))
    prog.drain("vector")
    hdv = prog.emit("vector", lambda v: v.nop(), inc=sDV)

    if upto == "h":
        for sm in (sW, sU, sSp, sOut):
            prog.wait("sync", sm)
        with nc.Block() as blk:
            prog.run(blk)
        ctx.close()
        return nc
    prog.wait("vector", sW)
    acc_gp = 0
    hup_dv = [hdv] * NH
    P2_sd = [0]
    P2_y = [0, 0]
    P2_prd = [(0, 0), (0, 0)]
    yev_sc = 0
    epi_dv = hdv
    epi_sc = 0
    ld_done = [0, 0]
    st_dv = hdv
    for c in range(NCH):
        buf = c % 2
        for (dst, dsrc) in (
                (Xc[:, 2048 * buf:2048 * buf + 2048], Xtm_d[c]),
                (Bdc[:, 1024 * buf:1024 * buf + 1024], Btm_d[c]),
                (Bc[:, 1024 * buf:1024 * buf + 1024],
                 Bfm_d[:, :, 128 * c:128 * c + 128].transpose([1, 0, 2])),
                (Cc[:, 1024 * buf:1024 * buf + 1024],
                 Cfm_d[:, :, 128 * c:128 * c + 128].transpose([1, 0, 2]))):
            prog.emit("sync", lambda s, d=dst, r=dsrc: s.dma_start(
                out=d if len(r.shape) == 2 else
                d.rearrange("p (a b) -> p a b", a=8), in_=r),
                waits=[(sPE, ld_done[buf]), (sDV, st_dv)],
                inc=sU, inc_amt=16)
        cld = sU.n
        prog.emit("sync", lambda s, c=c: s.dma_start(
            out=d_lr, in_=lam[:, 128 * c:128 * c + 129]),
            waits=[(sDV, st_dv), (sSC, epi_sc)], inc=sSp, inc_amt=16)
        prog.emit("sync", lambda s: s.dma_start(
            out=lamrow[0:1, 0:2064],
            in_=d_lr.rearrange("a b -> (a b)")[None, :]),
            inc=sSp, inc_amt=16)
        prog.wait("vector", sSp, None)
        prog.wait("vector", sPE, None)
        rel_ap_in = bass.AP(lamrow.tensor, lamrow.offset + 1,
                            [list(lamrow.ap)[0], [129, 16], [1, 128]])
        off_ap = bass.AP(lamrow.tensor, lamrow.offset,
                         [list(lamrow.ap)[0], [129, 16], [0, 128]])
        rl = prog.emit("vector", lambda v, rel_ap_in=rel_ap_in, off_ap=off_ap:
                       v.tensor_tensor(
            relrow[0:1, 0:2048].rearrange("p (a b) -> p a b", a=16),
            rel_ap_in, off_ap, ALU.subtract), inc=sDV)
        prog.emit("sync", lambda s: s.dma_start(
            out=d3[1:2, :], in_=relrow[0:1, 0:2048]),
            waits=[(sDV, rl)], inc=sSp, inc_amt=16)
        ex_sc = prog.emit("scalar", lambda s: s.activation(
            relrow[0:1, 0:2048], relrow[0:1, 0:2048], AF.Exp),
            waits=[(sSp, None), (sDV, rl)], inc=sSC)
        sA = prog.emit("sync", lambda s: s.dma_start(
            out=stagA[0:2, :], in_=d3[1:3, :]),
            waits=[(sDV, st_dv), (sOut, None), (sGP, acc_gp), (sPE, None)],
            inc=sSp, inc_amt=16)
        sB = prog.emit("sync", lambda s: s.dma_start(
            out=stagB[0:2, :], in_=d3[0:2, :]), inc=sSp, inc_amt=16)
        prog.wait("vector", sSp, None)
        ng = prog.emit("vector", lambda v: v.tensor_scalar_mul(
            stagA[0:1, :], stagA[0:1, :], -1.0), inc=sDV)
        erp_ap = bass.AP(relrow.tensor, relrow.offset + 127,
                         [list(relrow.ap)[0], [128, 16]])
        em = prog.emit("tensor", lambda t, erp_ap=erp_ap: t.matmul(
            PS_U[0:64, 64:80], onef[0:1, 0:64], erp_ap,
            start=True, stop=True),
            waits=[(sSC, ex_sc), (sSC, epi_sc), (sDV, None)], inc=sPE)
        ela_sc = prog.emit("scalar", lambda s: s.activation(
            ela2[:, :], PS_U[0:64, 64:80], AF.Identity),
            waits=[(sPE, em)], inc=sSC)
        prog.wait("tensor", sU, cld)
        prog.wait("tensor", sSC, ela_sc)
        prog.wait("tensor", sDV, ng)
        prog.wait("tensor", sDV, epi_dv)
        import os as _os
        m2s = _os.environ.get("M2_STAGE", "all")
        if m2s == "stream":
            st_dv = sDV.n
            continue
        for h in range(NH):
            pr = (c * NH + h) % 2
            sb = 512 * pr
            pY = PS_U if pr == 0 else PS_B
            prog.wait("tensor", sDV, P2_prd[pr][0])
            prog.wait("tensor", sSC, P2_prd[pr][1])
            sm = prog.emit("tensor", lambda t, h=h, buf=buf, sb=sb: t.matmul(
                PS_S[:, sb + 0: sb + 128],
                Bc[64 * (h % 2): 64 * (h % 2) + 64,
                   1024 * buf + 128 * (h // 2): 1024 * buf + 128 * (h // 2) + 128],
                Cc[64 * (h % 2): 64 * (h % 2) + 64,
                   1024 * buf + 128 * (h // 2): 1024 * buf + 128 * (h // 2) + 128],
                start=True, stop=True), inc=sPE)
            dm_ = prog.emit("tensor", lambda t, h=h, sb=sb: t.matmul(
                PS_S[:, sb + 128: sb + 256], stagA[0:2, 128 * h:128 * h + 128],
                stagB[0:2, 128 * h:128 * h + 128], start=True, stop=True),
                inc=sPE)
            tm = prog.emit("vector", lambda v, sb=sb: v.tensor_tensor(
                PS_S[:, sb + 128: sb + 256], PS_S[:, sb + 128: sb + 256],
                dmask[:, :], ALU.add), waits=[(sPE, dm_)], inc=sDV)
            exd = prog.emit("scalar", lambda s, sb=sb: s.activation(
                Dm[:, :], PS_S[:, sb + 128: sb + 256], AF.Exp),
                waits=[(sDV, tm), (sDV, P2_sd[0])], inc=sSC)
            sd = prog.emit("vector", lambda v, sb=sb: v.tensor_tensor(
                Sdb[:, :], PS_S[:, sb + 0: sb + 128], Dm[:, :], ALU.mult),
                waits=[(sSC, exd), (sPE, sm), (sPE, P2_y[0])], inc=sDV)
            P2_sd[0] = sd
            if m2s == "sd":
                P2_prd[pr] = (sd, exd)
                continue
            cb = prog.emit("tensor", lambda t, h=h, sb=sb: t.matmul(
                PS_S[:, sb + 256: sb + 384][0:64, :], onef[0:1, 0:64],
                relrow[0:1, 128 * h:128 * h + 128], start=True, stop=True),
                waits=[(sDV, sd)], inc=sPE)
            cd = prog.emit("vector", lambda v, h=h, buf=buf, sb=sb:
                           v.tensor_tensor(
                Cdb[:, :],
                Cc[64 * (h % 2): 64 * (h % 2) + 64,
                   1024 * buf + 128 * (h // 2): 1024 * buf + 128 * (h // 2) + 128],
                PS_S[:, sb + 256: sb + 384][0:64, :], ALU.mult),
                waits=[(sPE, cb), (sPE, P2_y[1])], inc=sDV)
            y1 = prog.emit("tensor", lambda t, h=h, buf=buf, pY=pY: t.matmul(
                pY[:, 0:128], Xc[:, 2048 * buf + 128 * h:
                                 2048 * buf + 128 * h + 128],
                Sdb[:, :], start=True, stop=False),
                waits=[(sDV, sd), (sDV, hup_dv[h]), (sSC, yev_sc),
                       (sDV, epi_dv)], inc=sPE)
            y2 = prog.emit("tensor", lambda t, h=h, pY=pY: t.matmul(
                pY[:, 0:128], Hbf[0:64, 128 * h:128 * h + 128], Cdb[:, :],
                start=False, stop=True), waits=[(sDV, cd)], inc=sPE)
            P2_y[0] = y1
            P2_y[1] = y2
            P2_prd[pr] = (hup_dv[h], exd)
            if m2s == "y":
                P2_prd[pr] = (cd, exd)
                yev_sc = prog.emit("scalar", lambda s, h=h, c=c, pY=pY:
                                   s.activation(
                    Yb[:, 512 * h + 128 * (c % 4):
                       512 * h + 128 * (c % 4) + 128],
                    pY[:, 0:128], AF.Identity), waits=[(sPE, y2)], inc=sSC)
                continue
            gm = prog.emit("tensor", lambda t, h=h, buf=buf, sb=sb: t.matmul(
                PS_S[0:64, sb + 384: sb + 512],
                Bdc[:, 1024 * buf + 64 * h: 1024 * buf + 64 * h + 64],
                Xc[:, 2048 * buf + 128 * h: 2048 * buf + 128 * h + 128],
                start=True, stop=True), inc=sPE)
            if m2s == "g":
                P2_prd[pr] = (cd, exd)
                hup_dv[h] = prog.emit("vector", lambda v, sb=sb:
                                      v.tensor_copy(
                    Dm[:, 0:64], PS_S[0:64, sb + 384: sb + 448]),
                    waits=[(sPE, gm)], inc=sDV)
                yev_sc = prog.emit("scalar", lambda s, h=h, c=c, pY=pY:
                                   s.activation(
                    Yb[:, 512 * h + 128 * (c % 4):
                       512 * h + 128 * (c % 4) + 128],
                    pY[:, 0:128], AF.Identity), waits=[(sPE, gm)], inc=sSC)
                continue
            prog.emit(
                "vector", lambda v, h=h, sb=sb: v.scalar_tensor_tensor(
                    Hb[:, 128 * h: 128 * h + 128],
                    Hb[:, 128 * h: 128 * h + 128],
                    ela2[:, h: h + 1], PS_S[0:64, sb + 384: sb + 512],
                    ALU.mult, ALU.add), waits=[(sPE, gm)])
            prog.drain("vector")
            hup_dv[h] = prog.emit("vector", lambda v, h=h: v.tensor_copy(
                Hbf[:, 128 * h: 128 * h + 128],
                Hb[:, 128 * h: 128 * h + 128]), inc=sDV)
            prog.drain("vector")
            yev_sc = prog.emit("scalar", lambda s, h=h, c=c, pY=pY:
                               s.activation(
                Yb[:, 512 * h + 128 * (c % 4): 512 * h + 128 * (c % 4) + 128],
                pY[:, 0:128], AF.Identity), waits=[(sPE, gm)], inc=sSC)
        ld_done[buf] = sPE.n
        st_dv = sDV.n
        if c % 4 == 3 and m2s not in ("units",):
            s4 = c // 4
            if DEV_DEBUG and s4 == 0:
                dbg_Y = nc.dram_tensor("dbg_Y", [128, 16 * 512], F32,
                                       kind="ExternalOutput").ap()
                prog.emit("sync", lambda s: s.dma_start(
                    out=dbg_Y, in_=Yb[:, :]), waits=[(sSC, yev_sc)],
                    inc=sSp, inc_amt=16)
            xl = prog.emit("sync", lambda s, s4=s4: s.dma_start(
                out=xnsl.rearrange("p (a b) -> p a b", a=8),
                in_=_tslice(xn_sp, OWN0 + 512 * s4, 512)),
                waits=[(sPE, None)], inc=sU, inc_amt=16)
            gev = 0
            for mj in range(16):
                pb = PS_A if mj % 2 == 0 else PS_B
                gl_ = None
                for kk in range(8):
                    gl_ = prog.emit("tensor",
                                    lambda t, mj=mj, kk=kk, pb=pb: t.matmul(
                        pb[:, 0:512],
                        Wg[:, kk * DI + 128 * mj: kk * DI + 128 * mj + 128],
                        xnsl[:, kk * 512: kk * 512 + 512],
                        start=(kk == 0), stop=(kk == 7)),
                        waits=([(sU, None), (sW, None), (sSC, yev_sc),
                                (sSC, gev), (sDV, None)] if kk == 0 else ()),
                        inc=sPE if kk == 7 else None)
                gev = prog.emit("scalar", lambda s, mj=mj, pb=pb:
                                s.activation(
                    gate[:, 512 * mj: 512 * mj + 512], pb[:, 0:512],
                    AF.Sigmoid), waits=[(sPE, gl_)], inc=sSC)
            prog.wait("vector", sSC, gev)
            prog.wait("vector", sSp)
            for mj in range(16):
                prog.emit("vector", lambda v, mj=mj: v.tensor_mul(
                    Yb[:, 512 * mj: 512 * mj + 512],
                    Yb[:, 512 * mj: 512 * mj + 512],
                    gate[:, 512 * mj: 512 * mj + 512]))
            prog.drain("vector")
            yg_dv = prog.emit("vector", lambda v: v.nop(), inc=sDV)
            sqs = prog.emit("scalar", lambda s: [s.activation(
                gate[:, 512 * mj: 512 * mj + 512],
                Yb[:, 512 * mj: 512 * mj + 512], AF.Square)
                for mj in range(16)][-1], waits=[(sDV, yg_dv)], inc=sSC)
            sq_mm = None
            for mj in range(16):
                sq_mm = prog.emit("tensor", lambda t, mj=mj: t.matmul(
                    PS_U[0:1, 0:512], oneb[:, 0:1],
                    gate[:, 512 * mj: 512 * mj + 512],
                    start=(mj == 0), stop=(mj == 15)),
                    waits=([(sSC, sqs)] if mj == 0 else ()),
                    inc=sPE if mj == 15 else None)
            rs = prog.emit("scalar", lambda s: s.activation(
                rrow[0:1, 0:512], PS_U[0:1, 0:512], AF.Sqrt,
                scale=1.0 / DI, bias=RMS_EPS), waits=[(sPE, sq_mm)], inc=sSC)
            prog.wait("vector", sSC, rs)
            prog.emit("vector", lambda v: v.reciprocal(
                rrow[0:1, 0:512], rrow[0:1, 0:512]))
            prog.drain("vector")
            rdv = prog.emit("vector", lambda v: v.nop(), inc=sDV)
            rb = prog.emit("tensor", lambda t: t.matmul(
                PS_S[:, 0:512], onef[0:1, :], rrow[0:1, 0:512],
                start=True, stop=True),
                waits=[(sDV, rdv), (sDV, None), (sSC, None)], inc=sPE)
            prog.wait("vector", sPE, rb)
            prog.wait("vector", sSC, sqs)
            yn_dv = None
            for mj in range(16):
                yn_dv = prog.emit("vector", lambda v, mj=mj:
                                  v.scalar_tensor_tensor(
                    gate[:, 512 * mj: 512 * mj + 512],
                    Yb[:, 512 * mj: 512 * mj + 512],
                    nwc[:, mj: mj + 1], PS_S[:, 0:512],
                    ALU.mult, ALU.mult), inc=sDV)
            prog.drain("vector")
            yn_dv = prog.emit("vector", lambda v: v.nop(), inc=sDV)
            wev = 0
            for m in range(8):
                pb = PS_A if m % 2 == 0 else PS_B
                wm = None
                for kk in range(16):
                    wm = prog.emit("tensor", lambda t, kk=kk, m=m, pb=pb:
                                   t.matmul(
                        pb[:, 0:512],
                        Wo2[:, kk * 1024 + 128 * m: kk * 1024 + 128 * m + 128],
                        gate[:, 512 * kk: 512 * kk + 512],
                        start=(kk == 0), stop=(kk == 15)),
                        waits=([(sDV, yn_dv), (sSC, wev), (sSp, None)]
                               if kk == 0 else ()),
                        inc=sPE if kk == 15 else None)
                wev = prog.emit("scalar", lambda s, m=m, pb=pb: s.activation(
                    cslab2[:, 512 * m: 512 * m + 512], pb[:, 0:512],
                    AF.Identity), waits=[(sPE, wm), (sOut, None),
                                          (sGP, acc_gp)], inc=sSC)
            if DEV_DEBUG and s4 == 0:
                dbg_c2 = nc.dram_tensor("dbg_c2", [128, 8 * 512], F32,
                                        kind="ExternalOutput").ap()
                prog.emit("sync", lambda s: s.dma_start(
                    out=dbg_c2, in_=cslab2[:, :]), waits=[(sSC, wev)],
                    inc=sSp, inc_amt=16)
            acc_gp = emit_accum(
                prog, sGP, _tslice(out_d, 512 * s4, 512),
                cslab2.rearrange("p (a b) -> p a b", a=8),
                ([(sSp, None)] if (DEV_DEBUG and s4 == 0) else [])
                + [(sSC, wev), (sGP, acc_gp)], sOut=sOut)
            epi_dv = sDV.n
            epi_sc = sSC.n

    if upto == "scan":
        for sm in (sW, sU, sSp, sOut):
            prog.wait("sync", sm)
        with nc.Block() as blk:
            prog.run(blk)
        ctx.close()
        return nc
    KT = TOK + 640
    x1full = pv(pool, XN_AT, 128, 8 * XT, BF16)
    for e in ("gpsimd", "sync"):
        for sm in (sPE, sSC, sDV, sGP):
            prog.wait(e, sm)
    prog.wait("sync", sOut)
    x1l = prog.emit("sync", lambda s: s.dma_start(
        out=x1full.rearrange("p (a b) -> p a b", a=8)[:, :, 0:HALO],
        in_=x1_halo.rearrange("(a p) t -> p a t", p=128)),
        inc=sU, inc_amt=16)
    x1l = prog.emit("sync", lambda s: s.dma_start(
        out=x1full.rearrange("p (a b) -> p a b", a=8)[:, :, HALO:XT],
        in_=x1_sp.rearrange("(a p) t -> p a t", p=128)),
        inc=sU, inc_amt=16)
    LA = Layout(pool)
    LA.cur = HDR
    cslabF = LA.add("cslabF", 128, 8 * 512, F32)
    wqkv_sb = LA.add("wqkv_sb", 128, 8 * 1536, BF16)
    wo_sb = LA.add("wo_sb", 128, 4 * 1024, BF16)
    tbs = [LA.add(f"tb{i}", 128, NKB * 64, BF16) for i in range(4)]
    masksF = LA.add("masksF", 128, 6 * 640, BF16)
    qsrow = LA.add("qsrow", 1, 64, F32)
    ksrow = LA.add("ksrow", 1, 64, F32)
    csrowF = LA.add("csrowF", 1, 256, F32)
    assert LA.cur <= X1_AT, LA.cur
    x2 = pv(pool, X1_AT, 128, 8 * (TOK + 128), BF16)
    LB = Layout(pool)
    LB.cur = RB_AT
    kfm = LB.add("kfm", 128, 4 * KT, BF16)
    Vb = LB.add("Vb", 128, 6 * 512, BF16)
    Oring = LB.add("Oring", 128, 6 * 512, BF16)
    qsl = LB.add("qsl", 128, 2 * 512, BF16)
    kprep = LB.add("kprep", 128, 512, F32)
    qprep = LB.add("qprep", 128, 512, F32)
    rexpF = LB.add("rexpF", 128, 512, F32)
    rot1F = LB.add("rot1F", 128, 512, F32)
    sqbF = LB.add("sqbF", 128, 512, BF16)
    nrmF = LB.add("nrmF", 128, 16, F32)
    AparF = LB.add("AparF", 128, 2 * 640, BF16)
    osbF = LB.add("osbF", 128, 256, F32)
    assert LB.cur <= XN_AT, LB.cur

    for kk in range(8):
        prog.emit("gpsimd", lambda g, kk=kk: g.dma_start(
            out=wqkv_sb[:, kk * 1536:(kk + 1) * 1536],
            in_=wqkv_w[128 * kk:128 * (kk + 1), :]), inc=sW, inc_amt=16)
    for kk in range(4):
        prog.emit("gpsimd", lambda g, kk=kk: g.dma_start(
            out=wo_sb[:, kk * 1024:(kk + 1) * 1024],
            in_=wo_w[128 * kk:128 * (kk + 1), :]), inc=sW, inc_amt=16)
    for ti in range(4):
        prog.emit("gpsimd", lambda g, ti=ti: g.dma_start(
            out=tbs[ti].rearrange("p (a b) -> p a b", a=NKB),
            in_=rotab.rearrange("q (a p) b -> q p a b", p=128)[ti]),
            inc=sW, inc_amt=16)
    prog.emit("gpsimd", lambda g: g.dma_start(
        out=masksF.rearrange("p (a b) -> p a b", a=6),
        in_=masks_in.transpose([1, 0, 2])), inc=sW, inc_amt=16)
    prog.emit("sync", lambda s: s.dma_start(out=qsrow[0:1, :], in_=qs_in),
              inc=sU, inc_amt=16)
    prog.emit("sync", lambda s: s.dma_start(out=ksrow[0:1, :], in_=ks_in),
              inc=sU, inc_amt=16)

    P = dict(nc=nc, prog=prog, sPE=sPE, sDV=sDV, sSC=sSC, sGP=sGP, sW=sW,
             sU=sU, sSp=sSp, sOut=sOut, PS_A=PS_A, PS_B=PS_B, PS_S=PS_S,
             PS_T=PS_T, PS_U=PS_U, src=x1full, xout=x2, wqkv_sb=wqkv_sb,
             wo_sb=wo_sb, tb=tbs, masks=masksF, qsrow=qsrow, ksrow=ksrow,
             kfm=kfm, Vb=Vb, Oring=Oring, qsl=qsl, kprep=kprep, qprep=qprep,
             rexp=rexpF, rot1=rot1F, sqb=sqbF, nrm=nrmF, Apar=AparF,
             csrow=csrowF, cslab=cslabF, out_d=out_d, oneb=oneb, onef=onef,
             idb=idb, osb=osbF, tag="2")
    P["_acc_gp"] = acc_gp
    import os as _os
    P["stage"] = _os.environ.get("F1A_STAGE", "all")
    emit_attn(P)
    if upto == "f2a":
        for sm in (sW, sU, sSp, sOut):
            prog.wait("sync", sm)
        with nc.Block() as blk:
            prog.run(blk)
        ctx.close()
        return nc

    LM2 = Layout(pool)
    LM2.cur = LA.map["wqkv_sb"][0]
    hlnF = LM2.add("hlnF", 128, 8 * 512, BF16)
    sq2F = LM2.add("sq2F", 128, 8 * 512, BF16)
    geluF = LM2.add("geluF", 128, 512, F32)
    mrowF = LM2.add("mrowF", 1, 512, F32)
    mrow2F = LM2.add("mrow2F", 1, 512, F32)
    b1cF = LM2.add("b1cF", 128, 44, F32)
    b2cF = LM2.add("b2cF", 128, 8, F32)
    lngF = LM2.add("lngF", 128, 8, F32)
    lnbF = LM2.add("lnbF", 128, 8, F32)
    actF = LM2.add("actF", 128, 22 * 512, BF16)
    assert LM2.cur <= X1_AT, LM2.cur
    w1p = pv(pool, RB_AT, 128, 8 * 5632, BF16)
    w2p = pv(pool, RB_AT, 128, 22 * 1024, BF16)
    for kk in range(8):
        prog.emit("gpsimd", lambda g, kk=kk: g.dma_start(
            out=w1p[:, kk * 5632:(kk + 1) * 5632],
            in_=w1_w[128 * kk:128 * (kk + 1), :]),
            waits=([(sPE, None), (sSC, None), (sDV, None), (sSp, None)]
                   if kk == 0 else ()), inc=sW, inc_amt=16)
    for (dst, srcd) in ((b1cF, b1_w), (b2cF, b2_w), (lngF, lng_w),
                        (lnbF, lnb_w)):
        prog.emit("sync", lambda s, d=dst, r=srcd: s.dma_start(
            out=d[:, :], in_=r),
            waits=[(sPE, None), (sSC, None), (sDV, None)],
            inc=sU, inc_amt=16)
    P2 = dict(P)
    P2.update(xio=x2, w1p=w1p, w2p=w2p, b1c=b1cF, b2c=b2cF, lngc=lngF,
              lnbc=lnbF, hln=hlnF, sq2=sq2F, geluS=geluF, mrow=mrowF,
              mrow2=mrow2F, act=actF, act_d=act_d, cslab=cslabF,
              w2_w=w2_w, skip_x=True)
    emit_mlp(P2)

    prog.drain("gpsimd")
    fin_gp = prog.emit("gpsimd", lambda g: g.nop(), inc=sGP)
    prog.wait("sync", sGP, fin_gp)
    for sm in (sW, sU, sSp, sOut):
        prog.wait("sync", sm)
    with nc.Block() as blk:
        prog.run(blk)
    ctx.close()
    return nc


_IOTA2 = None


def _core_inputs_l1(inputs, c):
    b, p = c // 4, c % 4
    S = TOK * p
    u = inputs["L0_attn_u"][b]
    useg = np.zeros((XT, D), np.float32)
    lo = max(0, S - HALO)
    useg[HALO - (S - lo):HALO + TOK] = u[lo:S + TOK]
    w1p, b1p = _pad_w1(inputs["L0_ffn_w1"][0], inputs["L0_ffn_b1"][0])
    cw = np.zeros((128, 64), np.float32)
    cwi = inputs["L0_attn_conv_w"]
    for fc in range(16):
        for k in range(4):
            cw[:, 4 * fc + k] = cwi[128 * fc:128 * (fc + 1), k]
    mtiles = _attn_mask_tiles(p)
    return {
        "u_t": np.ascontiguousarray(useg.T),
        "W_in": np.asarray(inputs["L0_attn_W_in"], np.float32),
        "conv_w": cw,
        "conv_b": _cols(inputs["L0_attn_conv_b"]),
        "delta_p": np.asarray(inputs["L0_attn_delta_param"],
                              np.float32).reshape(16, 1),
        "ident_f": np.eye(128, dtype=np.float32),
        "wqkv": np.asarray(inputs["L0_ffn_wqkv"][0], np.float32),
        "wo_w": np.asarray(inputs["L0_ffn_wo"][0], np.float32),
        "qs": np.asarray(inputs["L0_ffn_q_scale"][0],
                         np.float32).reshape(1, 64),
        "ks": np.asarray(inputs["L0_ffn_k_scale"][0],
                         np.float32).reshape(1, 64),
        "rotab": np.ascontiguousarray(_rotary_tables(S - 640, TOK + 640)),
        "masks": mtiles,
        "w1_w": w1p,
        "b1_w": _cols(b1p),
        "w2_w": _pad_w2(inputs["L0_ffn_w2"][0]),
        "b2_w": _cols(inputs["L0_ffn_b2"][0]),
        "lng": _cols(inputs["L0_ffn_ln_g"][0]),
        "lnb": _cols(inputs["L0_ffn_ln_b"][0]),
    }


def _core_inputs_l2(inputs, c, r1, m1):
    b, p = c // 4, c % 4
    S = TOK * p
    Gin = np.zeros((3, DS, NH * DH), np.float32)
    lamf = np.zeros((1, 48), np.float32)
    for j in range(p):
        pred = 4 * b + j
        Gin[3 - p + j] = r1[pred]["G_sp"].reshape(DS, NH * DH)
        lamf[0, 16 * (3 - p + j):16 * (3 - p + j) + 16] = \
            r1[pred]["lam_sp"].reshape(16, TOK + 1)[:, TOK]
    if p == 0:
        x1h = np.zeros((D, HALO), dtype=bf16)
    else:
        x1h = np.ascontiguousarray(
            r1[c - 1]["x1_sp"].reshape(D, TOK)[:, TOK - HALO:])
    d = {
        "out_init": r1[c]["out_part"].reshape(D, TOK),
        "xn_sp": r1[c]["xn_sp"].reshape(D, XT),
        "x1_sp": r1[c]["x1_sp"].reshape(D, TOK),
        "x1_halo": x1h,
        "lam_sp": r1[c]["lam_sp"].reshape(16, TOK + 1),
        "Xtm_sp": r1[c]["Xtm_sp"].reshape(NCH, 128, NH * DH),
        "Btm_sp": r1[c]["Btm_sp"].reshape(NCH, 128, NH * DS),
        "Bfm_sp": r1[c]["Bfm_sp"].reshape(8, 128, TOK),
        "Cfm_sp": r1[c]["Cfm_sp"].reshape(8, 128, TOK),
        "Gin": Gin,
        "lamf": lamf,
        "W_gate": np.asarray(inputs["L0_attn_W_gate"], np.float32),
        "W_out": np.asarray(inputs["L0_attn_W_out"], np.float32),
        "norm_w": _cols(inputs["L0_attn_norm_w"]),
        "wqkv": np.asarray(inputs["L0_ffn_wqkv"][1], np.float32),
        "wo_w": np.asarray(inputs["L0_ffn_wo"][1], np.float32),
        "qs": np.asarray(inputs["L0_ffn_q_scale"][1],
                         np.float32).reshape(1, 64),
        "ks": np.asarray(inputs["L0_ffn_k_scale"][1],
                         np.float32).reshape(1, 64),
        "w1_w": m1["w1p2"],
        "b1_w": m1["b1p2"],
        "w2_w": m1["w2p2"],
        "b2_w": _cols(inputs["L0_ffn_b2"][1]),
        "lng": _cols(inputs["L0_ffn_ln_g"][1]),
        "lnb": _cols(inputs["L0_ffn_ln_b"][1]),
        "rotab": m1["rotab"][p],
        "masks": m1["masks"][p],
        "ident_f": np.eye(128, dtype=np.float32),
        "iota2": np.stack([np.arange(128, dtype=np.float32),
                           np.full(128, 128.0, np.float32)], 1),
        "dmask": np.where(np.arange(128)[None, :] >= np.arange(128)[:, None],
                          0.0, -1.0e30).astype(np.float32),
    }
    return d


def kernel(**inputs):
    ins1 = [_core_inputs_l1(inputs, c) for c in range(NCORES)]
    nc1 = build_launch1()
    res1 = run_bass_kernel_spmd(nc1, ins1, core_ids=list(range(NCORES)))
    r1 = res1.results
    w1p2, b1p2 = _pad_w1(inputs["L0_ffn_w1"][1], inputs["L0_ffn_b1"][1])
    m1 = {
        "w1p2": w1p2, "b1p2": _cols(b1p2),
        "w2p2": _pad_w2(inputs["L0_ffn_w2"][1]),
        "rotab": [ins1[p]["rotab"] for p in range(4)],
        "masks": [ins1[p]["masks"] for p in range(4)],
    }
    ins2 = [_core_inputs_l2(inputs, c, r1, m1) for c in range(NCORES)]
    nc2 = build_launch2()
    res2 = run_bass_kernel_spmd(nc2, ins2, core_ids=list(range(NCORES)))
    out = np.zeros((B, L, D), np.float32)
    for c in range(NCORES):
        b, p = c // 4, c % 4
        out[b, TOK * p: TOK * (p + 1)] = res2.results[c]["out"].reshape(
            D, TOK).T
    return out


# version 12
# speedup vs baseline: 1.1395x; 1.1395x over previous
import contextlib
import numpy as np
import ml_dtypes
import concourse.bass as bass
import concourse.mybir as mybir
from concourse.bass_utils import run_bass_kernel_spmd

F32 = mybir.dt.float32
BF16 = mybir.dt.bfloat16
AF = mybir.ActivationFunctionType
ALU = mybir.AluOpType
AX = mybir.AxisListType.X
bf16 = ml_dtypes.bfloat16

NCORES = 8
B, L, D = 2, 8192, 1024
TOK = 2048
HALO = 648
XT = HALO + TOK
OWN0 = HALO
NH, DS, DH, DI = 16, 64, 128, 2048
ZD = 4112
NCH = 16
HEADS, DHEAD, INNER = 8, 64, 512
WIN = 512
HID, HIDP = 2730, 2816
NQS = 17
NKB = 21
KCOL0 = HALO - 640
RMS_EPS = 1.1920928955078125e-07
CONST_VALS = (1e-5, 1.0, -1.0, RMS_EPS, 1e-12, 1e-30, 0.5)
DEV_DEBUG = False
FUSED_DEBUG = False


class Sem:
    def __init__(self, handle, dma=False):
        self.h = handle
        self.n = 0
        self.dma = dma


class Prog:
    ENGS = ("sync", "tensor", "vector", "scalar", "gpsimd")

    def __init__(self, nc):
        self.nc = nc
        self.steps = {e: [] for e in self.ENGS}
        self.last_wait = {e: {} for e in self.ENGS}

    def on(self, eng, fn):
        self.steps[eng].append(fn)

    def wait(self, eng, sem, val=None):
        if val is None or sem.dma:
            val = sem.n
        if val <= 0:
            return
        lw = self.last_wait[eng]
        if lw.get(id(sem), -1) >= val:
            return
        lw[id(sem)] = val
        h = sem.h
        self.on(eng, lambda e, h=h, v=val: e.wait_ge(h, v))

    def emit(self, eng, emitfn, waits=(), inc=None, inc_amt=1):
        for w in waits:
            s, v = w if isinstance(w, tuple) else (w, None)
            self.wait(eng, s, v)
        if inc is not None and inc.dma:
            self.wait(eng, inc, inc.n)
        if inc is not None:
            inc.n += inc_amt
            h, amt = inc.h, inc_amt
            self.on(eng, lambda e, f=emitfn, h=h, a=amt: f(e).then_inc(h, a))
            return inc.n
        self.on(eng, lambda e, f=emitfn: f(e))
        return None

    def drain(self, eng):
        self.on(eng, lambda e: e.drain())

    def run(self, block):
        for eng in self.ENGS:
            steps = self.steps[eng]
            getattr(block, eng)(lambda e, steps=steps: [f(e) for f in steps])

    def n_steps(self):
        return {e: len(self.steps[e]) for e in self.ENGS}


def _rotary_tables(t0, n):
    half = np.arange(0, DHEAD, 2, dtype=np.float64)
    inv_freq = 1.0 / 10000.0 ** (half / DHEAD)
    t = np.arange(t0, t0 + n, dtype=np.float64)[:, None]
    fr = t * inv_freq[None, :]
    fr = np.concatenate([fr, fr], axis=1)
    base = (half + 0.4 * DHEAD) / (1.4 * DHEAD)
    power = (t - L // 2) / float(WIN // 2)
    sc = np.concatenate([base[None, :] ** power] * 2, axis=1)
    cos, sin = np.cos(fr), np.sin(fr)
    sign = np.where(np.arange(DHEAD) < 32, -1.0, 1.0)[None, :]
    return np.stack([
        (cos * sc * 8.0), (sin * sc * sign * 8.0),
        (cos / sc), (sin / sc * sign)]).astype(np.float32)


def _attn_mask_tiles(p):
    out = np.full((6, 128, 640), -1.0e30, np.float32)
    koffs = np.array((0, -384, -256, -128, -512))
    k = np.arange(128)
    qr = np.arange(128)
    for ti in range(6):
        i = 16 if ti == 0 else ti - 1
        q0 = 2048 * p - 128 + 128 * i
        tk = q0 + koffs[None, :, None] + k[:, None, None]
        tq = q0 + qr[None, None, :]
        kept = (tk >= 0) & (tk <= tq) & (tq <= tk + 512)
        out[ti] = np.where(kept, 0.0, -1.0e30).transpose(
            0, 1, 2).reshape(128, 640)
    return out


def _pad_w1(w1, b1):
    w1p = np.zeros((D, 2 * HIDP), np.float32)
    b1p = np.zeros((2 * HIDP,), np.float32)
    for j in range(HIDP // 128):
        lo, hi = 128 * j, min(128 * (j + 1), HID)
        n = hi - lo
        if n > 0:
            w1p[:, 256 * j: 256 * j + n] = w1[:, lo:hi]
            b1p[256 * j: 256 * j + n] = b1[lo:hi]
            w1p[:, 256 * j + 128: 256 * j + 128 + n] = w1[:, HID + lo: HID + hi]
            b1p[256 * j + 128: 256 * j + 128 + n] = b1[HID + lo: HID + hi]
    return w1p, b1p


def _pad_w2(w2):
    w2p = np.zeros((HIDP, D), np.float32)
    w2p[:HID] = w2
    return w2p


def _cols(vec, pdim=128):
    return np.ascontiguousarray(np.asarray(vec, np.float32).reshape(-1, pdim).T)


def _tslice(ap_dram, c0, w):
    return ap_dram.rearrange("(a p) t -> p a t", p=128)[:, :, c0:c0 + w]


POOL_COLS = 105472


def pv(pool, byte_off, prt, cols, dt):
    assert byte_off % 4 == 0
    nb = cols * (4 if dt == F32 else 2)
    assert byte_off + nb <= POOL_COLS * 2, (byte_off, nb)
    sl = pool[0:prt, byte_off // 2: (byte_off + nb) // 2]
    if dt == BF16:
        return sl
    return sl.bitcast(F32)


class Layout:
    def __init__(self, pool):
        self.pool = pool
        self.cur = 0
        self.map = {}

    def add(self, name, prt, cols, dt, at=None):
        nb = cols * (4 if dt == F32 else 2)
        off = self.cur if at is None else at
        off = (off + 255) // 256 * 256
        if at is None:
            self.cur = off + nb
        self.map[name] = (off, prt, cols, dt)
        assert off + nb <= POOL_COLS * 2, (name, off, nb)
        return pv(self.pool, off, prt, cols, dt)

    def mark(self):
        return self.cur

    def reset(self, pos):
        self.cur = pos


def chv(buf, stride, j, c0, w):
    return buf[:, j * stride + c0: j * stride + c0 + w]


def emit_rows_ln(prog, eng_rows, PS_row, srow, srow2, n, eps, sPE, sDV, sSC,
                 wait_pe):
    prog.wait("vector", sPE, wait_pe)
    prog.emit("vector", lambda v: v.tensor_scalar_mul(
        srow[0:1, :n], PS_row[0:1, :n], 1.0 / D))
    prog.drain("vector")
    prog.emit("vector", lambda v: v.tensor_mul(
        srow2[0:1, :n], srow[0:1, :n], srow[0:1, :n]))
    prog.drain("vector")
    dv = prog.emit("vector", lambda v: v.scalar_tensor_tensor(
        srow2[0:1, :n], PS_row[64:65, :n], 1.0 / D, srow2[0:1, :n],
        ALU.mult, ALU.subtract), inc=sDV)
    sc = prog.emit("scalar", lambda s: s.activation(
        srow2[0:1, :n], srow2[0:1, :n], AF.Sqrt, bias=eps),
        waits=[(sDV, dv)], inc=sSC)
    prog.wait("vector", sSC, sc)
    prog.emit("vector", lambda v: v.reciprocal(
        srow2[0:1, :n], srow2[0:1, :n]))
    prog.drain("vector")
    dv = prog.emit("vector", lambda v: v.scalar_tensor_tensor(
        srow[0:1, :n], srow[0:1, :n], -1.0, srow2[0:1, :n],
        ALU.mult, ALU.mult), inc=sDV)
    return dv





def emit_accum(prog, sGP, out_ap, in_ap, waits, sOut=None):
    prog.emit("gpsimd", lambda g, o=out_ap, i=in_ap: g.dma_start(
        out=o, in_=i, accum_op=ALU.add), waits=waits,
        inc=sOut, inc_amt=16)
    prog.drain("gpsimd")
    return prog.emit("gpsimd", lambda g: g.nop(), inc=sGP)

def emit_attn(P):
    prog, nc = P["prog"], P["nc"]
    sPE, sDV, sSC, sGP, sW, sOut = (P[k] for k in
                                    ("sPE", "sDV", "sSC", "sGP", "sW", "sOut"))
    PS_A, PS_B, PS_S, PS_T, PS_U = (P[k] for k in
                                    ("PS_A", "PS_B", "PS_S", "PS_T", "PS_U"))
    src, xout = P["src"], P["xout"]
    wqkv_sb, wo_sb = P["wqkv_sb"], P["wo_sb"]
    tb, masks = P["tb"], P["masks"]
    kfm, Vb, Oring, qsl = P["kfm"], P["Vb"], P["Oring"], P["qsl"]
    kprep, qprep, rexp, rot1, sqb, nrm = (P[k] for k in
        ("kprep", "qprep", "rexp", "rot1", "sqb", "nrm"))
    csrow, cslab, out_d = P["csrow"], P["cslab"], P["out_d"]
    oneb, onef, idb = P["oneb"], P["onef"], P["idb"]
    KT = TOK + 640
    NRING = 6
    KCOL0 = HALO - 640

    fq = prog.emit("tensor", lambda t: t.matmul(
        PS_U[:, 0:64], onef[0:1, :], P["qsrow"][0:1, 0:64],
        start=True, stop=True),
        waits=[(sW, None), (P["sU"], None), (sDV, None), (sSC, None)])
    prog.emit("tensor", lambda t: t.matmul(
        PS_U[:, 64:96], onef[0:1, :], P["qsrow"][0:1, 32:64],
        start=True, stop=True))
    prog.emit("tensor", lambda t: t.matmul(
        PS_U[:, 96:128], onef[0:1, :], P["qsrow"][0:1, 0:32],
        start=True, stop=True))
    prog.emit("tensor", lambda t: t.matmul(
        PS_U[:, 128:192], onef[0:1, :], P["ksrow"][0:1, 0:64],
        start=True, stop=True))
    prog.emit("tensor", lambda t: t.matmul(
        PS_U[:, 192:224], onef[0:1, :], P["ksrow"][0:1, 32:64],
        start=True, stop=True))
    fq = prog.emit("tensor", lambda t: t.matmul(
        PS_U[:, 224:256], onef[0:1, :], P["ksrow"][0:1, 0:32],
        start=True, stop=True), inc=sPE)
    prog.wait("vector", sPE, fq)
    prog.wait("vector", sW, None)
    for (ti, pc) in ((0, 0), (1, 64), (2, 128), (3, 192)):
        prog.emit("vector", lambda v, ti=ti, pc=pc: v.tensor_tensor(
            tb[ti].rearrange("p (a b) -> p a b", a=NKB),
            tb[ti].rearrange("p (a b) -> p a b", a=NKB),
            PS_U[:, pc:pc + 64][:, None, :].broadcast_to([128, NKB, 64]),
            ALU.mult))
    prog.drain("vector")
    fold_dv = prog.emit("vector", lambda v: v.nop(), inc=sDV)
    prog.wait("vector", sW, None)
    P["_v_ev"] = P["_k_ev"] = P["_q_ev"] = 0
    P["_psA_ev"] = P["_psB_ev"] = 0
    P["_apfree"] = [0, 0]
    P["_vfree"] = [0] * 6

    sc_done = [0, 0]
    dv_done = [fold_dv, fold_dv]
    slot_free = [0] * NRING
    kq_dv = fold_dv
    wo_done_dv = fold_dv

    def prep(j, which):
        nonlocal kq_dv
        pbuf = kprep if which == "k" else qprep
        t_cs = tb[2] if which == "k" else tb[0]
        t_ss = tb[3] if which == "k" else tb[1]
        prog.drain("scalar")
        sq = prog.emit("scalar", lambda s: s.activation(
            sqb[:, 0:512], pbuf[:, 0:512], AF.Square),
            waits=[(sPE, None)], inc=sSC)
        rd = prog.emit("vector", lambda v: v.reduce_sum(
            nrm[:, 0:8][:, :, None],
            sqb.rearrange("p (a b) -> p a b", a=8), axis=AX),
            waits=[(sSC, sq)], inc=sDV)
        s2 = prog.emit("scalar", lambda s: s.activation(
            nrm[:, 0:8], nrm[:, 0:8], AF.Sqrt), waits=[(sDV, rd)], inc=sSC)
        prog.wait("vector", sSC, s2)
        prog.emit("vector", lambda v: v.tensor_scalar_max(
            nrm[:, 0:8], nrm[:, 0:8], 1e-12))
        prog.drain("vector")
        prog.emit("vector", lambda v: v.reciprocal(nrm[:, 0:8], nrm[:, 0:8]))
        prog.drain("vector")
        prog.emit("vector", lambda v: v.tensor_copy(
            rexp.rearrange("p (a b) -> p a b", a=8),
            nrm[:, 0:8][:, :, None].broadcast_to([128, 8, 64])))
        prog.drain("vector")
        prog.emit("vector", lambda v: v.tensor_mul(
            pbuf[:, 0:512], pbuf[:, 0:512], rexp[:, 0:512]))
        prog.drain("vector")
        prog.emit("vector", lambda v, j=j: v.tensor_tensor(
            rot1[:, 0:512], pbuf[:, 0:512],
            t_cs[:, 64 * j:64 * j + 64][:, None, :].broadcast_to([128, 8, 64])
            .rearrange("p a b -> p (a b)") if False else
            t_cs[:, 64 * j:64 * j + 64][:, None, :].broadcast_to([128, 8, 64]),
            ALU.mult))
        prog.drain("vector")
        shuf = bass.AP(pbuf.tensor, pbuf.offset + 32,
                       [list(pbuf.ap)[0], [64, 8], [-32, 2], [1, 32]])
        prog.emit("vector", lambda v, j=j, shuf=shuf: v.tensor_tensor(
            rexp.rearrange("p (a b c) -> p a b c", a=8, b=2), shuf,
            t_ss[:, 64 * j:64 * j + 64][:, None, :].broadcast_to([128, 8, 64])
            .rearrange("p a (b c) -> p a b c", b=2), ALU.mult))
        prog.drain("vector")
        rot_dv = prog.emit("vector", lambda v: v.tensor_tensor(
            sqb[:, 0:512], rot1[:, 0:512], rexp[:, 0:512], ALU.add), inc=sDV)
        base = 0 if which == "k" else 4
        tpl = None
        for m in range(4):
            tpl = prog.emit("tensor", lambda t, m=m, base=base: t.transpose(
                PS_T[:, 128 * (base + m): 128 * (base + m) + 128],
                sqb[:, 128 * m: 128 * m + 128], idb[:, :]),
                waits=([(sDV, rot_dv), (sDV, kq_dv)] if m == 0 else ()),
                inc=sPE if m == 3 else None)
        prog.wait("vector", sPE, tpl)
        for m in range(4):
            if which == "k":
                prog.emit("vector", lambda v, m=m, j=j: v.tensor_copy(
                    kfm[:, KT * m + 128 * j: KT * m + 128 * j + 128],
                    PS_T[:, 128 * m: 128 * m + 128]))
            else:
                prog.emit("vector", lambda v, m=m, j=j: v.tensor_copy(
                    qsl[:, 512 * (j % 2) + 128 * m:
                        512 * (j % 2) + 128 * m + 128],
                    PS_T[:, 128 * (4 + m): 128 * (4 + m) + 128]))
        prog.drain("vector")
        kq_dv = prog.emit("vector", lambda v: v.nop(), inc=sDV)

    def attn_unit(i, h):
        pr = (i * 8 + h) % 2
        sb = 1024 * pr
        hp, oh = h // 2, h % 2
        ti = 0 if i >= 5 else 1 + i
        t1 = prog.emit("vector", lambda v, ti=ti: v.tensor_tensor(
            PS_S[:, sb: sb + 640], PS_S[:, sb: sb + 640],
            masks[:, 640 * ti: 640 * ti + 640], ALU.add),
            waits=[(sPE, P["_spair_pe"][pr])], inc=sDV)
        if P.get("stage") == "mask":
            sc_done[pr] = 0
            dv_done[pr] = t1
            return t1
        ex = prog.emit("scalar", lambda s, pr=pr: s.activation(
            P["Apar"][:, 640 * pr: 640 * pr + 640], PS_S[:, sb: sb + 640],
            AF.Exp), waits=[(sDV, t1), (sPE, P["_apfree"][pr])], inc=sSC)
        if P.get("stage") == "exp":
            sc_done[pr] = ex
            dv_done[pr] = t1
            return t1
        om = None
        cm = None
        for c, jj in enumerate((i + 4, i + 1, i + 2, i + 3, i)):
            om = prog.emit("tensor", lambda t, c=c, jj=jj % 6, pr=pr, h=h:
                           t.matmul(
                PS_S[:, sb + 640: sb + 768][0:64, :],
                Vb[:, 512 * jj + 64 * h: 512 * jj + 64 * h + 64],
                P["Apar"][:, 640 * pr + 128 * c: 640 * pr + 128 * c + 128],
                start=(c == 0), stop=(c == 4)),
                waits=([(sSC, ex), (sDV, dv_done[pr])] if c == 0 else ()),
                inc=sPE if c == 4 else None)
        oev = prog.emit("scalar", lambda s, pr=pr: s.activation(
            P["osb"][0:64, 128 * pr: 128 * pr + 128],
            PS_S[0:64, sb + 640: sb + 768], AF.Identity),
            waits=[(sPE, om)], inc=sSC)
        for c in range(5):
            cm = prog.emit("tensor", lambda t, c=c, pr=pr: t.matmul(
                PS_S[:, sb + 768: sb + 896][0:1, :], oneb[:, 0:1],
                P["Apar"][:, 640 * pr + 128 * c: 640 * pr + 128 * c + 128],
                start=(c == 0), stop=(c == 4)),
                waits=([(sSC, oev)] if c == 0 else ()),
                inc=sPE if c == 4 else None)
        if P.get("stage") == "omm":
            sc_done[pr] = ex
            dv_done[pr] = t1
            P["_apfree"][pr] = cm
            return t1
        prog.wait("vector", sPE, cm)
        prog.emit("vector", lambda v: v.tensor_scalar_max(
            PS_S[0:1, sb + 768: sb + 896], PS_S[0:1, sb + 768: sb + 896],
            1e-30))
        prog.drain("vector")
        rc = prog.emit("vector", lambda v, pr=pr: v.reciprocal(
            csrow[0:1, 128 * pr: 128 * pr + 128],
            PS_S[0:1, sb + 768: sb + 896]), inc=sDV)
        if P.get("stage") == "norm1":
            sc_done[pr] = ex
            dv_done[pr] = rc
            P["_apfree"][pr] = cm
            return rc
        bc = prog.emit("tensor", lambda t, pr=pr: t.matmul(
            PS_S[:, sb + 896: sb + 1024][0:64, :], onef[0:1, 0:64],
            csrow[0:1, 128 * pr: 128 * pr + 128], start=True, stop=True),
            waits=[(sDV, rc)], inc=sPE)
        if P.get("stage") == "norm2":
            sc_done[pr] = ex
            dv_done[pr] = rc
            P["_apfree"][pr] = cm
            return rc
        slot = i % NRING
        if P.get("stage") == "norm3":
            sc_done[pr] = oev
            dv_done[pr] = rc
            P["_apfree"][pr] = cm
            return rc
        prog.wait("vector", sPE, bc)
        prog.wait("vector", sPE, slot_free[slot])
        prog.wait("vector", sSC, oev)
        mu = prog.emit("vector", lambda v, slot=slot, hp=hp, oh=oh, pr=pr:
                       v.tensor_tensor(
            Oring[64 * oh: 64 * oh + 64,
                  512 * slot + 128 * hp: 512 * slot + 128 * hp + 128],
            P["osb"][0:64, 128 * pr: 128 * pr + 128],
            PS_S[0:64, sb + 896: sb + 1024], ALU.mult), inc=sDV)
        sc_done[pr] = oev
        dv_done[pr] = mu
        P["_apfree"][pr] = bc
        return mu

    def spair(i, hp):
        nonlocal kq_dv
        pe = [None, None]
        for c, jj in enumerate((i + 4, i + 1, i + 2, i + 3, i)):
            for oh in range(2):
                h = 2 * hp + oh
                pr = (i * 8 + h) % 2
                sb = 1024 * pr
                pe[oh] = prog.emit(
                    "tensor", lambda t, c=c, jj=jj, hp=hp, oh=oh, sb=sb, i=i:
                    t.matmul(
                        PS_S[:, sb + 128 * [0, 1, 2, 3, 4][c]:
                             sb + 128 * [0, 1, 2, 3, 4][c] + 128],
                        kfm[64 * oh: 64 * oh + 64,
                            KT * hp + 128 * jj: KT * hp + 128 * jj + 128],
                        qsl[64 * oh: 64 * oh + 64,
                            512 * (i % 2): 512 * (i % 2) + 128]
                        if False else
                        qsl[64 * oh: 64 * oh + 64,
                            512 * ((i + 4) % 2) + 128 * hp:
                            512 * ((i + 4) % 2) + 128 * hp + 128],
                        start=True, stop=True,
                        tile_position=(64 * oh, 0)),
                    waits=([(sSC, sc_done[(i * 8 + 2 * hp + oh) % 2]),
                            (sDV, dv_done[(i * 8 + 2 * hp + oh) % 2]),
                            (sDV, kq_dv)]
                           if c == 0 else ()),
                    inc=sPE)
        P["_spair_pe"][(i * 8 + 2 * hp) % 2] = pe[0]
        P["_spair_pe"][(i * 8 + 2 * hp + 1) % 2] = pe[1]

    P["_spair_pe"] = [0, 0]

    wo_slab_done = []
    for j in range(NKB):
        kc0 = KCOL0 + 128 * j
        for (tgt, ps, ofs) in (("v", PS_A, 1024), ("k", PS_B, 512)) + (
                (("q", PS_A, 0),) if j >= 4 else ()):
            mm_last = None
            for kk in range(8):
                sh = 1 if kk >= 4 else 0
                wl = []
                if kk == 0 and tgt == "v":
                    wl = [(sW, None), (sSC, P["_psA_ev"])]
                if kk == 0 and tgt == "k":
                    wl = [(sSC, P["_psB_ev"])]
                if kk == 0 and tgt == "q":
                    wl = [(sSC, P["_v_ev"])]
                mm_last = prog.emit(
                    "tensor", lambda t, kk=kk, kc0=kc0, ps=ps, ofs=ofs, sh=sh:
                    t.matmul(
                        ps[:, 0:512],
                        src[:, kk * XT + kc0 - sh: kk * XT + kc0 + 128 - sh],
                        wqkv_sb[:, kk * 1536 + ofs: kk * 1536 + ofs + 512],
                        start=(kk == 0), stop=(kk == 7)),
                    waits=wl, inc=sPE if kk == 7 else None)
            if tgt == "v":
                P["_v_ev"] = prog.emit(
                    "scalar", lambda s, j=j: s.activation(
                        Vb[:, 512 * (j % 6): 512 * (j % 6) + 512],
                        PS_A[:, 0:512], AF.Identity),
                    waits=[(sPE, mm_last), (sPE, P["_vfree"][j % 6])],
                    inc=sSC)
                P["_psA_ev"] = P["_v_ev"]
            elif tgt == "k":
                P["_k_ev"] = prog.emit(
                    "scalar", lambda s: s.activation(
                        kprep[:, 0:512], PS_B[:, 0:512], AF.Identity),
                    waits=[(sPE, mm_last)], inc=sSC)
                P["_psB_ev"] = P["_k_ev"]
            else:
                P["_q_ev"] = prog.emit(
                    "scalar", lambda s: s.activation(
                        qprep[:, 0:512], PS_A[:, 0:512], AF.Identity),
                    waits=[(sPE, mm_last)], inc=sSC)
                P["_psA_ev"] = P["_q_ev"]
        if P.get("stage") == "qkv0":
            continue
        prog.wait("vector", sSC, P["_k_ev"])
        prep(j, "k")
        if j >= 4:
            prog.wait("vector", sSC, P["_q_ev"])
            prep(j, "q")
        if P.get("stage") == "qkv":
            continue
        if j >= 4:
            i = j - 4
            for hp in range(4):
                spair(i, hp)
                if P.get("stage") in ("spair",):
                    sc_done[0] = sc_done[1] = 0
                    continue
                attn_unit(i, 2 * hp)
                attn_unit(i, 2 * hp + 1)
            P["_vfree"][i % 6] = sPE.n
            if i == 0 or (i % 4 == 0 and i > 0) or i == NQS - 1:
                pass
        def emit_wo(slab_qsubs, x1c0, own_c0, width):
            nonlocal wo_done_dv
            runs = []
            for qq in slab_qsubs:
                sl = qq % NRING
                if runs and runs[-1][-1] == sl - 1:
                    runs[-1].append(sl)
                else:
                    runs.append([sl])
            ev_sc = []
            prev_ev = P.get("_wo_ev", 0)
            for m in range(8):
                pu_last = None
                ncol = 0
                for run in runs:
                    nr = len(run) * 128
                    for hp in range(4):
                        pu_last = prog.emit(
                            "tensor",
                            lambda t, m=m, hp=hp, run=run, ncol=ncol, nr=nr:
                            t.matmul(
                                PS_U[:, ncol:ncol + nr],
                                wo_sb[:, 1024 * hp + 128 * m:
                                      1024 * hp + 128 * m + 128],
                                Oring[:, 512 * run[0] + 128 * hp:
                                      512 * run[0] + 128 * hp + 128]
                                .broadcast_to([128, nr]) if False else
                                bass.AP(Oring.tensor,
                                        Oring.offset + 512 * run[0] + 128 * hp,
                                        [list(Oring.ap)[0], [512, len(run)],
                                         [1, 128]]),
                                start=(hp == 0), stop=(hp == 3)),
                            waits=([(sDV, dv_done[0]), (sDV, dv_done[1]),
                                    (sSC, prev_ev)]
                                   if (hp == 0 and ncol == 0) else ()),
                            inc=sPE if (hp == 3) else None)
                    ncol += nr
                ev = prog.emit("scalar", lambda s, m=m, width=width:
                               s.activation(
                    cslab[:, 512 * m: 512 * m + width], PS_U[:, 0:width],
                    AF.Identity),
                    waits=[(sPE, pu_last), (sOut, None),
                           (sGP, P.get("_acc_gp", 0))],
                    inc=sSC)
                ev_sc.append(ev)
                prev_ev = ev
                xsrc0 = HALO - 128 + x1c0
                x1w = TOK + 128
                xu = prog.emit("vector", lambda v, m=m, x1c0=x1c0, x1w=x1w,
                               xsrc0=xsrc0, width=width: v.tensor_tensor(
                    xout[:, m * x1w + x1c0: m * x1w + x1c0 + width],
                    src[:, m * XT + xsrc0: m * XT + xsrc0 + width],
                    cslab[:, 512 * m: 512 * m + width], ALU.add),
                    waits=[(sSC, ev)], inc=sDV)
            P["_wo_ev"] = ev_sc[-1]
            prog.drain("vector")
            wo_done_dv = prog.emit("vector", lambda v: v.nop(), inc=sDV)
            for qq in slab_qsubs:
                slot_free[qq % NRING] = sPE.n
            if own_c0 >= 0:
                P["_acc_gp"] = emit_accum(
                    prog, sGP, _tslice(out_d, own_c0, width),
                    cslab.rearrange("p (a b) -> p a b", a=8)[:, :, :width],
                    [(sSC, ev_sc[-1]), (sDV, wo_done_dv),
                     (sGP, P.get("_acc_gp", 0))], sOut=sOut)
        if P.get("stage") == "attn":
            continue
        if j == 4:
            emit_wo([0], 0, -1, 128)
        elif j >= 8 and (j - 4) % 4 == 0:
            s4 = (j - 8) // 4
            emit_wo([4 * s4 + 1 + t for t in range(4)], 128 + 512 * s4,
                    512 * s4, 512)
    return wo_done_dv



def emit_mlp(P):
    prog, nc = P["prog"], P["nc"]
    sPE, sDV, sSC, sGP, sW, sSp, sOut = (P[k] for k in
        ("sPE", "sDV", "sSC", "sGP", "sW", "sSp", "sOut"))
    sU = P["sU"]
    PS_A, PS_B, PS_S, PS_T, PS_U = (P[k] for k in
        ("PS_A", "PS_B", "PS_S", "PS_T", "PS_U"))
    xio, w1p, w2p = P["xio"], P["w1p"], P["w2p"]
    b1c, b2c, lngc, lnbc = P["b1c"], P["b2c"], P["lngc"], P["lnbc"]
    hln, sq2, geluS, mrow, mrow2 = (P[k] for k in
        ("hln", "sq2", "geluS", "mrow", "mrow2"))
    act_d, out_d, cslab = P["act_d"], P["out_d"], P["cslab"]
    oneb, onef = P["oneb"], P["onef"]
    x1w = TOK + 128
    NSL = TOK // 512
    xs = lambda kk, c0, w: xio[:, kk * x1w + c0 - (1 if kk >= 4 else 0):
                               kk * x1w + c0 + w - (1 if kk >= 4 else 0)]
    act_sp = []
    hln_pe_last = 0
    for s4 in range(NSL):
        c0 = 128 + 512 * s4
        sq = prog.emit("scalar", lambda s, c0=c0: [s.activation(
            sq2[:, kk * 512: kk * 512 + 512], xs(kk, c0, 512), AF.Square)
            for kk in range(8)][-1], waits=[(sDV, None), (sPE, None)],
            inc=sSC)
        mm1 = None
        for kk in range(8):
            mm1 = prog.emit("tensor", lambda t, kk=kk, c0=c0: t.matmul(
                PS_U[0:1, 0:512], oneb[:, 0:1], xs(kk, c0, 512),
                start=(kk == 0), stop=(kk == 7)),
                waits=([(sDV, None), (sW, None)] if kk == 0 else ()),
                inc=sPE if kk == 7 else None)
        mm2 = None
        for kk in range(8):
            mm2 = prog.emit("tensor", lambda t, kk=kk: t.matmul(
                PS_U[64:65, 0:512], oneb[:, 0:1],
                sq2[:, kk * 512: kk * 512 + 512],
                start=(kk == 0), stop=(kk == 7)),
                waits=([(sSC, sq)] if kk == 0 else ()),
                inc=sPE if kk == 7 else None)
        dvr = emit_rows_ln(prog, "vector", PS_U, mrow, mrow2, 512, 1e-5,
                           sPE, sDV, sSC, mm2)
        if DEV_DEBUG and s4 == 0:
            dbg_r = P["nc"].dram_tensor(
                "dbg_rows" + P.get("tag", ""), [2, 512], F32,
                kind="ExternalOutput").ap()
            prog.emit("sync", lambda s: s.dma_start(
                out=dbg_r[0:1, :], in_=mrow[0:1, 0:512]),
                waits=[(sDV, dvr)], inc=sSp, inc_amt=16)
            prog.emit("sync", lambda s: s.dma_start(
                out=dbg_r[1:2, :], in_=mrow2[0:1, 0:512]),
                inc=sSp, inc_amt=16)
        bc1 = prog.emit("tensor", lambda t: t.matmul(
            PS_S[:, 0:512], onef[0:1, :], mrow2[0:1, 0:512],
            start=True, stop=True), waits=[(sDV, dvr)])
        bc2 = prog.emit("tensor", lambda t: t.matmul(
            PS_S[:, 512:1024], onef[0:1, :], mrow[0:1, 0:512],
            start=True, stop=True), inc=sPE)
        prog.wait("vector", sPE, bc2)
        prog.wait("vector", sU)
        hl_dv = None
        for kk in range(8):
            prog.emit("vector", lambda v, kk=kk, c0=c0: v.tensor_tensor(
                geluS[:, 0:512], xs(kk, c0, 512), PS_S[:, 0:512], ALU.mult))
            prog.drain("vector")
            prog.emit("vector", lambda v: v.tensor_add(
                geluS[:, 0:512], geluS[:, 0:512], PS_S[:, 512:1024]))
            prog.drain("vector")
            hl_dv = prog.emit("vector", lambda v, kk=kk: v.tensor_scalar(
                hln[:, kk * 512: kk * 512 + 512], geluS[:, 0:512],
                lngc[:, kk: kk + 1], lnbc[:, kk: kk + 1],
                ALU.mult, ALU.add), inc=sDV)
            prog.drain("vector")
        a_rd, g_rd = 0, 0
        for jm in range(22):
            am = None
            for kk in range(8):
                am = prog.emit("tensor", lambda t, kk=kk, jm=jm: t.matmul(
                    PS_A[:, 0:512],
                    w1p[:, kk * 5632 + 256 * jm: kk * 5632 + 256 * jm + 128],
                    hln[:, kk * 512: kk * 512 + 512],
                    start=(kk == 0), stop=(kk == 7)),
                    waits=([(sDV, hl_dv), (sDV, a_rd), (sW, None),
                            (P["sU"], None)] if kk == 0 else ()),
                    inc=sPE if kk == 7 else None)
            gm = None
            for kk in range(8):
                gm = prog.emit("tensor", lambda t, kk=kk, jm=jm: t.matmul(
                    PS_B[:, 0:512],
                    w1p[:, kk * 5632 + 256 * jm + 128:
                        kk * 5632 + 256 * jm + 256],
                    hln[:, kk * 512: kk * 512 + 512],
                    start=(kk == 0), stop=(kk == 7)),
                    waits=([(sSC, g_rd)] if kk == 0 else ()),
                    inc=sPE if kk == 7 else None)
            g_rd = prog.emit("scalar", lambda s, jm=jm: s.activation(
                geluS[:, 0:512], PS_B[:, 0:512], AF.Gelu,
                bias=b1c[:, 2 * jm + 1: 2 * jm + 2]),
                waits=[(sPE, gm), (sDV, a_rd)], inc=sSC)
            a_rd = prog.emit("vector", lambda v, jm=jm: v.scalar_tensor_tensor(
                P["act"][:, 512 * jm: 512 * jm + 512], PS_A[:, 0:512],
                b1c[:, 2 * jm: 2 * jm + 1], geluS[:, 0:512],
                ALU.add, ALU.mult),
                waits=[(sSC, g_rd), (sPE, am), (sSp, None)], inc=sDV)
        act_sp.append(prog.emit("sync", lambda s, s4=s4: s.dma_start(
            out=act_d[s4], in_=P["act"][:, :]),
            waits=[(sDV, a_rd)], inc=sSp, inc_amt=16))
        if DEV_DEBUG and s4 == 0:
            dbg_h = P["nc"].dram_tensor(
                "dbg_hln" + P.get("tag", ""), [128, 8 * 512], BF16,
                kind="ExternalOutput").ap()
            dbg_a = P["nc"].dram_tensor(
                "dbg_act" + P.get("tag", ""), [128, 22 * 512], BF16,
                kind="ExternalOutput").ap()
            prog.emit("sync", lambda s: s.dma_start(out=dbg_h, in_=hln[:, :]),
                      inc=sSp, inc_amt=16)
            prog.emit("sync", lambda s: s.dma_start(
                out=dbg_a, in_=P["act"][:, :]), inc=sSp, inc_amt=16)
        hln_pe_last = sPE.n
    w2_ld = None
    for kk in range(22):
        w2_ld = prog.emit("gpsimd", lambda g, kk=kk: g.dma_start(
            out=w2p[:, kk * 1024:(kk + 1) * 1024],
            in_=P["w2_w"][128 * kk:128 * (kk + 1), :]),
            waits=([(sPE, hln_pe_last), (sSp, None)] if kk == 0 else ()),
            inc=sW, inc_amt=16)
    for s4 in range(NSL):
        c0 = 128 + 512 * s4
        ar = prog.emit("sync", lambda s, s4=s4: s.dma_start(
            out=P["act"][:, :], in_=act_d[s4]),
            waits=[(sDV, None), (sW, None), (sSC, None)],
            inc=sSp, inc_amt=16)
        ev_sc = None
        xu_dv = None
        for m in range(8):
            pb = PS_A if m % 2 == 0 else PS_B
            wm = None
            for kk in range(22):
                wm = prog.emit("tensor", lambda t, kk=kk, m=m, pb=pb:
                               t.matmul(
                    pb[:, 0:512],
                    w2p[:, kk * 1024 + 128 * m: kk * 1024 + 128 * m + 128],
                    P["act"][:, 512 * kk: 512 * kk + 512],
                    start=(kk == 0), stop=(kk == 21)),
                    waits=([(sSp, None), (sW, None), (sSC, None)]
                           if kk == 0 else ()),
                    inc=sPE if kk == 21 else None)
            ev_sc = prog.emit("scalar", lambda s, m=m, pb=pb: s.activation(
                cslab[:, 512 * m: 512 * m + 512], pb[:, 0:512],
                AF.Identity, bias=b2c[:, m: m + 1]),
                waits=[(sPE, wm), (sOut, None),
                       (sGP, P.get("_acc_gp", 0))], inc=sSC)
            if not P.get("skip_x"):
                xu_dv = prog.emit("vector", lambda v, m=m, c0=c0:
                                  v.tensor_tensor(
                    xio[:, m * x1w + c0: m * x1w + c0 + 512],
                    xio[:, m * x1w + c0: m * x1w + c0 + 512],
                    cslab[:, 512 * m: 512 * m + 512], ALU.add),
                    waits=[(sSC, ev_sc)], inc=sDV)
        P["_acc_gp"] = emit_accum(
            prog, sGP, _tslice(out_d, 512 * s4, 512),
            cslab.rearrange("p (a b) -> p a b", a=8),
            [(sSC, ev_sc), (sGP, P.get("_acc_gp", 0))], sOut=sOut)
        if DEV_DEBUG and P.get("tag"):
            dbg_ml = P["nc"].dram_tensor(
                f"dbg_mlp{s4}", [128, 8 * 512], F32,
                kind="ExternalOutput").ap()
            prog.emit("sync", lambda s, dbg_ml=dbg_ml: s.dma_start(
                out=dbg_ml, in_=cslab[:, :]),
                waits=[(sSC, ev_sc)], inc=sSp, inc_amt=16)
            prog.wait("scalar", sSp)
    prog.drain("vector")
    return prog.emit("vector", lambda v: v.nop(), inc=sDV)


def build_launch1(upto="all"):
    nc = bass.Bass("TRN2", target_bir_lowering=False)
    dt_in = lambda n, s, d=F32: nc.dram_tensor(n, s, d, kind="ExternalInput").ap()
    dt_out = lambda n, s, d=F32: nc.dram_tensor(n, s, d, kind="ExternalOutput").ap()
    dt_int = lambda n, s, d=F32: nc.dram_tensor(n, s, d).ap()

    u_t = dt_in("u_t", [D, XT])
    W_in = dt_in("W_in", [D, ZD])
    conv_w = dt_in("conv_w", [128, 16 * 4])
    conv_b = dt_in("conv_b", [128, 16])
    delta_p = dt_in("delta_p", [16, 1])
    wqkv_w = dt_in("wqkv", [D, 3 * INNER])
    wo_w = dt_in("wo_w", [INNER, D])
    w1_w = dt_in("w1_w", [D, 2 * HIDP])
    b1_w = dt_in("b1_w", [128, 2 * HIDP // 128])
    w2_w = dt_in("w2_w", [HIDP, D])
    b2_w = dt_in("b2_w", [128, D // 128])
    lng_w = dt_in("lng", [128, 8])
    lnb_w = dt_in("lnb", [128, 8])
    qs_in = dt_in("qs", [1, DHEAD])
    ks_in = dt_in("ks", [1, DHEAD])
    rotab = dt_in("rotab", [4, NKB * 128, DHEAD])
    masks_in = dt_in("masks", [6, 128, 640])
    ident_f = dt_in("ident_f", [128, 128])
    out_d = dt_out("out_part", [D, TOK])
    xn_d = dt_out("xn_sp", [D, XT], BF16)
    x1_d = dt_out("x1_sp", [D, TOK], BF16)
    lam_d = dt_out("lam_sp", [16, NCH * 128 + 1])
    G_d = dt_out("G_sp", [DS, NH * DH])
    Xtm_d = dt_out("Xtm_sp", [NCH, 128, NH * DH], BF16)
    Btm_d = dt_out("Btm_sp", [NCH, 128, NH * DS], BF16)
    Bfm_d = dt_out("Bfm_sp", [8, 128, TOK], BF16)
    Cfm_d = dt_out("Cfm_sp", [8, 128, TOK], BF16)
    act_d = dt_int("act_sc", [4, 128, 22 * 512], BF16)

    ctx = contextlib.ExitStack()
    pool = ctx.enter_context(nc.sbuf_tensor("pool", [128, POOL_COLS], BF16))
    ps = lambda n, s, d: ctx.enter_context(nc.psum_tensor(n, s, d))
    PS_A = ps("PS_A", [128, 512], F32)
    PS_B = ps("PS_B", [128, 512], F32)
    PS_S = ps("PS_S", [128, 2048], F32)
    PS_T = ps("PS_T", [128, 1024], BF16)
    PS_U = ps("PS_U", [128, 512], F32)
    mksem = lambda n, dma=False: Sem(ctx.enter_context(nc.semaphore(n)), dma)
    sW = mksem("sW", True); sU = mksem("sU", True); sSp = mksem("sSp", True)
    sPE = mksem("sPE"); sDV = mksem("sDV"); sSC = mksem("sSC")
    sGP = mksem("sGP"); sOut = mksem("sOut", True)
    prog = Prog(nc)

    HDR = 2048
    XN_AT = 161024 + HDR
    X1_AT = 70144 + HDR
    RB_AT = 104960 + HDR
    assert XN_AT + 8 * XT * 2 <= POOL_COLS * 2, (XN_AT, XT)

    LM = Layout(pool)
    idf = LM.add("idf", 128, 128, F32)
    idb = LM.add("idb", 128, 128, BF16)
    oneb = LM.add("oneb", 128, 128, BF16)
    onef = LM.add("onef", 128, 128, F32)
    cap_c = LM.add("cap_c", 128, 8, F32)
    assert LM.cur <= HDR, LM.cur
    LM.cur = HDR

    Wsb = LM.add("Wsb", 128, 8 * ZD, BF16)
    u_sl = LM.add("u_sl", 128, 8 * 512, F32)
    ubf = LM.add("ubf", 128, 8 * 512, BF16)
    xr = LM.add("xr", 128, 16 * 520, BF16)
    sil = LM.add("sil", 128, 16 * 128, BF16)
    bst = LM.add("bst", 128, 8 * 512, BF16)
    cst = LM.add("cst", 128, 2 * 512, BF16)
    bds = LM.add("bds", 128, 16 * 64, BF16)
    xts = LM.add("xts", 128, 16 * 128, BF16)
    dtb = LM.add("dtb", 16, 2048, F32)
    lam = LM.add("lam", 16, 2056, F32)
    ebd = LM.add("ebd", 128, NCH * 16, F32)
    ela = LM.add("ela", 64, NCH * 16, F32)
    Gb = LM.add("Gb", 64, NH * DH, F32)
    cwc = LM.add("cwc", 128, 64, F32)
    cbc = LM.add("cbc", 128, 16, F32)
    dpc = LM.add("dpc", 16, 1, F32)
    srow = LM.add("srow", 1, 512, F32)
    srow2 = LM.add("srow2", 1, 512, F32)
    prebuf = LM.add("prebuf", 16, 128, F32)
    cacc = LM.add("cacc", 128, 128, F32)
    row1 = LM.add("row1", 1, 16, F32)
    assert LM.cur <= XN_AT, LM.cur
    xn = pv(pool, XN_AT, 128, 8 * XT, BF16)
    osl = pv(pool, LM.map["xr"][0], 128, 8 * 512, F32)
    dts = pv(pool, LM.map["u_sl"][0], 16, 2048, F32)

    for k in range(8):
        prog.emit("gpsimd", lambda g, k=k: g.dma_start(
            out=Wsb[:, k * ZD:(k + 1) * ZD],
            in_=W_in[128 * k:128 * (k + 1), :]), inc=sW, inc_amt=16)
    for (dst, src) in ((cwc, conv_w), (cbc, conv_b), (dpc, delta_p),
                      (idf, ident_f)):
        prog.emit("sync", lambda s, d=dst, r=src: s.dma_start(
            out=d[:, :], in_=r), inc=sU, inc_amt=16)
    prog.emit("gpsimd", lambda g: g.dma_start(out=idb[:, :], in_=ident_f),
              inc=sW, inc_amt=16)
    w_all = sW.n
    c_all = sU.n
    prog.emit("vector", lambda v: v.memset(oneb[:, :], 1.0))
    prog.emit("vector", lambda v: v.memset(onef[:, :], 1.0))
    prog.emit("vector", lambda v: v.memset(lam[:, 0:1], 0.0))
    for i, vv in enumerate(CONST_VALS):
        nc.const_aps.register(F32, vv, cap_c[:, i:i + 1])
        prog.emit("vector", lambda v, i=i, vv=vv: v.memset(
            cap_c[:, i:i + 1], vv))
    prog.drain("vector")
    ones_dv = prog.emit("vector", lambda v: v.nop(), inc=sDV)
    for e in ("scalar", "tensor", "gpsimd", "sync"):
        prog.wait(e, sDV, ones_dv)

    slabs = [(c, min(512, XT - c)) for c in range(0, XT, 512)]
    prev_done_dv = ones_dv
    prev_out = 0
    prev_mm2 = 0
    prev_rows_dv = 0
    for (c0, w) in slabs:
        ud = prog.emit("sync", lambda s, c0=c0, w=w: s.dma_start(
            out=u_sl.rearrange("p (a b) -> p a b", a=8)[:, :, :w],
            in_=_tslice(u_t, c0, w)),
            waits=[(sDV, prev_done_dv), (sOut, prev_out)],
            inc=sU, inc_amt=16)
        vb = prog.emit("vector", lambda v, w=w: v.tensor_copy(
            ubf.rearrange("p (a b) -> p a b", a=8)[:, :, :w],
            u_sl.rearrange("p (a b) -> p a b", a=8)[:, :, :w]),
            waits=[(sU, ud), (sPE, prev_mm2)], inc=sDV)
        mm1 = None
        for kk in range(8):
            mm1 = prog.emit("tensor", lambda t, kk=kk, w=w: t.matmul(
                PS_U[0:1, :w], oneb[:, 0:1], ubf[:, kk * 512: kk * 512 + w],
                start=(kk == 0), stop=(kk == 7)),
                waits=[(sDV, vb), (sDV, prev_rows_dv)] if kk == 0 else (),
                inc=sPE if kk == 7 else None)
        sq = prog.emit("scalar", lambda s, w=w: s.activation(
            ubf.rearrange("p (a b) -> p a b", a=8)[:, :, :w],
            ubf.rearrange("p (a b) -> p a b", a=8)[:, :, :w], AF.Square),
            waits=[(sPE, mm1)], inc=sSC)
        mm2 = None
        for kk in range(8):
            mm2 = prog.emit("tensor", lambda t, kk=kk, w=w: t.matmul(
                PS_U[64:65, :w], oneb[:, 0:1], ubf[:, kk * 512: kk * 512 + w],
                start=(kk == 0), stop=(kk == 7)),
                waits=[(sSC, sq)] if kk == 0 else (),
                inc=sPE if kk == 7 else None)
        dvr = emit_rows_ln(prog, "vector", PS_U, srow, srow2, w, 1e-5,
                           sPE, sDV, sSC, mm2)
        own0 = max(c0, OWN0)
        w2 = c0 + w - own0
        prog.emit("tensor", lambda t, w=w: t.matmul(
            PS_S[:, 0:w], onef[0:1, :], srow2[0:1, :w], start=True, stop=True),
            waits=[(sDV, dvr)])
        bc2 = prog.emit("tensor", lambda t, w=w: t.matmul(
            PS_S[:, 512:512 + w], onef[0:1, :], srow[0:1, :w],
            start=True, stop=True), inc=sPE)
        prog.wait("vector", sPE, bc2)
        for kk in range(8):
            prog.emit("vector", lambda v, kk=kk, w=w: v.tensor_mul(
                u_sl[:, kk * 512: kk * 512 + w],
                u_sl[:, kk * 512: kk * 512 + w], PS_S[:, 0:w]))
        prog.drain("vector")
        for kk in range(8):
            prog.emit("vector", lambda v, kk=kk, w=w: v.tensor_add(
                u_sl[:, kk * 512: kk * 512 + w],
                u_sl[:, kk * 512: kk * 512 + w], PS_S[:, 512:512 + w]))
        prog.drain("vector")
        xn_dv = prog.emit("vector", lambda v, c0=c0, w=w: v.tensor_copy(
            xn.rearrange("p (a b) -> p a b", a=8)[:, :, c0:c0 + w],
            u_sl.rearrange("p (a b) -> p a b", a=8)[:, :, :w]), inc=sDV)
        prev_done_dv = xn_dv
        if w2 > 0:
            o_ = own0 - c0
            ud2 = prog.emit("sync", lambda s, o=own0, w2=w2: s.dma_start(
                out=osl.rearrange("p (a b) -> p a b", a=8)[:, :, :w2],
                in_=_tslice(u_t, o, w2)), inc=sU, inc_amt=16)
            prog.wait("vector", sU, ud2)
            for kk in range(8):
                prog.emit("vector", lambda v, kk=kk, o_=o_, w2=w2:
                          v.tensor_tensor(
                              osl[:, kk * 512: kk * 512 + w2],
                              osl[:, kk * 512: kk * 512 + w2],
                              u_sl[:, kk * 512 + o_: kk * 512 + o_ + w2],
                              ALU.add))
            prog.drain("vector")
            odv = prog.emit("vector", lambda v: v.nop(), inc=sDV)
            prog.emit("gpsimd", lambda g, o=own0 - OWN0, w2=w2: g.dma_start(
                out=_tslice(out_d, o, w2),
                in_=osl.rearrange("p (a b) -> p a b", a=8)[:, :, :w2]),
                waits=[(sDV, odv)], inc=sOut, inc_amt=16)
            prev_done_dv = odv
        prev_out = sOut.n
        prev_mm2 = mm2
        prev_rows_dv = dvr
    ln_done_dv = sDV.n
    ln_done_out = sOut.n

    NSL = TOK // 512
    prev_dt_sc = 0
    for s4 in range(NSL):
        c0 = OWN0 + 512 * s4
        dt_pe = None
        for kk in range(8):
            dt_pe = prog.emit("tensor", lambda t, kk=kk, c0=c0: t.matmul(
                PS_U[0:16, 0:512], Wsb[:, kk * ZD + 4096: kk * ZD + 4112],
                xn[:, kk * XT + c0: kk * XT + c0 + 512],
                start=(kk == 0), stop=(kk == 7)),
                waits=([(sW, w_all), (sDV, ln_done_dv), (sSC, prev_dt_sc)]
                       if kk == 0 else ()), inc=sPE if kk == 7 else None)
        prev_dt_sc = prog.emit("scalar", lambda s, s4=s4: s.activation(
            dtb[:, 512 * s4: 512 * (s4 + 1)], PS_U[0:16, 0:512], AF.Identity),
            waits=[(sPE, dt_pe)], inc=sSC)
    prog.wait("scalar", sU, c_all)
    prog.wait("scalar", sOut)
    prog.drain("scalar")
    prog.emit("scalar", lambda s: s.activation(
        dts[:, 0:TOK], dtb[:, 0:TOK], AF.Exp, bias=dpc[:, 0:1]))
    prog.drain("scalar")
    prog.emit("scalar", lambda s: s.activation(
        dts[:, 0:TOK], dts[:, 0:TOK], AF.Ln, bias=1.0))
    prog.drain("scalar")
    sp_sc = prog.emit("scalar", lambda s: s.activation(
        dtb[:, 0:TOK], dtb[:, 0:TOK], AF.Abs), inc=sSC)
    prog.wait("vector", sSC, sp_sc)
    prog.emit("vector", lambda v: v.scalar_tensor_tensor(
        dtb[:, 0:TOK], dts[:, 0:TOK], -1.0, dtb[:, 0:TOK],
        ALU.mult, ALU.mult))
    prog.drain("vector")
    lam_dv = prog.emit("vector", lambda v: v.tensor_tensor_scan(
        lam[:, 1:TOK + 1], dtb[:, 0:TOK], dtb[:, 0:TOK], lam[:, 0:1],
        ALU.add, ALU.min), inc=sDV)
    prog.emit("sync", lambda s: s.dma_start(
        out=lam_d, in_=lam[:, 0:TOK + 1]), waits=[(sDV, lam_dv)],
        inc=sSp, inc_amt=16)

    pre_sc = 0
    pre_dv = 0
    for c in range(NCH):
        e0, s0 = 128 * c + 128, 128 * c
        pd = prog.emit("vector", lambda v, c=c, e0=e0: v.tensor_scalar(
            prebuf[:, :], lam[:, 128 * c + 1: 128 * c + 129],
            lam[:, e0:e0 + 1], None, ALU.subtract),
            waits=[(sDV, lam_dv), (sSC, pre_sc)], inc=sDV)
        tp = prog.emit("tensor", lambda t: t.transpose(
            PS_S[:, 512:528], prebuf[:, :], idf[0:16, 0:16]),
            waits=[(sDV, pd), (sU, c_all), (sSC, pre_sc)], inc=sPE)
        ebd_sc = prog.emit("scalar", lambda s, c=c: s.activation(
            ebd[:, 16 * c: 16 * c + 16], PS_S[:, 512:528], AF.Exp,
            scale=-1.0), waits=[(sPE, tp)], inc=sSC)
        dcol = prog.emit("vector", lambda v, e0=e0, s0=s0: v.tensor_tensor(
            prebuf[:, 0:1], lam[:, e0:e0 + 1], lam[:, s0:s0 + 1],
            ALU.subtract), waits=[(sPE, tp)], inc=sDV)
        ec = prog.emit("scalar", lambda s: s.activation(
            prebuf[:, 1:2], prebuf[:, 0:1], AF.Exp),
            waits=[(sDV, dcol)], inc=sSC)
        tp2 = prog.emit("tensor", lambda t: t.transpose(
            PS_U[0:1, 0:16], prebuf[:, 1:2], idf[0:16, 0:16]),
            waits=[(sSC, ec), (sSC, pre_sc)], inc=sPE)
        rc = prog.emit("vector", lambda v: v.tensor_copy(
            row1[0:1, 0:16], PS_U[0:1, 0:16]), waits=[(sPE, tp2)], inc=sDV)
        bc = prog.emit("tensor", lambda t: t.matmul(
            PS_U[0:64, 32:48], onef[0:1, 0:64], row1[0:1, 0:16],
            start=True, stop=True), waits=[(sDV, rc)], inc=sPE)
        pre_sc = prog.emit("scalar", lambda s, c=c: s.activation(
            ela[:, 16 * c: 16 * c + 16], PS_U[0:64, 32:48], AF.Identity),
            waits=[(sPE, bc)], inc=sSC)
    epre_sc = pre_sc

    prog.emit("vector", lambda v: v.memset(Gb[:, :], 0.0))
    prog.drain("vector")
    g_dv = prog.emit("vector", lambda v: v.nop(), inc=sDV)
    conv_gp = 0
    sil_pe = 0
    xts_sp = 0
    bds_sp = 0
    bst_pe = 0
    xt_dv = g_dv
    bd_dv = g_dv
    ctr_dv = g_dv
    sil_sc = 0
    for s4 in range(NSL):
        c0 = OWN0 + 512 * s4
        pfx_dv = 0
        if s4 == 0:
            for mj in range(16):
                hp = None
                for kk in range(8):
                    hp = prog.emit("tensor", lambda t, mj=mj, kk=kk: t.matmul(
                        PS_A[0:128, 0:8],
                        Wsb[:, kk * ZD + 128 * mj: kk * ZD + 128 * mj + 128],
                        xn[:, kk * XT + OWN0 - 8: kk * XT + OWN0],
                        start=(kk == 0), stop=(kk == 7)),
                        waits=([(sW, w_all), (sSC, epre_sc)]
                               if (mj == 0 and kk == 0) else ()),
                        inc=sPE if kk == 7 else None)
                hsc = prog.emit("scalar", lambda s, mj=mj: s.activation(
                    xr[:, 520 * mj: 520 * mj + 8], PS_A[0:128, 0:8],
                    AF.Identity), waits=[(sPE, hp)], inc=sSC)
                prog.wait("tensor", sSC, hsc)
        else:
            prog.wait("vector", sGP, conv_gp)
            prog.wait("vector", sSC, None)
            prog.emit("vector", lambda v: v.tensor_copy(
                xr.rearrange("p (a b) -> p a b", a=16)[:, :, 0:8],
                xr.rearrange("p (a b) -> p a b", a=16)[:, :, 512:520]))
            prog.drain("vector")
            pfx_dv = prog.emit("vector", lambda v: v.nop(), inc=sDV)
        for mj in range(16):
            pb = PS_A if mj % 2 == 0 else PS_B
            for kk in range(8):
                prog.emit("tensor", lambda t, mj=mj, kk=kk, c0=c0, pb=pb:
                          t.matmul(
                              pb[:, 0:512],
                              Wsb[:, kk * ZD + 128 * mj:
                                  kk * ZD + 128 * mj + 128],
                              xn[:, kk * XT + c0: kk * XT + c0 + 512],
                              start=(kk == 0), stop=(kk == 7)),
                          waits=([(sSC, None), (sGP, conv_gp),
                                  (sDV, pfx_dv)]
                                 if kk == 0 else ()),
                          inc=sPE if kk == 7 else None)
            zp = sPE.n
            prog.emit("scalar", lambda s, mj=mj, pb=pb: s.activation(
                xr[:, 520 * mj + 8: 520 * mj + 520], pb[:, 0:512],
                AF.Identity), waits=[(sPE, zp)], inc=sSC)
        zx_sc = sSC.n
        for mj in range(8):
            pb = PS_A if mj % 2 == 0 else PS_B
            for kk in range(8):
                prog.emit("tensor", lambda t, mj=mj, kk=kk, c0=c0, pb=pb:
                          t.matmul(
                              pb[:, 0:512],
                              Wsb[:, kk * ZD + 2048 + 128 * mj:
                                  kk * ZD + 2048 + 128 * mj + 128],
                              xn[:, kk * XT + c0: kk * XT + c0 + 512],
                              start=(kk == 0), stop=(kk == 7)),
                          waits=([(sSC, None), (sSp, bds_sp), (sPE, bst_pe)]
                                 if kk == 0 else ()),
                          inc=sPE if kk == 7 else None)
            zp = sPE.n
            bev = prog.emit("scalar", lambda s, mj=mj, pb=pb: s.activation(
                bst[:, 512 * mj: 512 * mj + 512], pb[:, 0:512],
                AF.Identity), waits=[(sPE, zp)], inc=sSC)
            prog.emit("sync", lambda s, mj=mj, s4=s4: s.dma_start(
                out=Bfm_d[mj, :, 512 * s4: 512 * s4 + 512],
                in_=bst[:, 512 * mj: 512 * mj + 512]),
                waits=[(sSC, bev)], inc=sSp, inc_amt=16)
        cst_sp = [0, 0]
        for mj in range(8):
            pb = PS_A if mj % 2 == 0 else PS_B
            buf = mj % 2
            for kk in range(8):
                prog.emit("tensor", lambda t, mj=mj, kk=kk, c0=c0, pb=pb:
                          t.matmul(
                              pb[:, 0:512],
                              Wsb[:, kk * ZD + 3072 + 128 * mj:
                                  kk * ZD + 3072 + 128 * mj + 128],
                              xn[:, kk * XT + c0: kk * XT + c0 + 512],
                              start=(kk == 0), stop=(kk == 7)),
                          waits=([(sSC, None)] if kk == 0 else ()),
                          inc=sPE if kk == 7 else None)
            zp = sPE.n
            cev = prog.emit("scalar", lambda s, buf=buf, pb=pb: s.activation(
                cst[:, 512 * buf: 512 * buf + 512], pb[:, 0:512],
                AF.Identity), waits=[(sPE, zp), (sSp, cst_sp[buf])], inc=sSC)
            cst_sp[buf] = prog.emit(
                "sync", lambda s, mj=mj, s4=s4, buf=buf: s.dma_start(
                    out=Cfm_d[mj, :, 512 * s4: 512 * s4 + 512],
                    in_=cst[:, 512 * buf: 512 * buf + 512]),
                waits=[(sSC, cev)], inc=sSp, inc_amt=16)
        for cc in range(4):
            c = 4 * s4 + cc
            tc0 = 8 + 128 * cc
            prog.wait("vector", sSC, zx_sc)
            last_sil = sil_sc
            for fc in range(16):
                prog.emit("vector", lambda v, fc=fc, tc0=tc0: v.tensor_scalar(
                    cacc[:, :], xr[:, 520 * fc + tc0 - 3:
                                   520 * fc + tc0 + 125],
                    cwc[:, 4 * fc: 4 * fc + 1], None, ALU.mult),
                    waits=[(sSC, last_sil), (sPE, sil_pe)])
                for tap in range(1, 4):
                    prog.drain("vector")
                    prog.emit("vector", lambda v, fc=fc, tc0=tc0, tap=tap:
                              v.scalar_tensor_tensor(
                                  cacc[:, :],
                                  xr[:, 520 * fc + tc0 - 3 + tap:
                                     520 * fc + tc0 + 125 + tap],
                                  cwc[:, 4 * fc + tap: 4 * fc + tap + 1],
                                  cacc[:, :], ALU.mult, ALU.add))
                prog.drain("vector")
                cg = prog.emit("vector", lambda v: v.nop(), inc=sGP)
                last_sil = prog.emit("scalar", lambda s, fc=fc: s.activation(
                    sil[:, 128 * fc: 128 * fc + 128], cacc[:, :], AF.Silu,
                    bias=cbc[:, fc: fc + 1]), waits=[(sGP, cg)], inc=sSC)
            conv_gp = sGP.n
            sil_sc = last_sil
            for wv in range(2):
                tpw = None
                for sl in range(8):
                    fc = 8 * wv + sl
                    tpw = prog.emit("tensor", lambda t, fc=fc, sl=sl:
                                    t.transpose(
                        PS_T[:, 128 * sl: 128 * sl + 128],
                        sil[:, 128 * fc: 128 * fc + 128], idb[:, :]),
                        waits=([(sSC, sil_sc), (sDV, xt_dv), (sDV, bd_dv)]
                               if sl == 0 else ()),
                        inc=sPE if sl == 7 else None)
                prog.wait("vector", sPE, tpw)
                prog.wait("vector", sSp, xts_sp)
                for sl in range(8):
                    fc = 8 * wv + sl
                    prog.emit("vector", lambda v, fc=fc, sl=sl: v.tensor_copy(
                        xts[:, 128 * fc: 128 * fc + 128],
                        PS_T[:, 128 * sl: 128 * sl + 128]))
                prog.drain("vector")
                xt_dv = prog.emit("vector", lambda v: v.nop(), inc=sDV)
            sil_pe = sPE.n
            xts_sp = prog.emit("sync", lambda s, c=c: s.dma_start(
                out=Xtm_d[c, :, :], in_=xts[:, :]),
                waits=[(sDV, xt_dv)], inc=sSp, inc_amt=16)
            tpb = None
            for mj in range(8):
                tpb = prog.emit("tensor", lambda t, mj=mj, cc=cc: t.transpose(
                    PS_T[:, 128 * mj: 128 * mj + 128],
                    bst[:, 512 * mj + 128 * cc: 512 * mj + 128 * cc + 128],
                    idb[:, :]),
                    waits=([(sDV, xt_dv), (sDV, bd_dv)] if mj == 0 else ()),
                    inc=sPE if mj == 7 else None)
            bst_pe = tpb
            prog.wait("vector", sPE, tpb)
            prog.wait("vector", sSp, bds_sp)
            for mj in range(8):
                for hh in range(2):
                    h = 2 * mj + hh
                    prog.emit("vector", lambda v, mj=mj, hh=hh, h=h, c=c:
                              v.tensor_scalar(
                                  bds[:, 64 * h: 64 * h + 64],
                                  PS_T[:, 128 * mj + 64 * hh:
                                       128 * mj + 64 * hh + 64],
                                  ebd[:, 16 * c + h: 16 * c + h + 1],
                                  None, ALU.mult))
            prog.drain("vector")
            bd_dv = prog.emit("vector", lambda v: v.nop(), inc=sDV)
            bds_sp = prog.emit("sync", lambda s, c=c: s.dma_start(
                out=Btm_d[c, :, :], in_=bds[:, :]),
                waits=[(sDV, bd_dv)], inc=sSp, inc_amt=16)
            stt_dv = {}
            for h in range(NH):
                pg = PS_U if h % 2 == 0 else PS_S
                cm = prog.emit("tensor", lambda t, h=h, pg=pg: t.matmul(
                    pg[0:64, 0:128], bds[:, 64 * h: 64 * h + 64],
                    xts[:, 128 * h: 128 * h + 128], start=True, stop=True),
                    waits=([(sDV, bd_dv), (sDV, ctr_dv)] if h <= 1 else
                           [(sDV, stt_dv[h - 2])]), inc=sPE)
                stt_dv[h] = prog.emit(
                    "vector", lambda v, h=h, c=c, pg=pg:
                    v.scalar_tensor_tensor(
                        Gb[:, 128 * h: 128 * h + 128],
                        Gb[:, 128 * h: 128 * h + 128],
                        ela[:, 16 * c + h: 16 * c + h + 1],
                        pg[0:64, 0:128], ALU.mult, ALU.add),
                    waits=[(sPE, cm)], inc=sDV)
                prog.drain("vector")
            ctr_dv = stt_dv[NH - 1]
    prog.emit("sync", lambda s: s.dma_start(out=G_d, in_=Gb[:, :]),
              waits=[(sDV, ctr_dv)], inc=sSp, inc_amt=16)
    prog.emit("sync", lambda s: s.dma_start(
        out=xn_d.rearrange("(a p) t -> p a t", p=128),
        in_=xn.rearrange("p (a b) -> p a b", a=8)), inc=sSp, inc_amt=16)
    m1_done = dict(sp=sSp.n)

    if upto == "m1":
        for sm in (sW, sU, sSp, sOut):
            prog.wait("sync", sm)
        with nc.Block() as blk:
            prog.run(blk)
        ctx.close()
        return nc
    KT = TOK + 640
    LA = Layout(pool)
    LA.cur = HDR
    cslabF = LA.add("cslabF", 128, 8 * 512, F32)
    wqkv_sb = LA.add("wqkv_sb", 128, 8 * 1536, BF16)
    wo_sb = LA.add("wo_sb", 128, 4 * 1024, BF16)
    tbs = [LA.add(f"tb{i}", 128, NKB * 64, BF16) for i in range(4)]
    masksF = LA.add("masksF", 128, 6 * 640, BF16)
    qsrow = LA.add("qsrow", 1, 64, F32)
    ksrow = LA.add("ksrow", 1, 64, F32)
    csrowF = LA.add("csrowF", 1, 256, F32)
    assert LA.cur <= X1_AT, LA.cur
    x1 = pv(pool, X1_AT, 128, 8 * (TOK + 128), BF16)
    LB = Layout(pool)
    LB.cur = RB_AT
    kfm = LB.add("kfm", 128, 4 * KT, BF16)
    Vb = LB.add("Vb", 128, 6 * 512, BF16)
    Oring = LB.add("Oring", 128, 6 * 512, BF16)
    qsl = LB.add("qsl", 128, 2 * 512, BF16)
    kprep = LB.add("kprep", 128, 512, F32)
    qprep = LB.add("qprep", 128, 512, F32)
    rexpF = LB.add("rexpF", 128, 512, F32)
    rot1F = LB.add("rot1F", 128, 512, F32)
    sqbF = LB.add("sqbF", 128, 512, BF16)
    nrmF = LB.add("nrmF", 128, 16, F32)
    AparF = LB.add("AparF", 128, 2 * 640, BF16)
    osbF = LB.add("osbF", 128, 256, F32)
    assert LB.cur <= XN_AT, LB.cur

    for e in ("gpsimd", "sync"):
        for sm in (sPE, sSC, sDV, sGP):
            prog.wait(e, sm)
    for kk in range(8):
        prog.emit("gpsimd", lambda g, kk=kk, _w=wqkv_w, _d=wqkv_sb:
                  g.dma_start(
            out=_d[:, kk * 1536:(kk + 1) * 1536],
            in_=_w[128 * kk:128 * (kk + 1), :]), inc=sW, inc_amt=16)
    for kk in range(4):
        prog.emit("gpsimd", lambda g, kk=kk, _w=wo_w, _d=wo_sb:
                  g.dma_start(
            out=_d[:, kk * 1024:(kk + 1) * 1024],
            in_=_w[128 * kk:128 * (kk + 1), :]), inc=sW, inc_amt=16)
    for ti in range(4):
        prog.emit("gpsimd", lambda g, ti=ti, _d=tbs[ti]: g.dma_start(
            out=_d.rearrange("p (a b) -> p a b", a=NKB),
            in_=rotab.rearrange("q (a p) b -> q p a b", p=128)[ti]),
            inc=sW, inc_amt=16)
    prog.emit("gpsimd", lambda g, _d=masksF: g.dma_start(
        out=_d.rearrange("p (a b) -> p a b", a=6),
        in_=masks_in.transpose([1, 0, 2])), inc=sW, inc_amt=16)
    prog.emit("sync", lambda s, _w=qs_in, _d=qsrow: s.dma_start(
        out=_d[0:1, :], in_=_w), inc=sU, inc_amt=16)
    prog.emit("sync", lambda s, _w=ks_in, _d=ksrow: s.dma_start(
        out=_d[0:1, :], in_=_w), inc=sU, inc_amt=16)

    prog.wait("gpsimd", sOut)
    prog.drain("gpsimd")
    ln_out_gp = prog.emit("gpsimd", lambda g: g.nop(), inc=sGP)
    P = dict(nc=nc, prog=prog, sPE=sPE, sDV=sDV, sSC=sSC, sGP=sGP, sW=sW,
             sU=sU, sSp=sSp, sOut=sOut, PS_A=PS_A, PS_B=PS_B, PS_S=PS_S, PS_T=PS_T,
             PS_U=PS_U, src=xn, xout=x1, wqkv_sb=wqkv_sb, wo_sb=wo_sb,
             tb=tbs, masks=masksF, qsrow=qsrow, ksrow=ksrow, kfm=kfm, Vb=Vb,
             Oring=Oring, qsl=qsl, kprep=kprep, qprep=qprep, rexp=rexpF,
             rot1=rot1F, sqb=sqbF, nrm=nrmF, Apar=AparF, csrow=csrowF,
             cslab=cslabF, out_d=out_d, oneb=oneb, onef=onef,
             idb=idb, osb=osbF)
    P["_acc_gp"] = ln_out_gp
    import os as _os
    P["stage"] = _os.environ.get("F1A_STAGE", "all")
    emit_attn(P)
    if upto == "f1a":
        for sm in (sW, sU, sSp, sOut):
            prog.wait("sync", sm)
        with nc.Block() as blk:
            prog.run(blk)
        ctx.close()
        return nc
    if DEV_DEBUG:
        dbg_x1 = dt_out("dbg_x1", [D, TOK + 128], BF16)
        dbg_kfm = dt_out("dbg_kfm", [128, 4 * KT], BF16)
        dbg_q = dt_out("dbg_q", [128, 1024], BF16)
        prog.emit("sync", lambda s: s.dma_start(
            out=dbg_x1.rearrange("(a p) t -> p a t", p=128),
            in_=x1.rearrange("p (a b) -> p a b", a=8)),
            waits=[(sDV, None)], inc=sSp, inc_amt=16)
        prog.emit("sync", lambda s: s.dma_start(out=dbg_kfm, in_=kfm[:, :]),
                  inc=sSp, inc_amt=16)
        prog.emit("sync", lambda s: s.dma_start(out=dbg_q, in_=qsl[:, :]),
                  inc=sSp, inc_amt=16)

    LM2 = Layout(pool)
    LM2.cur = LA.map["wqkv_sb"][0]
    hlnF = LM2.add("hlnF", 128, 8 * 512, BF16)
    sq2F = LM2.add("sq2F", 128, 8 * 512, BF16)
    geluF = LM2.add("geluF", 128, 512, F32)
    mrowF = LM2.add("mrowF", 1, 512, F32)
    mrow2F = LM2.add("mrow2F", 1, 512, F32)
    b1cF = LM2.add("b1cF", 128, 44, F32)
    b2cF = LM2.add("b2cF", 128, 8, F32)
    lngF = LM2.add("lngF", 128, 8, F32)
    lnbF = LM2.add("lnbF", 128, 8, F32)
    actF = LM2.add("actF", 128, 22 * 512, BF16)
    assert LM2.cur <= X1_AT, LM2.cur
    w1p = pv(pool, RB_AT, 128, 8 * 5632, BF16)
    w2p = pv(pool, RB_AT, 128, 22 * 1024, BF16)

    for kk in range(8):
        prog.emit("gpsimd", lambda g, kk=kk, _w=w1_w, _d=w1p: g.dma_start(
            out=_d[:, kk * 5632:(kk + 1) * 5632],
            in_=_w[128 * kk:128 * (kk + 1), :]),
            waits=([(sPE, None), (sSC, None), (sDV, None), (sSp, None)]
                   if kk == 0 else ()), inc=sW, inc_amt=16)
    for (dst, srcd) in ((b1cF, b1_w), (b2cF, b2_w), (lngF, lng_w),
                        (lnbF, lnb_w)):
        prog.emit("sync", lambda s, d=dst, r=srcd: s.dma_start(
            out=d[:, :], in_=r),
            waits=[(sPE, None), (sSC, None), (sDV, None)],
            inc=sU, inc_amt=16)
    P2 = dict(P)
    P2.update(xio=x1, w1p=w1p, w2p=w2p, b1c=b1cF, b2c=b2cF, lngc=lngF,
              lnbc=lnbF, hln=hlnF, sq2=sq2F, geluS=geluF, mrow=mrowF,
              mrow2=mrow2F, act=actF, act_d=act_d, cslab=cslabF,
              w2_w=w2_w, skip_x=False)
    emit_mlp(P2)

    prog.emit("sync", lambda s: s.dma_start(
        out=x1_d.rearrange("(a p) t -> p a t", p=128),
        in_=x1.rearrange("p (a b) -> p a b", a=8)[:, :, 128:128 + TOK]),
        waits=[(sDV, None)], inc=sSp, inc_amt=16)

    prog.drain("gpsimd")
    fin_gp = prog.emit("gpsimd", lambda g: g.nop(), inc=sGP)
    prog.wait("sync", sGP, fin_gp)
    for sm in (sW, sU, sSp, sOut):
        prog.wait("sync", sm)
    with nc.Block() as blk:
        prog.run(blk)
    ctx.close()
    return nc


def set_tok(tok):
    global TOK, XT, NCH, NQS, NKB
    TOK = tok
    XT = HALO + TOK
    NCH = TOK // 128
    NQS = TOK // 128 + 1
    NKB = (TOK + 640) // 128


def build_launch2(upto="all"):
    nc = bass.Bass("TRN2", target_bir_lowering=False)
    dt_in = lambda n, s, d=F32: nc.dram_tensor(n, s, d, kind="ExternalInput").ap()
    dt_out = lambda n, s, d=F32: nc.dram_tensor(n, s, d, kind="ExternalOutput").ap()
    dt_int = lambda n, s, d=F32: nc.dram_tensor(n, s, d).ap()

    out_init = dt_in("out_init", [D, TOK])
    xn_sp = dt_in("xn_sp", [D, XT], BF16)
    x1_sp = dt_in("x1_sp", [D, TOK], BF16)
    x1_halo = dt_in("x1_halo", [D, HALO], BF16)
    lam_sp = dt_in("lam_sp", [16, NCH * 128 + 1])
    Xtm_d = dt_in("Xtm_sp", [NCH, 128, NH * DH], BF16)
    Btm_d = dt_in("Btm_sp", [NCH, 128, NH * DS], BF16)
    Bfm_d = dt_in("Bfm_sp", [8, 128, TOK], BF16)
    Cfm_d = dt_in("Cfm_sp", [8, 128, TOK], BF16)
    Gin_d = dt_in("Gin", [3, DS, NH * DH])
    lamf_d = dt_in("lamf", [1, 48])
    Wg_w = dt_in("W_gate", [D, DI])
    Wo2_w = dt_in("W_out", [DI, D])
    nw_w = dt_in("norm_w", [128, 16])
    wqkv_w = dt_in("wqkv", [D, 3 * INNER])
    wo_w = dt_in("wo_w", [INNER, D])
    w1_w = dt_in("w1_w", [D, 2 * HIDP])
    b1_w = dt_in("b1_w", [128, 2 * HIDP // 128])
    w2_w = dt_in("w2_w", [HIDP, D])
    b2_w = dt_in("b2_w", [128, D // 128])
    lng_w = dt_in("lng", [128, 8])
    lnb_w = dt_in("lnb", [128, 8])
    qs_in = dt_in("qs", [1, DHEAD])
    ks_in = dt_in("ks", [1, DHEAD])
    rotab = dt_in("rotab", [4, NKB * 128, DHEAD])
    masks_in = dt_in("masks", [6, 128, 640])
    ident_f = dt_in("ident_f", [128, 128])
    iota2_in = dt_in("iota2", [128, 2])
    dmask_in = dt_in("dmask", [128, 128])
    out_d = dt_out("out", [D, TOK])
    act_d = dt_int("act_sc", [4, 128, 22 * 512], BF16)
    d_lr = dt_int("d_lr", [16, 129])
    d3 = dt_int("d3", [3, 2048])

    ctx = contextlib.ExitStack()
    pool = ctx.enter_context(nc.sbuf_tensor("pool", [128, POOL_COLS], BF16))
    ps = lambda n, s, d: ctx.enter_context(nc.psum_tensor(n, s, d))
    PS_A = ps("PS_A", [128, 512], F32)
    PS_B = ps("PS_B", [128, 512], F32)
    PS_S = ps("PS_S", [128, 2048], F32)
    PS_T = ps("PS_T", [128, 1024], BF16)
    PS_U = ps("PS_U", [128, 512], F32)
    mksem = lambda n, dma=False: Sem(ctx.enter_context(nc.semaphore(n)), dma)
    sW = mksem("sW", True); sU = mksem("sU", True); sSp = mksem("sSp", True)
    sPE = mksem("sPE"); sDV = mksem("sDV"); sSC = mksem("sSC")
    sGP = mksem("sGP"); sOut = mksem("sOut", True)
    prog = Prog(nc)

    HDR = 2048
    XN_AT = 161024 + HDR
    X1_AT = 70144 + HDR
    RB_AT = 104960 + HDR

    LM = Layout(pool)
    idf = LM.add("idf", 128, 128, F32)
    idb = LM.add("idb", 128, 128, BF16)
    oneb = LM.add("oneb", 128, 128, BF16)
    onef = LM.add("onef", 128, 128, F32)
    cap_c = LM.add("cap_c", 128, 8, F32)
    assert LM.cur <= HDR, LM.cur
    LM.cur = HDR
    Wg = LM.add("Wg", 128, 8 * DI, BF16)
    Wo2 = LM.add("Wo2", 128, 16 * 1024, BF16)
    Yb = LM.add("Yb", 128, 16 * 512, F32)
    gate = LM.add("gate", 128, 16 * 512, BF16)
    xnsl = LM.add("xnsl", 128, 8 * 512, BF16)
    lam = LM.add("lam", 16, 2056, F32)
    lamrow = LM.add("lamrow", 1, 2064, F32)
    relrow = LM.add("relrow", 1, 2048, F32)
    stagA = LM.add("stagA", 2, 2048, F32)
    stagB = LM.add("stagB", 2, 2048, F32)
    Xc = LM.add("Xc", 128, 2 * 2048, BF16)
    Bc = LM.add("Bc", 128, 2 * 1024, BF16)
    Cc = LM.add("Cc", 128, 2 * 1024, BF16)
    Bdc = LM.add("Bdc", 128, 2 * 1024, BF16)
    Hb = LM.add("Hb", 64, NH * DH, F32)
    Hbf = LM.add("Hbf", 64, NH * DH, BF16)
    ela2 = LM.add("ela2", 64, 16, F32)
    ecol = LM.add("ecol", 64, 48, F32)
    Dm = LM.add("Dm", 128, 128, F32)
    Sdb = LM.add("Sdb", 128, 128, BF16)
    Cdb = LM.add("Cdb", 64, 128, BF16)
    iota2 = LM.add("iota2", 128, 2, F32)
    dmask = LM.add("dmask", 128, 128, BF16)
    lamf = LM.add("lamf", 1, 48, F32)
    nwc = LM.add("nwc", 128, 16, F32)
    rrow = LM.add("rrow", 1, 512, F32)
    cslab2 = pv(pool, LM.map["stagA"][0], 128, 8 * 512, F32)
    Gsl = pv(pool, LM.map["stagA"][0], 64, NH * DH, F32)
    assert LM.cur <= POOL_COLS * 2, LM.cur

    for i, vv in enumerate(CONST_VALS):
        nc.const_aps.register(F32, vv, cap_c[:, i:i + 1])
        prog.emit("vector", lambda v, i=i, vv=vv: v.memset(
            cap_c[:, i:i + 1], vv))
    prog.emit("vector", lambda v: v.memset(oneb[:, :], 1.0))
    prog.emit("vector", lambda v: v.memset(onef[:, :], 1.0))
    prog.emit("vector", lambda v: v.memset(relrow[0:1, :], 1.0))
    prog.drain("vector")
    ones_dv = prog.emit("vector", lambda v: v.nop(), inc=sDV)
    for e in ("scalar", "tensor", "gpsimd", "sync"):
        prog.wait(e, sDV, ones_dv)
    prog.emit("sync", lambda s: s.dma_start(out=d3[0:1, :],
              in_=relrow[0:1, 0:2048]), inc=sSp, inc_amt=16)
    prog.emit("sync", lambda s: s.dma_start(out=d3[2:3, :],
              in_=relrow[0:1, 0:2048]), inc=sSp, inc_amt=16)
    prog.emit("gpsimd", lambda g: g.dma_start(out=out_d, in_=out_init),
              inc=sOut, inc_amt=16)
    prog.drain("gpsimd")
    prog.emit("gpsimd", lambda g: g.nop(), inc=sGP)
    prog.emit("gpsimd", lambda g: g.dma_start(out=dmask[:, :], in_=dmask_in),
              inc=sW, inc_amt=16)
    for (dst, srcd) in ((idf, ident_f), (iota2, iota2_in), (nwc, nw_w),
                        (lamf, lamf_d)):
        prog.emit("sync", lambda s, d=dst, r=srcd: s.dma_start(
            out=d[:, :], in_=r), inc=sU, inc_amt=16)
    prog.emit("sync", lambda s: s.dma_start(
        out=lam[:, 0:NCH * 128 + 1], in_=lam_sp), inc=sU, inc_amt=16)
    prog.emit("gpsimd", lambda g: g.dma_start(out=idb[:, :], in_=ident_f),
              inc=sW, inc_amt=16)
    for kk in range(8):
        prog.emit("gpsimd", lambda g, kk=kk: g.dma_start(
            out=Wg[:, kk * DI:(kk + 1) * DI],
            in_=Wg_w[128 * kk:128 * (kk + 1), :]), inc=sW, inc_amt=16)
    for kk in range(16):
        prog.emit("gpsimd", lambda g, kk=kk: g.dma_start(
            out=Wo2[:, kk * 1024:(kk + 1) * 1024],
            in_=Wo2_w[128 * kk:128 * (kk + 1), :]), inc=sW, inc_amt=16)

    el = prog.emit("scalar", lambda s: s.activation(
        lamf[0:1, :], lamf[0:1, :], AF.Exp), waits=[(sU, None)], inc=sSC)
    eb = prog.emit("tensor", lambda t: t.matmul(
        PS_U[0:64, 0:48], onef[0:1, 0:64], lamf[0:1, 0:48],
        start=True, stop=True), waits=[(sSC, el)], inc=sPE)
    ec_sc = prog.emit("scalar", lambda s: s.activation(
        ecol[:, :], PS_U[0:64, 0:48], AF.Identity),
        waits=[(sPE, eb)], inc=sSC)
    prog.emit("vector", lambda v: v.memset(Hb[:, :], 0.0),
              waits=[(sSC, ec_sc)])
    prog.drain("vector")
    hdv = prog.emit("vector", lambda v: v.nop(), inc=sDV)
    for j in range(3):
        gl = prog.emit("sync", lambda s, j=j: s.dma_start(
            out=Gsl[:, :], in_=Gin_d[j]), waits=[(sDV, hdv)],
            inc=sU, inc_amt=16)
        prog.wait("vector", sU, gl)
        for h in range(NH):
            prog.emit("vector", lambda v, j=j, h=h: v.scalar_tensor_tensor(
                Hb[:, 128 * h: 128 * h + 128], Hb[:, 128 * h: 128 * h + 128],
                ecol[:, 16 * j + h: 16 * j + h + 1],
                Gsl[:, 128 * h: 128 * h + 128], ALU.mult, ALU.add))
            prog.drain("vector")
        hdv = prog.emit("vector", lambda v: v.nop(), inc=sDV)
    prog.wait("vector", sDV, hdv)
    prog.emit("vector", lambda v: v.tensor_copy(Hbf[:, :], Hb[:, :]))
    prog.drain("vector")
    hdv = prog.emit("vector", lambda v: v.nop(), inc=sDV)

    if upto == "h":
        for sm in (sW, sU, sSp, sOut):
            prog.wait("sync", sm)
        with nc.Block() as blk:
            prog.run(blk)
        ctx.close()
        return nc
    prog.wait("vector", sW)
    acc_gp = 0
    hup_dv = [hdv] * NH
    P2_sd = [0]
    P2_y = [0, 0]
    P2_prd = [(0, 0), (0, 0)]
    yev_sc = 0
    epi_dv = hdv
    epi_sc = 0
    ld_done = [0, 0]
    st_dv = hdv
    for c in range(NCH):
        buf = c % 2
        for (dst, dsrc) in (
                (Xc[:, 2048 * buf:2048 * buf + 2048], Xtm_d[c]),
                (Bdc[:, 1024 * buf:1024 * buf + 1024], Btm_d[c]),
                (Bc[:, 1024 * buf:1024 * buf + 1024],
                 Bfm_d[:, :, 128 * c:128 * c + 128].transpose([1, 0, 2])),
                (Cc[:, 1024 * buf:1024 * buf + 1024],
                 Cfm_d[:, :, 128 * c:128 * c + 128].transpose([1, 0, 2]))):
            prog.emit("sync", lambda s, d=dst, r=dsrc: s.dma_start(
                out=d if len(r.shape) == 2 else
                d.rearrange("p (a b) -> p a b", a=8), in_=r),
                waits=[(sPE, ld_done[buf]), (sDV, st_dv)],
                inc=sU, inc_amt=16)
        cld = sU.n
        prog.emit("sync", lambda s, c=c: s.dma_start(
            out=d_lr, in_=lam[:, 128 * c:128 * c + 129]),
            waits=[(sDV, st_dv), (sSC, epi_sc)], inc=sSp, inc_amt=16)
        prog.emit("sync", lambda s: s.dma_start(
            out=lamrow[0:1, 0:2064],
            in_=d_lr.rearrange("a b -> (a b)")[None, :]),
            inc=sSp, inc_amt=16)
        prog.wait("vector", sSp, None)
        prog.wait("vector", sPE, None)
        rel_ap_in = bass.AP(lamrow.tensor, lamrow.offset + 1,
                            [list(lamrow.ap)[0], [129, 16], [1, 128]])
        off_ap = bass.AP(lamrow.tensor, lamrow.offset,
                         [list(lamrow.ap)[0], [129, 16], [0, 128]])
        rl = prog.emit("vector", lambda v, rel_ap_in=rel_ap_in, off_ap=off_ap:
                       v.tensor_tensor(
            relrow[0:1, 0:2048].rearrange("p (a b) -> p a b", a=16),
            rel_ap_in, off_ap, ALU.subtract), inc=sDV)
        prog.emit("sync", lambda s: s.dma_start(
            out=d3[1:2, :], in_=relrow[0:1, 0:2048]),
            waits=[(sDV, rl)], inc=sSp, inc_amt=16)
        ex_sc = prog.emit("scalar", lambda s: s.activation(
            relrow[0:1, 0:2048], relrow[0:1, 0:2048], AF.Exp),
            waits=[(sSp, None), (sDV, rl)], inc=sSC)
        sA = prog.emit("sync", lambda s: s.dma_start(
            out=stagA[0:2, :], in_=d3[1:3, :]),
            waits=[(sDV, st_dv), (sOut, None), (sGP, acc_gp), (sPE, None)],
            inc=sSp, inc_amt=16)
        sB = prog.emit("sync", lambda s: s.dma_start(
            out=stagB[0:2, :], in_=d3[0:2, :]), inc=sSp, inc_amt=16)
        prog.wait("vector", sSp, None)
        ng = prog.emit("vector", lambda v: v.tensor_scalar_mul(
            stagA[0:1, :], stagA[0:1, :], -1.0), inc=sDV)
        erp_ap = bass.AP(relrow.tensor, relrow.offset + 127,
                         [list(relrow.ap)[0], [128, 16]])
        em = prog.emit("tensor", lambda t, erp_ap=erp_ap: t.matmul(
            PS_U[0:64, 64:80], onef[0:1, 0:64], erp_ap,
            start=True, stop=True),
            waits=[(sSC, ex_sc), (sSC, epi_sc), (sDV, None)], inc=sPE)
        ela_sc = prog.emit("scalar", lambda s: s.activation(
            ela2[:, :], PS_U[0:64, 64:80], AF.Identity),
            waits=[(sPE, em)], inc=sSC)
        prog.wait("tensor", sU, cld)
        prog.wait("tensor", sSC, ela_sc)
        prog.wait("tensor", sDV, ng)
        prog.wait("tensor", sDV, epi_dv)
        import os as _os
        m2s = _os.environ.get("M2_STAGE", "all")
        if m2s == "stream":
            st_dv = sDV.n
            continue
        for h in range(NH):
            pr = (c * NH + h) % 2
            sb = 512 * pr
            pY = PS_U if pr == 0 else PS_B
            prog.wait("tensor", sDV, P2_prd[pr][0])
            prog.wait("tensor", sSC, P2_prd[pr][1])
            sm = prog.emit("tensor", lambda t, h=h, buf=buf, sb=sb: t.matmul(
                PS_S[:, sb + 0: sb + 128],
                Bc[64 * (h % 2): 64 * (h % 2) + 64,
                   1024 * buf + 128 * (h // 2): 1024 * buf + 128 * (h // 2) + 128],
                Cc[64 * (h % 2): 64 * (h % 2) + 64,
                   1024 * buf + 128 * (h // 2): 1024 * buf + 128 * (h // 2) + 128],
                start=True, stop=True), inc=sPE)
            dm_ = prog.emit("tensor", lambda t, h=h, sb=sb: t.matmul(
                PS_S[:, sb + 128: sb + 256], stagA[0:2, 128 * h:128 * h + 128],
                stagB[0:2, 128 * h:128 * h + 128], start=True, stop=True),
                inc=sPE)
            tm = prog.emit("vector", lambda v, sb=sb: v.tensor_tensor(
                PS_S[:, sb + 128: sb + 256], PS_S[:, sb + 128: sb + 256],
                dmask[:, :], ALU.add), waits=[(sPE, dm_)], inc=sDV)
            exd = prog.emit("scalar", lambda s, sb=sb: s.activation(
                Dm[:, :], PS_S[:, sb + 128: sb + 256], AF.Exp),
                waits=[(sDV, tm), (sDV, P2_sd[0])], inc=sSC)
            sd = prog.emit("vector", lambda v, sb=sb: v.tensor_tensor(
                Sdb[:, :], PS_S[:, sb + 0: sb + 128], Dm[:, :], ALU.mult),
                waits=[(sSC, exd), (sPE, sm), (sPE, P2_y[0])], inc=sDV)
            P2_sd[0] = sd
            if m2s == "sd":
                P2_prd[pr] = (sd, exd)
                continue
            cb = prog.emit("tensor", lambda t, h=h, sb=sb: t.matmul(
                PS_S[:, sb + 256: sb + 384][0:64, :], onef[0:1, 0:64],
                relrow[0:1, 128 * h:128 * h + 128], start=True, stop=True),
                waits=[(sDV, sd)], inc=sPE)
            cd = prog.emit("vector", lambda v, h=h, buf=buf, sb=sb:
                           v.tensor_tensor(
                Cdb[:, :],
                Cc[64 * (h % 2): 64 * (h % 2) + 64,
                   1024 * buf + 128 * (h // 2): 1024 * buf + 128 * (h // 2) + 128],
                PS_S[:, sb + 256: sb + 384][0:64, :], ALU.mult),
                waits=[(sPE, cb), (sPE, P2_y[1])], inc=sDV)
            y1 = prog.emit("tensor", lambda t, h=h, buf=buf, pY=pY: t.matmul(
                pY[:, 0:128], Xc[:, 2048 * buf + 128 * h:
                                 2048 * buf + 128 * h + 128],
                Sdb[:, :], start=True, stop=False),
                waits=[(sDV, sd), (sDV, hup_dv[h]), (sSC, yev_sc),
                       (sDV, epi_dv)], inc=sPE)
            y2 = prog.emit("tensor", lambda t, h=h, pY=pY: t.matmul(
                pY[:, 0:128], Hbf[0:64, 128 * h:128 * h + 128], Cdb[:, :],
                start=False, stop=True), waits=[(sDV, cd)], inc=sPE)
            P2_y[0] = y1
            P2_y[1] = y2
            P2_prd[pr] = (hup_dv[h], exd)
            if m2s == "y":
                P2_prd[pr] = (cd, exd)
                yev_sc = prog.emit("scalar", lambda s, h=h, c=c, pY=pY:
                                   s.activation(
                    Yb[:, 512 * h + 128 * (c % 4):
                       512 * h + 128 * (c % 4) + 128],
                    pY[:, 0:128], AF.Identity), waits=[(sPE, y2)], inc=sSC)
                continue
            gm = prog.emit("tensor", lambda t, h=h, buf=buf, sb=sb: t.matmul(
                PS_S[0:64, sb + 384: sb + 512],
                Bdc[:, 1024 * buf + 64 * h: 1024 * buf + 64 * h + 64],
                Xc[:, 2048 * buf + 128 * h: 2048 * buf + 128 * h + 128],
                start=True, stop=True), inc=sPE)
            if m2s == "g":
                P2_prd[pr] = (cd, exd)
                hup_dv[h] = prog.emit("vector", lambda v, sb=sb:
                                      v.tensor_copy(
                    Dm[:, 0:64], PS_S[0:64, sb + 384: sb + 448]),
                    waits=[(sPE, gm)], inc=sDV)
                yev_sc = prog.emit("scalar", lambda s, h=h, c=c, pY=pY:
                                   s.activation(
                    Yb[:, 512 * h + 128 * (c % 4):
                       512 * h + 128 * (c % 4) + 128],
                    pY[:, 0:128], AF.Identity), waits=[(sPE, gm)], inc=sSC)
                continue
            prog.emit(
                "vector", lambda v, h=h, sb=sb: v.scalar_tensor_tensor(
                    Hb[:, 128 * h: 128 * h + 128],
                    Hb[:, 128 * h: 128 * h + 128],
                    ela2[:, h: h + 1], PS_S[0:64, sb + 384: sb + 512],
                    ALU.mult, ALU.add), waits=[(sPE, gm)])
            prog.drain("vector")
            hup_dv[h] = prog.emit("vector", lambda v, h=h: v.tensor_copy(
                Hbf[:, 128 * h: 128 * h + 128],
                Hb[:, 128 * h: 128 * h + 128]), inc=sDV)
            prog.drain("vector")
            yev_sc = prog.emit("scalar", lambda s, h=h, c=c, pY=pY:
                               s.activation(
                Yb[:, 512 * h + 128 * (c % 4): 512 * h + 128 * (c % 4) + 128],
                pY[:, 0:128], AF.Identity), waits=[(sPE, gm)], inc=sSC)
        ld_done[buf] = sPE.n
        st_dv = sDV.n
        if c % 4 == 3 and m2s not in ("units",):
            s4 = c // 4
            if DEV_DEBUG and s4 == 0:
                dbg_Y = nc.dram_tensor("dbg_Y", [128, 16 * 512], F32,
                                       kind="ExternalOutput").ap()
                prog.emit("sync", lambda s: s.dma_start(
                    out=dbg_Y, in_=Yb[:, :]), waits=[(sSC, yev_sc)],
                    inc=sSp, inc_amt=16)
            xl = prog.emit("sync", lambda s, s4=s4: s.dma_start(
                out=xnsl.rearrange("p (a b) -> p a b", a=8),
                in_=_tslice(xn_sp, OWN0 + 512 * s4, 512)),
                waits=[(sPE, None)], inc=sU, inc_amt=16)
            gev = 0
            for mj in range(16):
                pb = PS_A if mj % 2 == 0 else PS_B
                gl_ = None
                for kk in range(8):
                    gl_ = prog.emit("tensor",
                                    lambda t, mj=mj, kk=kk, pb=pb: t.matmul(
                        pb[:, 0:512],
                        Wg[:, kk * DI + 128 * mj: kk * DI + 128 * mj + 128],
                        xnsl[:, kk * 512: kk * 512 + 512],
                        start=(kk == 0), stop=(kk == 7)),
                        waits=([(sU, None), (sW, None), (sSC, yev_sc),
                                (sSC, gev), (sDV, None)] if kk == 0 else ()),
                        inc=sPE if kk == 7 else None)
                gev = prog.emit("scalar", lambda s, mj=mj, pb=pb:
                                s.activation(
                    gate[:, 512 * mj: 512 * mj + 512], pb[:, 0:512],
                    AF.Sigmoid), waits=[(sPE, gl_)], inc=sSC)
            prog.wait("vector", sSC, gev)
            prog.wait("vector", sSp)
            for mj in range(16):
                prog.emit("vector", lambda v, mj=mj: v.tensor_mul(
                    Yb[:, 512 * mj: 512 * mj + 512],
                    Yb[:, 512 * mj: 512 * mj + 512],
                    gate[:, 512 * mj: 512 * mj + 512]))
            prog.drain("vector")
            yg_dv = prog.emit("vector", lambda v: v.nop(), inc=sDV)
            sqs = prog.emit("scalar", lambda s: [s.activation(
                gate[:, 512 * mj: 512 * mj + 512],
                Yb[:, 512 * mj: 512 * mj + 512], AF.Square)
                for mj in range(16)][-1], waits=[(sDV, yg_dv)], inc=sSC)
            sq_mm = None
            for mj in range(16):
                sq_mm = prog.emit("tensor", lambda t, mj=mj: t.matmul(
                    PS_U[0:1, 0:512], oneb[:, 0:1],
                    gate[:, 512 * mj: 512 * mj + 512],
                    start=(mj == 0), stop=(mj == 15)),
                    waits=([(sSC, sqs)] if mj == 0 else ()),
                    inc=sPE if mj == 15 else None)
            rs = prog.emit("scalar", lambda s: s.activation(
                rrow[0:1, 0:512], PS_U[0:1, 0:512], AF.Sqrt,
                scale=1.0 / DI, bias=RMS_EPS), waits=[(sPE, sq_mm)], inc=sSC)
            prog.wait("vector", sSC, rs)
            prog.emit("vector", lambda v: v.reciprocal(
                rrow[0:1, 0:512], rrow[0:1, 0:512]))
            prog.drain("vector")
            rdv = prog.emit("vector", lambda v: v.nop(), inc=sDV)
            rb = prog.emit("tensor", lambda t: t.matmul(
                PS_S[:, 0:512], onef[0:1, :], rrow[0:1, 0:512],
                start=True, stop=True),
                waits=[(sDV, rdv), (sDV, None), (sSC, None)], inc=sPE)
            prog.wait("vector", sPE, rb)
            prog.wait("vector", sSC, sqs)
            yn_dv = None
            for mj in range(16):
                yn_dv = prog.emit("vector", lambda v, mj=mj:
                                  v.scalar_tensor_tensor(
                    gate[:, 512 * mj: 512 * mj + 512],
                    Yb[:, 512 * mj: 512 * mj + 512],
                    nwc[:, mj: mj + 1], PS_S[:, 0:512],
                    ALU.mult, ALU.mult), inc=sDV)
            prog.drain("vector")
            yn_dv = prog.emit("vector", lambda v: v.nop(), inc=sDV)
            wev = 0
            for m in range(8):
                pb = PS_A if m % 2 == 0 else PS_B
                wm = None
                for kk in range(16):
                    wm = prog.emit("tensor", lambda t, kk=kk, m=m, pb=pb:
                                   t.matmul(
                        pb[:, 0:512],
                        Wo2[:, kk * 1024 + 128 * m: kk * 1024 + 128 * m + 128],
                        gate[:, 512 * kk: 512 * kk + 512],
                        start=(kk == 0), stop=(kk == 15)),
                        waits=([(sDV, yn_dv), (sSC, wev), (sSp, None)]
                               if kk == 0 else ()),
                        inc=sPE if kk == 15 else None)
                wev = prog.emit("scalar", lambda s, m=m, pb=pb: s.activation(
                    cslab2[:, 512 * m: 512 * m + 512], pb[:, 0:512],
                    AF.Identity), waits=[(sPE, wm), (sOut, None),
                                          (sGP, acc_gp)], inc=sSC)
            if DEV_DEBUG and s4 == 0:
                dbg_c2 = nc.dram_tensor("dbg_c2", [128, 8 * 512], F32,
                                        kind="ExternalOutput").ap()
                prog.emit("sync", lambda s: s.dma_start(
                    out=dbg_c2, in_=cslab2[:, :]), waits=[(sSC, wev)],
                    inc=sSp, inc_amt=16)
            acc_gp = emit_accum(
                prog, sGP, _tslice(out_d, 512 * s4, 512),
                cslab2.rearrange("p (a b) -> p a b", a=8),
                ([(sSp, None)] if (DEV_DEBUG and s4 == 0) else [])
                + [(sSC, wev), (sGP, acc_gp)], sOut=sOut)
            epi_dv = sDV.n
            epi_sc = sSC.n

    if upto == "scan":
        for sm in (sW, sU, sSp, sOut):
            prog.wait("sync", sm)
        with nc.Block() as blk:
            prog.run(blk)
        ctx.close()
        return nc
    KT = TOK + 640
    x1full = pv(pool, XN_AT, 128, 8 * XT, BF16)
    for e in ("gpsimd", "sync"):
        for sm in (sPE, sSC, sDV, sGP):
            prog.wait(e, sm)
    prog.wait("sync", sOut)
    x1l = prog.emit("sync", lambda s: s.dma_start(
        out=x1full.rearrange("p (a b) -> p a b", a=8)[:, :, 0:HALO],
        in_=x1_halo.rearrange("(a p) t -> p a t", p=128)),
        inc=sU, inc_amt=16)
    x1l = prog.emit("sync", lambda s: s.dma_start(
        out=x1full.rearrange("p (a b) -> p a b", a=8)[:, :, HALO:XT],
        in_=x1_sp.rearrange("(a p) t -> p a t", p=128)),
        inc=sU, inc_amt=16)
    LA = Layout(pool)
    LA.cur = HDR
    cslabF = LA.add("cslabF", 128, 8 * 512, F32)
    wqkv_sb = LA.add("wqkv_sb", 128, 8 * 1536, BF16)
    wo_sb = LA.add("wo_sb", 128, 4 * 1024, BF16)
    tbs = [LA.add(f"tb{i}", 128, NKB * 64, BF16) for i in range(4)]
    masksF = LA.add("masksF", 128, 6 * 640, BF16)
    qsrow = LA.add("qsrow", 1, 64, F32)
    ksrow = LA.add("ksrow", 1, 64, F32)
    csrowF = LA.add("csrowF", 1, 256, F32)
    assert LA.cur <= X1_AT, LA.cur
    x2 = pv(pool, X1_AT, 128, 8 * (TOK + 128), BF16)
    LB = Layout(pool)
    LB.cur = RB_AT
    kfm = LB.add("kfm", 128, 4 * KT, BF16)
    Vb = LB.add("Vb", 128, 6 * 512, BF16)
    Oring = LB.add("Oring", 128, 6 * 512, BF16)
    qsl = LB.add("qsl", 128, 2 * 512, BF16)
    kprep = LB.add("kprep", 128, 512, F32)
    qprep = LB.add("qprep", 128, 512, F32)
    rexpF = LB.add("rexpF", 128, 512, F32)
    rot1F = LB.add("rot1F", 128, 512, F32)
    sqbF = LB.add("sqbF", 128, 512, BF16)
    nrmF = LB.add("nrmF", 128, 16, F32)
    AparF = LB.add("AparF", 128, 2 * 640, BF16)
    osbF = LB.add("osbF", 128, 256, F32)
    assert LB.cur <= XN_AT, LB.cur

    for kk in range(8):
        prog.emit("gpsimd", lambda g, kk=kk, _w=wqkv_w, _d=wqkv_sb:
                  g.dma_start(
            out=_d[:, kk * 1536:(kk + 1) * 1536],
            in_=_w[128 * kk:128 * (kk + 1), :]), inc=sW, inc_amt=16)
    for kk in range(4):
        prog.emit("gpsimd", lambda g, kk=kk, _w=wo_w, _d=wo_sb:
                  g.dma_start(
            out=_d[:, kk * 1024:(kk + 1) * 1024],
            in_=_w[128 * kk:128 * (kk + 1), :]), inc=sW, inc_amt=16)
    for ti in range(4):
        prog.emit("gpsimd", lambda g, ti=ti, _d=tbs[ti]: g.dma_start(
            out=_d.rearrange("p (a b) -> p a b", a=NKB),
            in_=rotab.rearrange("q (a p) b -> q p a b", p=128)[ti]),
            inc=sW, inc_amt=16)
    prog.emit("gpsimd", lambda g, _d=masksF: g.dma_start(
        out=_d.rearrange("p (a b) -> p a b", a=6),
        in_=masks_in.transpose([1, 0, 2])), inc=sW, inc_amt=16)
    prog.emit("sync", lambda s, _w=qs_in, _d=qsrow: s.dma_start(
        out=_d[0:1, :], in_=_w), inc=sU, inc_amt=16)
    prog.emit("sync", lambda s, _w=ks_in, _d=ksrow: s.dma_start(
        out=_d[0:1, :], in_=_w), inc=sU, inc_amt=16)

    P = dict(nc=nc, prog=prog, sPE=sPE, sDV=sDV, sSC=sSC, sGP=sGP, sW=sW,
             sU=sU, sSp=sSp, sOut=sOut, PS_A=PS_A, PS_B=PS_B, PS_S=PS_S,
             PS_T=PS_T, PS_U=PS_U, src=x1full, xout=x2, wqkv_sb=wqkv_sb,
             wo_sb=wo_sb, tb=tbs, masks=masksF, qsrow=qsrow, ksrow=ksrow,
             kfm=kfm, Vb=Vb, Oring=Oring, qsl=qsl, kprep=kprep, qprep=qprep,
             rexp=rexpF, rot1=rot1F, sqb=sqbF, nrm=nrmF, Apar=AparF,
             csrow=csrowF, cslab=cslabF, out_d=out_d, oneb=oneb, onef=onef,
             idb=idb, osb=osbF, tag="2")
    P["_acc_gp"] = acc_gp
    import os as _os
    P["stage"] = _os.environ.get("F1A_STAGE", "all")
    emit_attn(P)
    if upto == "f2a":
        for sm in (sW, sU, sSp, sOut):
            prog.wait("sync", sm)
        with nc.Block() as blk:
            prog.run(blk)
        ctx.close()
        return nc

    LM2 = Layout(pool)
    LM2.cur = LA.map["wqkv_sb"][0]
    hlnF = LM2.add("hlnF", 128, 8 * 512, BF16)
    sq2F = LM2.add("sq2F", 128, 8 * 512, BF16)
    geluF = LM2.add("geluF", 128, 512, F32)
    mrowF = LM2.add("mrowF", 1, 512, F32)
    mrow2F = LM2.add("mrow2F", 1, 512, F32)
    b1cF = LM2.add("b1cF", 128, 44, F32)
    b2cF = LM2.add("b2cF", 128, 8, F32)
    lngF = LM2.add("lngF", 128, 8, F32)
    lnbF = LM2.add("lnbF", 128, 8, F32)
    actF = LM2.add("actF", 128, 22 * 512, BF16)
    assert LM2.cur <= X1_AT, LM2.cur
    w1p = pv(pool, RB_AT, 128, 8 * 5632, BF16)
    w2p = pv(pool, RB_AT, 128, 22 * 1024, BF16)
    for kk in range(8):
        prog.emit("gpsimd", lambda g, kk=kk, _w=w1_w, _d=w1p: g.dma_start(
            out=_d[:, kk * 5632:(kk + 1) * 5632],
            in_=_w[128 * kk:128 * (kk + 1), :]),
            waits=([(sPE, None), (sSC, None), (sDV, None), (sSp, None)]
                   if kk == 0 else ()), inc=sW, inc_amt=16)
    for (dst, srcd) in ((b1cF, b1_w), (b2cF, b2_w), (lngF, lng_w),
                        (lnbF, lnb_w)):
        prog.emit("sync", lambda s, d=dst, r=srcd: s.dma_start(
            out=d[:, :], in_=r),
            waits=[(sPE, None), (sSC, None), (sDV, None)],
            inc=sU, inc_amt=16)
    P2 = dict(P)
    P2.update(xio=x2, w1p=w1p, w2p=w2p, b1c=b1cF, b2c=b2cF, lngc=lngF,
              lnbc=lnbF, hln=hlnF, sq2=sq2F, geluS=geluF, mrow=mrowF,
              mrow2=mrow2F, act=actF, act_d=act_d, cslab=cslabF,
              w2_w=w2_w, skip_x=True)
    emit_mlp(P2)

    prog.drain("gpsimd")
    fin_gp = prog.emit("gpsimd", lambda g: g.nop(), inc=sGP)
    prog.wait("sync", sGP, fin_gp)
    for sm in (sW, sU, sSp, sOut):
        prog.wait("sync", sm)
    with nc.Block() as blk:
        prog.run(blk)
    ctx.close()
    return nc



def build_fused():
    nc = bass.Bass("TRN2", target_bir_lowering=False)
    dt_in = lambda n, s, d=F32: nc.dram_tensor(n, s, d, kind="ExternalInput").ap()
    dt_out = lambda n, s, d=F32: nc.dram_tensor(n, s, d, kind="ExternalOutput").ap()
    dt_int = lambda n, s, d=F32: nc.dram_tensor(n, s, d).ap()

    u_t = dt_in("u_t", [D, XT])
    W_in = dt_in("W_in", [D, ZD])
    conv_w = dt_in("conv_w", [128, 16 * 4])
    conv_b = dt_in("conv_b", [128, 16])
    delta_p = dt_in("delta_p", [16, 1])
    wqkv_w = dt_in("wqkv", [D, 3 * INNER])
    wo_w = dt_in("wo_w", [INNER, D])
    w1_w = dt_in("w1_w", [D, 2 * HIDP])
    b1_w = dt_in("b1_w", [128, 2 * HIDP // 128])
    w2_w = dt_in("w2_w", [HIDP, D])
    b2_w = dt_in("b2_w", [128, D // 128])
    lng_w = dt_in("lng", [128, 8])
    lnb_w = dt_in("lnb", [128, 8])
    qs_in = dt_in("qs", [1, DHEAD])
    ks_in = dt_in("ks", [1, DHEAD])
    rotab = dt_in("rotab", [4, NKB * 128, DHEAD])
    masks_in = dt_in("masks", [6, 128, 640])
    ident_f = dt_in("ident_f", [128, 128])
    out_d = dt_out("out", [D, TOK])
    xn_d = dt_int("xn_sp", [D, XT], BF16)
    lam_d = dt_int("lam_sp", [16, NCH * 128 + 1])
    G_d = dt_int("G_sp", [DS, NH * DH])
    Xtm_d = dt_int("Xtm_sp", [NCH, 128, NH * DH], BF16)
    Btm_d = dt_int("Btm_sp", [NCH, 128, NH * DS], BF16)
    Bfm_d = dt_int("Bfm_sp", [8, 128, TOK], BF16)
    Cfm_d = dt_int("Cfm_sp", [8, 128, TOK], BF16)
    act_d = dt_int("act_sc", [4, 128, 22 * 512], BF16)
    Wg_w = dt_in("W_gate", [D, DI])
    Wo2_w = dt_in("W_out", [DI, D])
    nw_w = dt_in("norm_w", [128, 16])
    wqkv2_w = dt_in("wqkv2", [D, 3 * INNER])
    wo2_w = dt_in("wo2_w", [INNER, D])
    w12_w = dt_in("w1_w2", [D, 2 * HIDP])
    b12_w = dt_in("b1_w2", [128, 2 * HIDP // 128])
    w22_w = dt_in("w2_w2", [HIDP, D])
    b22_w = dt_in("b2_w2", [128, D // 128])
    lng2_w = dt_in("lng2", [128, 8])
    lnb2_w = dt_in("lnb2", [128, 8])
    qs2_in = dt_in("qs2", [1, DHEAD])
    ks2_in = dt_in("ks2", [1, DHEAD])
    dmask_in = dt_in("dmask", [128, 128])
    ohv_in = dt_in("ohv", [64, 8])
    selv_in = dt_in("selv", [128, 8])
    gall_d = dt_int("gall", [8, DS, NH * DH])
    lall_d = dt_int("lall", [8, 16])
    lsrc_d = dt_int("lsrc", [16, 1])
    x1t_d = dt_int("x1t", [128, 8 * 520], BF16)
    x1ta_d = dt_int("x1ta", [8, 128, 8 * 520], BF16)
    d_lr = dt_int("d_lr", [16, 129])
    d3 = dt_int("d3", [3, 2048])

    ctx = contextlib.ExitStack()
    pool = ctx.enter_context(nc.sbuf_tensor("pool", [128, POOL_COLS], BF16))
    ps = lambda n, s, d: ctx.enter_context(nc.psum_tensor(n, s, d))
    PS_A = ps("PS_A", [128, 512], F32)
    PS_B = ps("PS_B", [128, 512], F32)
    PS_S = ps("PS_S", [128, 2048], F32)
    PS_T = ps("PS_T", [128, 1024], BF16)
    PS_U = ps("PS_U", [128, 512], F32)
    mksem = lambda n, dma=False: Sem(ctx.enter_context(nc.semaphore(n)), dma)
    sW = mksem("sW", True); sU = mksem("sU", True); sSp = mksem("sSp", True)
    sPE = mksem("sPE"); sDV = mksem("sDV"); sSC = mksem("sSC")
    sGP = mksem("sGP"); sOut = mksem("sOut", True)
    sCC = mksem("sCC")
    prog = Prog(nc)

    HDR = 2048
    XN_AT = 161024 + HDR
    X1_AT = 70144 + HDR
    RB_AT = 104960 + HDR
    assert XN_AT + 8 * XT * 2 <= POOL_COLS * 2, (XN_AT, XT)

    LM = Layout(pool)
    idf = LM.add("idf", 128, 128, F32)
    idb = LM.add("idb", 128, 128, BF16)
    oneb = LM.add("oneb", 128, 128, BF16)
    onef = LM.add("onef", 128, 128, F32)
    cap_c = LM.add("cap_c", 128, 8, F32)
    assert LM.cur <= HDR, LM.cur
    LM.cur = HDR

    Wsb = LM.add("Wsb", 128, 8 * ZD, BF16)
    u_sl = LM.add("u_sl", 128, 8 * 512, F32)
    ubf = LM.add("ubf", 128, 8 * 512, BF16)
    xr = LM.add("xr", 128, 16 * 520, BF16)
    sil = LM.add("sil", 128, 16 * 128, BF16)
    bst = LM.add("bst", 128, 8 * 512, BF16)
    cst = LM.add("cst", 128, 2 * 512, BF16)
    bds = LM.add("bds", 128, 16 * 64, BF16)
    xts = LM.add("xts", 128, 16 * 128, BF16)
    dtb = LM.add("dtb", 16, 2048, F32)
    lam = LM.add("lam", 16, 2056, F32)
    ebd = LM.add("ebd", 128, NCH * 16, F32)
    ela = LM.add("ela", 64, NCH * 16, F32)
    Gb = LM.add("Gb", 64, NH * DH, F32)
    cwc = LM.add("cwc", 128, 64, F32)
    cbc = LM.add("cbc", 128, 16, F32)
    dpc = LM.add("dpc", 16, 1, F32)
    srow = LM.add("srow", 1, 512, F32)
    srow2 = LM.add("srow2", 1, 512, F32)
    prebuf = LM.add("prebuf", 16, 128, F32)
    cacc = LM.add("cacc", 128, 128, F32)
    row1 = LM.add("row1", 1, 16, F32)
    assert LM.cur <= XN_AT, LM.cur
    xn = pv(pool, XN_AT, 128, 8 * XT, BF16)
    osl = pv(pool, LM.map["xr"][0], 128, 8 * 512, F32)
    dts = pv(pool, LM.map["u_sl"][0], 16, 2048, F32)

    for k in range(8):
        prog.emit("gpsimd", lambda g, k=k: g.dma_start(
            out=Wsb[:, k * ZD:(k + 1) * ZD],
            in_=W_in[128 * k:128 * (k + 1), :]), inc=sW, inc_amt=16)
    for (dst, src) in ((cwc, conv_w), (cbc, conv_b), (dpc, delta_p),
                      (idf, ident_f)):
        prog.emit("sync", lambda s, d=dst, r=src: s.dma_start(
            out=d[:, :], in_=r), inc=sU, inc_amt=16)
    prog.emit("gpsimd", lambda g: g.dma_start(out=idb[:, :], in_=ident_f),
              inc=sW, inc_amt=16)
    w_all = sW.n
    c_all = sU.n
    prog.emit("vector", lambda v: v.memset(oneb[:, :], 1.0))
    prog.emit("vector", lambda v: v.memset(onef[:, :], 1.0))
    prog.emit("vector", lambda v: v.memset(lam[:, 0:1], 0.0))
    for i, vv in enumerate(CONST_VALS):
        nc.const_aps.register(F32, vv, cap_c[:, i:i + 1])
        prog.emit("vector", lambda v, i=i, vv=vv: v.memset(
            cap_c[:, i:i + 1], vv))
    prog.drain("vector")
    ones_dv = prog.emit("vector", lambda v: v.nop(), inc=sDV)
    for e in ("scalar", "tensor", "gpsimd", "sync"):
        prog.wait(e, sDV, ones_dv)

    slabs = [(c, min(512, XT - c)) for c in range(0, XT, 512)]
    prev_done_dv = ones_dv
    prev_out = 0
    prev_mm2 = 0
    prev_rows_dv = 0
    for (c0, w) in slabs:
        ud = prog.emit("sync", lambda s, c0=c0, w=w: s.dma_start(
            out=u_sl.rearrange("p (a b) -> p a b", a=8)[:, :, :w],
            in_=_tslice(u_t, c0, w)),
            waits=[(sDV, prev_done_dv), (sOut, prev_out)],
            inc=sU, inc_amt=16)
        vb = prog.emit("vector", lambda v, w=w: v.tensor_copy(
            ubf.rearrange("p (a b) -> p a b", a=8)[:, :, :w],
            u_sl.rearrange("p (a b) -> p a b", a=8)[:, :, :w]),
            waits=[(sU, ud), (sPE, prev_mm2)], inc=sDV)
        mm1 = None
        for kk in range(8):
            mm1 = prog.emit("tensor", lambda t, kk=kk, w=w: t.matmul(
                PS_U[0:1, :w], oneb[:, 0:1], ubf[:, kk * 512: kk * 512 + w],
                start=(kk == 0), stop=(kk == 7)),
                waits=[(sDV, vb), (sDV, prev_rows_dv)] if kk == 0 else (),
                inc=sPE if kk == 7 else None)
        sq = prog.emit("scalar", lambda s, w=w: s.activation(
            ubf.rearrange("p (a b) -> p a b", a=8)[:, :, :w],
            ubf.rearrange("p (a b) -> p a b", a=8)[:, :, :w], AF.Square),
            waits=[(sPE, mm1)], inc=sSC)
        mm2 = None
        for kk in range(8):
            mm2 = prog.emit("tensor", lambda t, kk=kk, w=w: t.matmul(
                PS_U[64:65, :w], oneb[:, 0:1], ubf[:, kk * 512: kk * 512 + w],
                start=(kk == 0), stop=(kk == 7)),
                waits=[(sSC, sq)] if kk == 0 else (),
                inc=sPE if kk == 7 else None)
        dvr = emit_rows_ln(prog, "vector", PS_U, srow, srow2, w, 1e-5,
                           sPE, sDV, sSC, mm2)
        own0 = max(c0, OWN0)
        w2 = c0 + w - own0
        prog.emit("tensor", lambda t, w=w: t.matmul(
            PS_S[:, 0:w], onef[0:1, :], srow2[0:1, :w], start=True, stop=True),
            waits=[(sDV, dvr)])
        bc2 = prog.emit("tensor", lambda t, w=w: t.matmul(
            PS_S[:, 512:512 + w], onef[0:1, :], srow[0:1, :w],
            start=True, stop=True), inc=sPE)
        prog.wait("vector", sPE, bc2)
        for kk in range(8):
            prog.emit("vector", lambda v, kk=kk, w=w: v.tensor_mul(
                u_sl[:, kk * 512: kk * 512 + w],
                u_sl[:, kk * 512: kk * 512 + w], PS_S[:, 0:w]))
        prog.drain("vector")
        for kk in range(8):
            prog.emit("vector", lambda v, kk=kk, w=w: v.tensor_add(
                u_sl[:, kk * 512: kk * 512 + w],
                u_sl[:, kk * 512: kk * 512 + w], PS_S[:, 512:512 + w]))
        prog.drain("vector")
        xn_dv = prog.emit("vector", lambda v, c0=c0, w=w: v.tensor_copy(
            xn.rearrange("p (a b) -> p a b", a=8)[:, :, c0:c0 + w],
            u_sl.rearrange("p (a b) -> p a b", a=8)[:, :, :w]), inc=sDV)
        prev_done_dv = xn_dv
        if w2 > 0:
            o_ = own0 - c0
            ud2 = prog.emit("sync", lambda s, o=own0, w2=w2: s.dma_start(
                out=osl.rearrange("p (a b) -> p a b", a=8)[:, :, :w2],
                in_=_tslice(u_t, o, w2)), inc=sU, inc_amt=16)
            prog.wait("vector", sU, ud2)
            for kk in range(8):
                prog.emit("vector", lambda v, kk=kk, o_=o_, w2=w2:
                          v.tensor_tensor(
                              osl[:, kk * 512: kk * 512 + w2],
                              osl[:, kk * 512: kk * 512 + w2],
                              u_sl[:, kk * 512 + o_: kk * 512 + o_ + w2],
                              ALU.add))
            prog.drain("vector")
            odv = prog.emit("vector", lambda v: v.nop(), inc=sDV)
            prog.emit("gpsimd", lambda g, o=own0 - OWN0, w2=w2: g.dma_start(
                out=_tslice(out_d, o, w2),
                in_=osl.rearrange("p (a b) -> p a b", a=8)[:, :, :w2]),
                waits=[(sDV, odv)], inc=sOut, inc_amt=16)
            prev_done_dv = odv
        prev_out = sOut.n
        prev_mm2 = mm2
        prev_rows_dv = dvr
    ln_done_dv = sDV.n
    ln_done_out = sOut.n

    NSL = TOK // 512
    prev_dt_sc = 0
    for s4 in range(NSL):
        c0 = OWN0 + 512 * s4
        dt_pe = None
        for kk in range(8):
            dt_pe = prog.emit("tensor", lambda t, kk=kk, c0=c0: t.matmul(
                PS_U[0:16, 0:512], Wsb[:, kk * ZD + 4096: kk * ZD + 4112],
                xn[:, kk * XT + c0: kk * XT + c0 + 512],
                start=(kk == 0), stop=(kk == 7)),
                waits=([(sW, w_all), (sDV, ln_done_dv), (sSC, prev_dt_sc)]
                       if kk == 0 else ()), inc=sPE if kk == 7 else None)
        prev_dt_sc = prog.emit("scalar", lambda s, s4=s4: s.activation(
            dtb[:, 512 * s4: 512 * (s4 + 1)], PS_U[0:16, 0:512], AF.Identity),
            waits=[(sPE, dt_pe)], inc=sSC)
    prog.wait("scalar", sU, c_all)
    prog.wait("scalar", sOut)
    prog.drain("scalar")
    prog.emit("scalar", lambda s: s.activation(
        dts[:, 0:TOK], dtb[:, 0:TOK], AF.Exp, bias=dpc[:, 0:1]))
    prog.drain("scalar")
    prog.emit("scalar", lambda s: s.activation(
        dts[:, 0:TOK], dts[:, 0:TOK], AF.Ln, bias=1.0))
    prog.drain("scalar")
    sp_sc = prog.emit("scalar", lambda s: s.activation(
        dtb[:, 0:TOK], dtb[:, 0:TOK], AF.Abs), inc=sSC)
    prog.wait("vector", sSC, sp_sc)
    prog.emit("vector", lambda v: v.scalar_tensor_tensor(
        dtb[:, 0:TOK], dts[:, 0:TOK], -1.0, dtb[:, 0:TOK],
        ALU.mult, ALU.mult))
    prog.drain("vector")
    lam_dv = prog.emit("vector", lambda v: v.tensor_tensor_scan(
        lam[:, 1:TOK + 1], dtb[:, 0:TOK], dtb[:, 0:TOK], lam[:, 0:1],
        ALU.add, ALU.min), inc=sDV)
    prog.emit("sync", lambda s: s.dma_start(
        out=lam_d, in_=lam[:, 0:TOK + 1]), waits=[(sDV, lam_dv)],
        inc=sSp, inc_amt=16)

    pre_sc = 0
    pre_dv = 0
    for c in range(NCH):
        e0, s0 = 128 * c + 128, 128 * c
        pd = prog.emit("vector", lambda v, c=c, e0=e0: v.tensor_scalar(
            prebuf[:, :], lam[:, 128 * c + 1: 128 * c + 129],
            lam[:, e0:e0 + 1], None, ALU.subtract),
            waits=[(sDV, lam_dv), (sSC, pre_sc)], inc=sDV)
        tp = prog.emit("tensor", lambda t: t.transpose(
            PS_S[:, 512:528], prebuf[:, :], idf[0:16, 0:16]),
            waits=[(sDV, pd), (sU, c_all), (sSC, pre_sc)], inc=sPE)
        ebd_sc = prog.emit("scalar", lambda s, c=c: s.activation(
            ebd[:, 16 * c: 16 * c + 16], PS_S[:, 512:528], AF.Exp,
            scale=-1.0), waits=[(sPE, tp)], inc=sSC)
        dcol = prog.emit("vector", lambda v, e0=e0, s0=s0: v.tensor_tensor(
            prebuf[:, 0:1], lam[:, e0:e0 + 1], lam[:, s0:s0 + 1],
            ALU.subtract), waits=[(sPE, tp)], inc=sDV)
        ec = prog.emit("scalar", lambda s: s.activation(
            prebuf[:, 1:2], prebuf[:, 0:1], AF.Exp),
            waits=[(sDV, dcol)], inc=sSC)
        tp2 = prog.emit("tensor", lambda t: t.transpose(
            PS_U[0:1, 0:16], prebuf[:, 1:2], idf[0:16, 0:16]),
            waits=[(sSC, ec), (sSC, pre_sc)], inc=sPE)
        rc = prog.emit("vector", lambda v: v.tensor_copy(
            row1[0:1, 0:16], PS_U[0:1, 0:16]), waits=[(sPE, tp2)], inc=sDV)
        bc = prog.emit("tensor", lambda t: t.matmul(
            PS_U[0:64, 32:48], onef[0:1, 0:64], row1[0:1, 0:16],
            start=True, stop=True), waits=[(sDV, rc)], inc=sPE)
        pre_sc = prog.emit("scalar", lambda s, c=c: s.activation(
            ela[:, 16 * c: 16 * c + 16], PS_U[0:64, 32:48], AF.Identity),
            waits=[(sPE, bc)], inc=sSC)
    epre_sc = pre_sc

    prog.emit("vector", lambda v: v.memset(Gb[:, :], 0.0))
    prog.drain("vector")
    g_dv = prog.emit("vector", lambda v: v.nop(), inc=sDV)
    conv_gp = 0
    sil_pe = 0
    xts_sp = 0
    bds_sp = 0
    bst_pe = 0
    xt_dv = g_dv
    bd_dv = g_dv
    ctr_dv = g_dv
    sil_sc = 0
    for s4 in range(NSL):
        c0 = OWN0 + 512 * s4
        pfx_dv = 0
        if s4 == 0:
            for mj in range(16):
                hp = None
                for kk in range(8):
                    hp = prog.emit("tensor", lambda t, mj=mj, kk=kk: t.matmul(
                        PS_A[0:128, 0:8],
                        Wsb[:, kk * ZD + 128 * mj: kk * ZD + 128 * mj + 128],
                        xn[:, kk * XT + OWN0 - 8: kk * XT + OWN0],
                        start=(kk == 0), stop=(kk == 7)),
                        waits=([(sW, w_all), (sSC, epre_sc)]
                               if (mj == 0 and kk == 0) else ()),
                        inc=sPE if kk == 7 else None)
                hsc = prog.emit("scalar", lambda s, mj=mj: s.activation(
                    xr[:, 520 * mj: 520 * mj + 8], PS_A[0:128, 0:8],
                    AF.Identity), waits=[(sPE, hp)], inc=sSC)
                prog.wait("tensor", sSC, hsc)
        else:
            prog.wait("vector", sGP, conv_gp)
            prog.wait("vector", sSC, None)
            prog.emit("vector", lambda v: v.tensor_copy(
                xr.rearrange("p (a b) -> p a b", a=16)[:, :, 0:8],
                xr.rearrange("p (a b) -> p a b", a=16)[:, :, 512:520]))
            prog.drain("vector")
            pfx_dv = prog.emit("vector", lambda v: v.nop(), inc=sDV)
        for mj in range(16):
            pb = PS_A if mj % 2 == 0 else PS_B
            for kk in range(8):
                prog.emit("tensor", lambda t, mj=mj, kk=kk, c0=c0, pb=pb:
                          t.matmul(
                              pb[:, 0:512],
                              Wsb[:, kk * ZD + 128 * mj:
                                  kk * ZD + 128 * mj + 128],
                              xn[:, kk * XT + c0: kk * XT + c0 + 512],
                              start=(kk == 0), stop=(kk == 7)),
                          waits=([(sSC, None), (sGP, conv_gp),
                                  (sDV, pfx_dv)]
                                 if kk == 0 else ()),
                          inc=sPE if kk == 7 else None)
            zp = sPE.n
            prog.emit("scalar", lambda s, mj=mj, pb=pb: s.activation(
                xr[:, 520 * mj + 8: 520 * mj + 520], pb[:, 0:512],
                AF.Identity), waits=[(sPE, zp)], inc=sSC)
        zx_sc = sSC.n
        for mj in range(8):
            pb = PS_A if mj % 2 == 0 else PS_B
            for kk in range(8):
                prog.emit("tensor", lambda t, mj=mj, kk=kk, c0=c0, pb=pb:
                          t.matmul(
                              pb[:, 0:512],
                              Wsb[:, kk * ZD + 2048 + 128 * mj:
                                  kk * ZD + 2048 + 128 * mj + 128],
                              xn[:, kk * XT + c0: kk * XT + c0 + 512],
                              start=(kk == 0), stop=(kk == 7)),
                          waits=([(sSC, None), (sSp, bds_sp), (sPE, bst_pe)]
                                 if kk == 0 else ()),
                          inc=sPE if kk == 7 else None)
            zp = sPE.n
            bev = prog.emit("scalar", lambda s, mj=mj, pb=pb: s.activation(
                bst[:, 512 * mj: 512 * mj + 512], pb[:, 0:512],
                AF.Identity), waits=[(sPE, zp)], inc=sSC)
            prog.emit("sync", lambda s, mj=mj, s4=s4: s.dma_start(
                out=Bfm_d[mj, :, 512 * s4: 512 * s4 + 512],
                in_=bst[:, 512 * mj: 512 * mj + 512]),
                waits=[(sSC, bev)], inc=sSp, inc_amt=16)
        cst_sp = [0, 0]
        for mj in range(8):
            pb = PS_A if mj % 2 == 0 else PS_B
            buf = mj % 2
            for kk in range(8):
                prog.emit("tensor", lambda t, mj=mj, kk=kk, c0=c0, pb=pb:
                          t.matmul(
                              pb[:, 0:512],
                              Wsb[:, kk * ZD + 3072 + 128 * mj:
                                  kk * ZD + 3072 + 128 * mj + 128],
                              xn[:, kk * XT + c0: kk * XT + c0 + 512],
                              start=(kk == 0), stop=(kk == 7)),
                          waits=([(sSC, None)] if kk == 0 else ()),
                          inc=sPE if kk == 7 else None)
            zp = sPE.n
            cev = prog.emit("scalar", lambda s, buf=buf, pb=pb: s.activation(
                cst[:, 512 * buf: 512 * buf + 512], pb[:, 0:512],
                AF.Identity), waits=[(sPE, zp), (sSp, cst_sp[buf])], inc=sSC)
            cst_sp[buf] = prog.emit(
                "sync", lambda s, mj=mj, s4=s4, buf=buf: s.dma_start(
                    out=Cfm_d[mj, :, 512 * s4: 512 * s4 + 512],
                    in_=cst[:, 512 * buf: 512 * buf + 512]),
                waits=[(sSC, cev)], inc=sSp, inc_amt=16)
        for cc in range(4):
            c = 4 * s4 + cc
            tc0 = 8 + 128 * cc
            prog.wait("vector", sSC, zx_sc)
            last_sil = sil_sc
            for fc in range(16):
                prog.emit("vector", lambda v, fc=fc, tc0=tc0: v.tensor_scalar(
                    cacc[:, :], xr[:, 520 * fc + tc0 - 3:
                                   520 * fc + tc0 + 125],
                    cwc[:, 4 * fc: 4 * fc + 1], None, ALU.mult),
                    waits=[(sSC, last_sil), (sPE, sil_pe)])
                for tap in range(1, 4):
                    prog.drain("vector")
                    prog.emit("vector", lambda v, fc=fc, tc0=tc0, tap=tap:
                              v.scalar_tensor_tensor(
                                  cacc[:, :],
                                  xr[:, 520 * fc + tc0 - 3 + tap:
                                     520 * fc + tc0 + 125 + tap],
                                  cwc[:, 4 * fc + tap: 4 * fc + tap + 1],
                                  cacc[:, :], ALU.mult, ALU.add))
                prog.drain("vector")
                cg = prog.emit("vector", lambda v: v.nop(), inc=sGP)
                last_sil = prog.emit("scalar", lambda s, fc=fc: s.activation(
                    sil[:, 128 * fc: 128 * fc + 128], cacc[:, :], AF.Silu,
                    bias=cbc[:, fc: fc + 1]), waits=[(sGP, cg)], inc=sSC)
            conv_gp = sGP.n
            sil_sc = last_sil
            for wv in range(2):
                tpw = None
                for sl in range(8):
                    fc = 8 * wv + sl
                    tpw = prog.emit("tensor", lambda t, fc=fc, sl=sl:
                                    t.transpose(
                        PS_T[:, 128 * sl: 128 * sl + 128],
                        sil[:, 128 * fc: 128 * fc + 128], idb[:, :]),
                        waits=([(sSC, sil_sc), (sDV, xt_dv), (sDV, bd_dv)]
                               if sl == 0 else ()),
                        inc=sPE if sl == 7 else None)
                prog.wait("vector", sPE, tpw)
                prog.wait("vector", sSp, xts_sp)
                for sl in range(8):
                    fc = 8 * wv + sl
                    prog.emit("vector", lambda v, fc=fc, sl=sl: v.tensor_copy(
                        xts[:, 128 * fc: 128 * fc + 128],
                        PS_T[:, 128 * sl: 128 * sl + 128]))
                prog.drain("vector")
                xt_dv = prog.emit("vector", lambda v: v.nop(), inc=sDV)
            sil_pe = sPE.n
            xts_sp = prog.emit("sync", lambda s, c=c: s.dma_start(
                out=Xtm_d[c, :, :], in_=xts[:, :]),
                waits=[(sDV, xt_dv)], inc=sSp, inc_amt=16)
            tpb = None
            for mj in range(8):
                tpb = prog.emit("tensor", lambda t, mj=mj, cc=cc: t.transpose(
                    PS_T[:, 128 * mj: 128 * mj + 128],
                    bst[:, 512 * mj + 128 * cc: 512 * mj + 128 * cc + 128],
                    idb[:, :]),
                    waits=([(sDV, xt_dv), (sDV, bd_dv)] if mj == 0 else ()),
                    inc=sPE if mj == 7 else None)
            bst_pe = tpb
            prog.wait("vector", sPE, tpb)
            prog.wait("vector", sSp, bds_sp)
            for mj in range(8):
                for hh in range(2):
                    h = 2 * mj + hh
                    prog.emit("vector", lambda v, mj=mj, hh=hh, h=h, c=c:
                              v.tensor_scalar(
                                  bds[:, 64 * h: 64 * h + 64],
                                  PS_T[:, 128 * mj + 64 * hh:
                                       128 * mj + 64 * hh + 64],
                                  ebd[:, 16 * c + h: 16 * c + h + 1],
                                  None, ALU.mult))
            prog.drain("vector")
            bd_dv = prog.emit("vector", lambda v: v.nop(), inc=sDV)
            bds_sp = prog.emit("sync", lambda s, c=c: s.dma_start(
                out=Btm_d[c, :, :], in_=bds[:, :]),
                waits=[(sDV, bd_dv)], inc=sSp, inc_amt=16)
            stt_dv = {}
            for h in range(NH):
                pg = PS_U if h % 2 == 0 else PS_S
                cm = prog.emit("tensor", lambda t, h=h, pg=pg: t.matmul(
                    pg[0:64, 0:128], bds[:, 64 * h: 64 * h + 64],
                    xts[:, 128 * h: 128 * h + 128], start=True, stop=True),
                    waits=([(sDV, bd_dv), (sDV, ctr_dv)] if h <= 1 else
                           [(sDV, stt_dv[h - 2])]), inc=sPE)
                stt_dv[h] = prog.emit(
                    "vector", lambda v, h=h, c=c, pg=pg:
                    v.scalar_tensor_tensor(
                        Gb[:, 128 * h: 128 * h + 128],
                        Gb[:, 128 * h: 128 * h + 128],
                        ela[:, 16 * c + h: 16 * c + h + 1],
                        pg[0:64, 0:128], ALU.mult, ALU.add),
                    waits=[(sPE, cm)], inc=sDV)
                prog.drain("vector")
            ctr_dv = stt_dv[NH - 1]
    prog.emit("sync", lambda s: s.dma_start(out=G_d, in_=Gb[:, :]),
              waits=[(sDV, ctr_dv)], inc=sSp, inc_amt=16)
    prog.emit("sync", lambda s: s.dma_start(
        out=xn_d.rearrange("(a p) t -> p a t", p=128),
        in_=xn.rearrange("p (a b) -> p a b", a=8)), inc=sSp, inc_amt=16)
    m1_done = dict(sp=sSp.n)

    prog.emit("sync", lambda s: s.dma_start(
        out=lsrc_d, in_=lam[:, TOK:TOK + 1]),
        waits=[(sDV, lam_dv)], inc=sSp, inc_amt=16)
    prog.wait("gpsimd", sOut)
    prog.wait("gpsimd", sGP)
    prog.drain("gpsimd")
    ln_gp = prog.emit("gpsimd", lambda g: g.nop(), inc=sGP)
    prog.emit("gpsimd", lambda g: g.collective_compute(
        "AllGather", ALU.bypass, replica_groups=[list(range(8))],
        ins=[G_d], outs=[gall_d]), waits=[(sSp, None)], inc=sCC)
    prog.emit("gpsimd", lambda g: g.collective_compute(
        "AllGather", ALU.bypass, replica_groups=[list(range(8))],
        ins=[lsrc_d], outs=[lall_d]), inc=sCC)
    for _e in ("vector", "scalar", "tensor", "gpsimd", "sync"):
        for _s in (sPE, sSC, sDV, sGP, sW, sU, sSp, sOut):
            prog.wait(_e, _s)

    lam_sp = lam_d
    xn_sp = xn_d
    LM = Layout(pool)
    LM.cur = HDR
    Wg = LM.add("Wg", 128, 8 * DI, BF16)
    Wo2 = LM.add("Wo2", 128, 16 * 1024, BF16)
    Yb = LM.add("Yb", 128, 16 * 512, F32)
    gate = LM.add("gate", 128, 16 * 512, BF16)
    xnsl = LM.add("xnsl", 128, 8 * 512, BF16)
    lam = LM.add("lam", 16, 2056, F32)
    lamrow = LM.add("lamrow", 1, 2064, F32)
    relrow = LM.add("relrow", 1, 2048, F32)
    stagA = LM.add("stagA", 2, 2048, F32)
    stagB = LM.add("stagB", 2, 2048, F32)
    Xc = LM.add("Xc", 128, 2 * 2048, BF16)
    Bc = LM.add("Bc", 128, 2 * 1024, BF16)
    Cc = LM.add("Cc", 128, 2 * 1024, BF16)
    Bdc = LM.add("Bdc", 128, 2 * 1024, BF16)
    Hb = LM.add("Hb", 64, NH * DH, F32)
    Hbf = LM.add("Hbf", 64, NH * DH, BF16)
    ela2 = LM.add("ela2", 64, 16, F32)
    ecol = LM.add("ecol", 64, 48, F32)
    Dm = LM.add("Dm", 128, 128, F32)
    Sdb = LM.add("Sdb", 128, 128, BF16)
    Cdb = LM.add("Cdb", 64, 128, BF16)
    iota2 = LM.add("iota2", 128, 2, F32)
    dmask = LM.add("dmask", 128, 128, BF16)
    lamf = LM.add("lamf", 1, 48, F32)
    nwc = LM.add("nwc", 128, 16, F32)
    rrow = LM.add("rrow", 1, 512, F32)
    cslab2 = pv(pool, LM.map["stagA"][0], 128, 8 * 512, F32)
    Gsl = pv(pool, LM.map["stagA"][0], 64, NH * DH, F32)
    assert LM.cur <= POOL_COLS * 2, LM.cur
    lam8 = LM.add("lam8", 1, 128, F32)
    elam8 = LM.add("elam8", 1, 128, F32)
    ohvc = LM.add("ohvc", 64, 8, F32)
    assert LM.cur <= POOL_COLS * 2, LM.cur
    Sfold = pv(pool, LM.map["stagB"][0], 64, NH * DH, F32)
    Gld = pv(pool, LM.map["stagA"][0], 64, NH * DH, F32)

    prog.emit("vector", lambda v: v.memset(relrow[0:1, :], 1.0))
    prog.drain("vector")
    rr_dv = prog.emit("vector", lambda v: v.nop(), inc=sDV)
    prog.emit("sync", lambda s: s.dma_start(out=d3[0:1, :],
              in_=relrow[0:1, 0:2048]), waits=[(sDV, rr_dv)],
              inc=sSp, inc_amt=16)
    prog.emit("sync", lambda s: s.dma_start(out=d3[2:3, :],
              in_=relrow[0:1, 0:2048]), inc=sSp, inc_amt=16)
    prog.emit("gpsimd", lambda g: g.dma_start(out=dmask[:, :], in_=dmask_in),
              inc=sW, inc_amt=16)
    prog.emit("sync", lambda s: s.dma_start(out=nwc[:, :], in_=nw_w),
              inc=sU, inc_amt=16)
    prog.emit("sync", lambda s: s.dma_start(
        out=lam[:, 0:NCH * 128 + 1], in_=lam_sp), inc=sU, inc_amt=16)
    for kk in range(8):
        prog.emit("gpsimd", lambda g, kk=kk: g.dma_start(
            out=Wg[:, kk * DI:(kk + 1) * DI],
            in_=Wg_w[128 * kk:128 * (kk + 1), :]), inc=sW, inc_amt=16)
    for kk in range(16):
        prog.emit("gpsimd", lambda g, kk=kk: g.dma_start(
            out=Wo2[:, kk * 1024:(kk + 1) * 1024],
            in_=Wo2_w[128 * kk:128 * (kk + 1), :]), inc=sW, inc_amt=16)

    prog.emit("sync", lambda s: s.dma_start(
        out=lam8[:, :], in_=lall_d.rearrange("a b -> (a b)")[None, :]),
        waits=[(sCC, 2)], inc=sU, inc_amt=16)
    prog.emit("sync", lambda s: s.dma_start(out=ohvc[:, :], in_=ohv_in),
              inc=sU, inc_amt=16)
    ex8 = prog.emit("scalar", lambda s: s.activation(
        elam8[:, :], lam8[:, :], AF.Exp), waits=[(sU, None)], inc=sSC)
    prog.emit("vector", lambda v: v.memset(Hb[:, :], 0.0))
    prog.drain("vector")
    prev_dv = prog.emit("vector", lambda v: v.nop(), inc=sDV)
    for j in range(8):
        gl = prog.emit("sync", lambda s, j=j: s.dma_start(
            out=Gld[:, :], in_=gall_d[j]),
            waits=[(sCC, 1), (sDV, prev_dv)], inc=sU, inc_amt=16)
        if j in (0, 4):
            prog.wait("vector", sU, gl)
            prog.emit("vector", lambda v: v.tensor_copy(
                Sfold[:, :], Gld[:, :]))
            prog.drain("vector")
        else:
            emm = None
            for q in range(4):
                row = elam8[0:1, 16 * j + 4 * q:16 * j + 4 * q + 4]
                rep = bass.AP(row.tensor, row.offset,
                              [list(row.ap)[0], [1, 4], [0, 128]])
                emm = prog.emit("tensor", lambda t, q=q, rep=rep: t.matmul(
                    PS_S[0:64, 512 * q:512 * q + 512], onef[0:1, 0:64], rep,
                    start=True, stop=True),
                    waits=([(sSC, ex8), (sDV, prev_dv)] if q == 0 else ()),
                    inc=sPE if q == 3 else None)
            prog.wait("vector", sPE, emm)
            prog.emit("vector", lambda v: v.tensor_tensor(
                Sfold[:, :], Sfold[:, :], PS_S[0:64, 0:2048], ALU.mult))
            prog.drain("vector")
            prog.wait("vector", sU, gl)
            prog.emit("vector", lambda v: v.tensor_tensor(
                Sfold[:, :], Sfold[:, :], Gld[:, :], ALU.add))
            prog.drain("vector")
        prev_dv = prog.emit("vector", lambda v, j=j: v.scalar_tensor_tensor(
            Hb[:, :], Sfold[:, :], ohvc[:, j:j + 1], Hb[:, :],
            ALU.mult, ALU.add), inc=sDV)
    prog.wait("vector", sDV, prev_dv)
    prog.emit("vector", lambda v: v.tensor_copy(Hbf[:, :], Hb[:, :]))
    prog.drain("vector")
    hdv = prog.emit("vector", lambda v: v.nop(), inc=sDV)

    prog.wait("vector", sW)
    acc_gp = ln_gp
    hup_dv = [hdv] * NH
    P2_sd = [0]
    P2_y = [0, 0]
    P2_prd = [(0, 0), (0, 0)]
    yev_sc = 0
    epi_dv = hdv
    epi_sc = 0
    ld_done = [0, 0]
    st_dv = hdv
    for c in range(NCH):
        buf = c % 2
        for (dst, dsrc) in (
                (Xc[:, 2048 * buf:2048 * buf + 2048], Xtm_d[c]),
                (Bdc[:, 1024 * buf:1024 * buf + 1024], Btm_d[c]),
                (Bc[:, 1024 * buf:1024 * buf + 1024],
                 Bfm_d[:, :, 128 * c:128 * c + 128].transpose([1, 0, 2])),
                (Cc[:, 1024 * buf:1024 * buf + 1024],
                 Cfm_d[:, :, 128 * c:128 * c + 128].transpose([1, 0, 2]))):
            prog.emit("sync", lambda s, d=dst, r=dsrc: s.dma_start(
                out=d if len(r.shape) == 2 else
                d.rearrange("p (a b) -> p a b", a=8), in_=r),
                waits=[(sPE, ld_done[buf]), (sDV, st_dv)],
                inc=sU, inc_amt=16)
        cld = sU.n
        prog.emit("sync", lambda s, c=c: s.dma_start(
            out=d_lr, in_=lam[:, 128 * c:128 * c + 129]),
            waits=[(sDV, st_dv), (sSC, epi_sc)], inc=sSp, inc_amt=16)
        prog.emit("sync", lambda s: s.dma_start(
            out=lamrow[0:1, 0:2064],
            in_=d_lr.rearrange("a b -> (a b)")[None, :]),
            inc=sSp, inc_amt=16)
        prog.wait("vector", sSp, None)
        prog.wait("vector", sPE, None)
        rel_ap_in = bass.AP(lamrow.tensor, lamrow.offset + 1,
                            [list(lamrow.ap)[0], [129, 16], [1, 128]])
        off_ap = bass.AP(lamrow.tensor, lamrow.offset,
                         [list(lamrow.ap)[0], [129, 16], [0, 128]])
        rl = prog.emit("vector", lambda v, rel_ap_in=rel_ap_in, off_ap=off_ap:
                       v.tensor_tensor(
            relrow[0:1, 0:2048].rearrange("p (a b) -> p a b", a=16),
            rel_ap_in, off_ap, ALU.subtract), inc=sDV)
        prog.emit("sync", lambda s: s.dma_start(
            out=d3[1:2, :], in_=relrow[0:1, 0:2048]),
            waits=[(sDV, rl)], inc=sSp, inc_amt=16)
        ex_sc = prog.emit("scalar", lambda s: s.activation(
            relrow[0:1, 0:2048], relrow[0:1, 0:2048], AF.Exp),
            waits=[(sSp, None), (sDV, rl)], inc=sSC)
        sA = prog.emit("sync", lambda s: s.dma_start(
            out=stagA[0:2, :], in_=d3[1:3, :]),
            waits=[(sDV, st_dv), (sOut, None), (sGP, acc_gp), (sPE, None)],
            inc=sSp, inc_amt=16)
        sB = prog.emit("sync", lambda s: s.dma_start(
            out=stagB[0:2, :], in_=d3[0:2, :]), inc=sSp, inc_amt=16)
        prog.wait("vector", sSp, None)
        ng = prog.emit("vector", lambda v: v.tensor_scalar_mul(
            stagA[0:1, :], stagA[0:1, :], -1.0), inc=sDV)
        erp_ap = bass.AP(relrow.tensor, relrow.offset + 127,
                         [list(relrow.ap)[0], [128, 16]])
        em = prog.emit("tensor", lambda t, erp_ap=erp_ap: t.matmul(
            PS_U[0:64, 64:80], onef[0:1, 0:64], erp_ap,
            start=True, stop=True),
            waits=[(sSC, ex_sc), (sSC, epi_sc), (sDV, None)], inc=sPE)
        ela_sc = prog.emit("scalar", lambda s: s.activation(
            ela2[:, :], PS_U[0:64, 64:80], AF.Identity),
            waits=[(sPE, em)], inc=sSC)
        prog.wait("tensor", sU, cld)
        prog.wait("tensor", sSC, ela_sc)
        prog.wait("tensor", sDV, ng)
        prog.wait("tensor", sDV, epi_dv)
        import os as _os
        m2s = _os.environ.get("M2_STAGE", "all")
        if m2s == "stream":
            st_dv = sDV.n
            continue
        for h in range(NH):
            pr = (c * NH + h) % 2
            sb = 512 * pr
            pY = PS_U if pr == 0 else PS_B
            prog.wait("tensor", sDV, P2_prd[pr][0])
            prog.wait("tensor", sSC, P2_prd[pr][1])
            sm = prog.emit("tensor", lambda t, h=h, buf=buf, sb=sb: t.matmul(
                PS_S[:, sb + 0: sb + 128],
                Bc[64 * (h % 2): 64 * (h % 2) + 64,
                   1024 * buf + 128 * (h // 2): 1024 * buf + 128 * (h // 2) + 128],
                Cc[64 * (h % 2): 64 * (h % 2) + 64,
                   1024 * buf + 128 * (h // 2): 1024 * buf + 128 * (h // 2) + 128],
                start=True, stop=True), inc=sPE)
            dm_ = prog.emit("tensor", lambda t, h=h, sb=sb: t.matmul(
                PS_S[:, sb + 128: sb + 256], stagA[0:2, 128 * h:128 * h + 128],
                stagB[0:2, 128 * h:128 * h + 128], start=True, stop=True),
                inc=sPE)
            tm = prog.emit("vector", lambda v, sb=sb: v.tensor_tensor(
                PS_S[:, sb + 128: sb + 256], PS_S[:, sb + 128: sb + 256],
                dmask[:, :], ALU.add), waits=[(sPE, dm_)], inc=sDV)
            exd = prog.emit("scalar", lambda s, sb=sb: s.activation(
                Dm[:, :], PS_S[:, sb + 128: sb + 256], AF.Exp),
                waits=[(sDV, tm), (sDV, P2_sd[0])], inc=sSC)
            sd = prog.emit("vector", lambda v, sb=sb: v.tensor_tensor(
                Sdb[:, :], PS_S[:, sb + 0: sb + 128], Dm[:, :], ALU.mult),
                waits=[(sSC, exd), (sPE, sm), (sPE, P2_y[0])], inc=sDV)
            P2_sd[0] = sd
            if m2s == "sd":
                P2_prd[pr] = (sd, exd)
                continue
            cb = prog.emit("tensor", lambda t, h=h, sb=sb: t.matmul(
                PS_S[:, sb + 256: sb + 384][0:64, :], onef[0:1, 0:64],
                relrow[0:1, 128 * h:128 * h + 128], start=True, stop=True),
                waits=[(sDV, sd)], inc=sPE)
            cd = prog.emit("vector", lambda v, h=h, buf=buf, sb=sb:
                           v.tensor_tensor(
                Cdb[:, :],
                Cc[64 * (h % 2): 64 * (h % 2) + 64,
                   1024 * buf + 128 * (h // 2): 1024 * buf + 128 * (h // 2) + 128],
                PS_S[:, sb + 256: sb + 384][0:64, :], ALU.mult),
                waits=[(sPE, cb), (sPE, P2_y[1])], inc=sDV)
            y1 = prog.emit("tensor", lambda t, h=h, buf=buf, pY=pY: t.matmul(
                pY[:, 0:128], Xc[:, 2048 * buf + 128 * h:
                                 2048 * buf + 128 * h + 128],
                Sdb[:, :], start=True, stop=False),
                waits=[(sDV, sd), (sDV, hup_dv[h]), (sSC, yev_sc),
                       (sDV, epi_dv)], inc=sPE)
            y2 = prog.emit("tensor", lambda t, h=h, pY=pY: t.matmul(
                pY[:, 0:128], Hbf[0:64, 128 * h:128 * h + 128], Cdb[:, :],
                start=False, stop=True), waits=[(sDV, cd)], inc=sPE)
            P2_y[0] = y1
            P2_y[1] = y2
            P2_prd[pr] = (hup_dv[h], exd)
            if m2s == "y":
                P2_prd[pr] = (cd, exd)
                yev_sc = prog.emit("scalar", lambda s, h=h, c=c, pY=pY:
                                   s.activation(
                    Yb[:, 512 * h + 128 * (c % 4):
                       512 * h + 128 * (c % 4) + 128],
                    pY[:, 0:128], AF.Identity), waits=[(sPE, y2)], inc=sSC)
                continue
            gm = prog.emit("tensor", lambda t, h=h, buf=buf, sb=sb: t.matmul(
                PS_S[0:64, sb + 384: sb + 512],
                Bdc[:, 1024 * buf + 64 * h: 1024 * buf + 64 * h + 64],
                Xc[:, 2048 * buf + 128 * h: 2048 * buf + 128 * h + 128],
                start=True, stop=True), inc=sPE)
            if m2s == "g":
                P2_prd[pr] = (cd, exd)
                hup_dv[h] = prog.emit("vector", lambda v, sb=sb:
                                      v.tensor_copy(
                    Dm[:, 0:64], PS_S[0:64, sb + 384: sb + 448]),
                    waits=[(sPE, gm)], inc=sDV)
                yev_sc = prog.emit("scalar", lambda s, h=h, c=c, pY=pY:
                                   s.activation(
                    Yb[:, 512 * h + 128 * (c % 4):
                       512 * h + 128 * (c % 4) + 128],
                    pY[:, 0:128], AF.Identity), waits=[(sPE, gm)], inc=sSC)
                continue
            prog.emit(
                "vector", lambda v, h=h, sb=sb: v.scalar_tensor_tensor(
                    Hb[:, 128 * h: 128 * h + 128],
                    Hb[:, 128 * h: 128 * h + 128],
                    ela2[:, h: h + 1], PS_S[0:64, sb + 384: sb + 512],
                    ALU.mult, ALU.add), waits=[(sPE, gm)])
            prog.drain("vector")
            hup_dv[h] = prog.emit("vector", lambda v, h=h: v.tensor_copy(
                Hbf[:, 128 * h: 128 * h + 128],
                Hb[:, 128 * h: 128 * h + 128]), inc=sDV)
            prog.drain("vector")
            yev_sc = prog.emit("scalar", lambda s, h=h, c=c, pY=pY:
                               s.activation(
                Yb[:, 512 * h + 128 * (c % 4): 512 * h + 128 * (c % 4) + 128],
                pY[:, 0:128], AF.Identity), waits=[(sPE, gm)], inc=sSC)
        ld_done[buf] = sPE.n
        st_dv = sDV.n
        if c % 4 == 3 and m2s not in ("units",):
            s4 = c // 4
            if DEV_DEBUG and s4 == 0:
                dbg_Y = nc.dram_tensor("dbg_Y", [128, 16 * 512], F32,
                                       kind="ExternalOutput").ap()
                prog.emit("sync", lambda s: s.dma_start(
                    out=dbg_Y, in_=Yb[:, :]), waits=[(sSC, yev_sc)],
                    inc=sSp, inc_amt=16)
            xl = prog.emit("sync", lambda s, s4=s4: s.dma_start(
                out=xnsl.rearrange("p (a b) -> p a b", a=8),
                in_=_tslice(xn_sp, OWN0 + 512 * s4, 512)),
                waits=[(sPE, None)], inc=sU, inc_amt=16)
            gev = 0
            for mj in range(16):
                pb = PS_A if mj % 2 == 0 else PS_B
                gl_ = None
                for kk in range(8):
                    gl_ = prog.emit("tensor",
                                    lambda t, mj=mj, kk=kk, pb=pb: t.matmul(
                        pb[:, 0:512],
                        Wg[:, kk * DI + 128 * mj: kk * DI + 128 * mj + 128],
                        xnsl[:, kk * 512: kk * 512 + 512],
                        start=(kk == 0), stop=(kk == 7)),
                        waits=([(sU, None), (sW, None), (sSC, yev_sc),
                                (sSC, gev), (sDV, None)] if kk == 0 else ()),
                        inc=sPE if kk == 7 else None)
                gev = prog.emit("scalar", lambda s, mj=mj, pb=pb:
                                s.activation(
                    gate[:, 512 * mj: 512 * mj + 512], pb[:, 0:512],
                    AF.Sigmoid), waits=[(sPE, gl_)], inc=sSC)
            prog.wait("vector", sSC, gev)
            prog.wait("vector", sSp)
            for mj in range(16):
                prog.emit("vector", lambda v, mj=mj: v.tensor_mul(
                    Yb[:, 512 * mj: 512 * mj + 512],
                    Yb[:, 512 * mj: 512 * mj + 512],
                    gate[:, 512 * mj: 512 * mj + 512]))
            prog.drain("vector")
            yg_dv = prog.emit("vector", lambda v: v.nop(), inc=sDV)
            sqs = prog.emit("scalar", lambda s: [s.activation(
                gate[:, 512 * mj: 512 * mj + 512],
                Yb[:, 512 * mj: 512 * mj + 512], AF.Square)
                for mj in range(16)][-1], waits=[(sDV, yg_dv)], inc=sSC)
            sq_mm = None
            for mj in range(16):
                sq_mm = prog.emit("tensor", lambda t, mj=mj: t.matmul(
                    PS_U[0:1, 0:512], oneb[:, 0:1],
                    gate[:, 512 * mj: 512 * mj + 512],
                    start=(mj == 0), stop=(mj == 15)),
                    waits=([(sSC, sqs)] if mj == 0 else ()),
                    inc=sPE if mj == 15 else None)
            rs = prog.emit("scalar", lambda s: s.activation(
                rrow[0:1, 0:512], PS_U[0:1, 0:512], AF.Sqrt,
                scale=1.0 / DI, bias=RMS_EPS), waits=[(sPE, sq_mm)], inc=sSC)
            prog.wait("vector", sSC, rs)
            prog.emit("vector", lambda v: v.reciprocal(
                rrow[0:1, 0:512], rrow[0:1, 0:512]))
            prog.drain("vector")
            rdv = prog.emit("vector", lambda v: v.nop(), inc=sDV)
            rb = prog.emit("tensor", lambda t: t.matmul(
                PS_S[:, 0:512], onef[0:1, :], rrow[0:1, 0:512],
                start=True, stop=True),
                waits=[(sDV, rdv), (sDV, None), (sSC, None)], inc=sPE)
            prog.wait("vector", sPE, rb)
            prog.wait("vector", sSC, sqs)
            yn_dv = None
            for mj in range(16):
                yn_dv = prog.emit("vector", lambda v, mj=mj:
                                  v.scalar_tensor_tensor(
                    gate[:, 512 * mj: 512 * mj + 512],
                    Yb[:, 512 * mj: 512 * mj + 512],
                    nwc[:, mj: mj + 1], PS_S[:, 0:512],
                    ALU.mult, ALU.mult), inc=sDV)
            prog.drain("vector")
            yn_dv = prog.emit("vector", lambda v: v.nop(), inc=sDV)
            wev = 0
            for m in range(8):
                pb = PS_A if m % 2 == 0 else PS_B
                wm = None
                for kk in range(16):
                    wm = prog.emit("tensor", lambda t, kk=kk, m=m, pb=pb:
                                   t.matmul(
                        pb[:, 0:512],
                        Wo2[:, kk * 1024 + 128 * m: kk * 1024 + 128 * m + 128],
                        gate[:, 512 * kk: 512 * kk + 512],
                        start=(kk == 0), stop=(kk == 15)),
                        waits=([(sDV, yn_dv), (sSC, wev), (sSp, None)]
                               if kk == 0 else ()),
                        inc=sPE if kk == 15 else None)
                wev = prog.emit("scalar", lambda s, m=m, pb=pb: s.activation(
                    cslab2[:, 512 * m: 512 * m + 512], pb[:, 0:512],
                    AF.Identity), waits=[(sPE, wm), (sOut, None),
                                          (sGP, acc_gp)], inc=sSC)
            if DEV_DEBUG and s4 == 0:
                dbg_c2 = nc.dram_tensor("dbg_c2", [128, 8 * 512], F32,
                                        kind="ExternalOutput").ap()
                prog.emit("sync", lambda s: s.dma_start(
                    out=dbg_c2, in_=cslab2[:, :]), waits=[(sSC, wev)],
                    inc=sSp, inc_amt=16)
            acc_gp = emit_accum(
                prog, sGP, _tslice(out_d, 512 * s4, 512),
                cslab2.rearrange("p (a b) -> p a b", a=8),
                ([(sSp, None)] if (DEV_DEBUG and s4 == 0) else [])
                + [(sSC, wev), (sGP, acc_gp)], sOut=sOut)
            epi_dv = sDV.n
            epi_sc = sSC.n

    if FUSED_DEBUG:
        dbg_o1 = dt_out("dbg_o1", [D, TOK])
        prog.wait("sync", sGP)
        prog.wait("sync", sOut)
        prog.emit("sync", lambda s: s.dma_start(out=dbg_o1, in_=out_d),
                  inc=sSp, inc_amt=16)
        prog.wait("gpsimd", sSp)
    for _s in (sPE, sSC, sDV, sGP, sOut):
        prog.wait("sync", _s)
    xnr = prog.emit("sync", lambda s: s.dma_start(
        out=xn.rearrange("p (a b) -> p a b", a=8),
        in_=xn_d.rearrange("(a p) t -> p a t", p=128)), inc=sU, inc_amt=16)
    if FUSED_DEBUG:
        dbg_xn = dt_out("dbg_xn", [128, 8 * XT], BF16)
        prog.emit("sync", lambda s: s.dma_start(
            out=dbg_xn, in_=xn[:, :]), waits=[(sU, xnr)],
            inc=sSp, inc_amt=16)
    KT = TOK + 640
    LA = Layout(pool)
    LA.cur = HDR
    cslabF = LA.add("cslabF", 128, 8 * 512, F32)
    wqkv_sb = LA.add("wqkv_sb", 128, 8 * 1536, BF16)
    wo_sb = LA.add("wo_sb", 128, 4 * 1024, BF16)
    tbs = [LA.add(f"tb{i}", 128, NKB * 64, BF16) for i in range(4)]
    masksF = LA.add("masksF", 128, 6 * 640, BF16)
    qsrow = LA.add("qsrow", 1, 64, F32)
    ksrow = LA.add("ksrow", 1, 64, F32)
    csrowF = LA.add("csrowF", 1, 256, F32)
    assert LA.cur <= X1_AT, LA.cur
    x1 = pv(pool, X1_AT, 128, 8 * (TOK + 128), BF16)
    LB = Layout(pool)
    LB.cur = RB_AT
    kfm = LB.add("kfm", 128, 4 * KT, BF16)
    Vb = LB.add("Vb", 128, 6 * 512, BF16)
    Oring = LB.add("Oring", 128, 6 * 512, BF16)
    qsl = LB.add("qsl", 128, 2 * 512, BF16)
    kprep = LB.add("kprep", 128, 512, F32)
    qprep = LB.add("qprep", 128, 512, F32)
    rexpF = LB.add("rexpF", 128, 512, F32)
    rot1F = LB.add("rot1F", 128, 512, F32)
    sqbF = LB.add("sqbF", 128, 512, BF16)
    nrmF = LB.add("nrmF", 128, 16, F32)
    AparF = LB.add("AparF", 128, 2 * 640, BF16)
    osbF = LB.add("osbF", 128, 256, F32)
    assert LB.cur <= XN_AT, LB.cur

    for e in ("gpsimd", "sync"):
        for sm in (sPE, sSC, sDV, sGP):
            prog.wait(e, sm)
    for kk in range(8):
        prog.emit("gpsimd", lambda g, kk=kk, _w=wqkv_w, _d=wqkv_sb:
                  g.dma_start(
            out=_d[:, kk * 1536:(kk + 1) * 1536],
            in_=_w[128 * kk:128 * (kk + 1), :]), inc=sW, inc_amt=16)
    for kk in range(4):
        prog.emit("gpsimd", lambda g, kk=kk, _w=wo_w, _d=wo_sb:
                  g.dma_start(
            out=_d[:, kk * 1024:(kk + 1) * 1024],
            in_=_w[128 * kk:128 * (kk + 1), :]), inc=sW, inc_amt=16)
    for ti in range(4):
        prog.emit("gpsimd", lambda g, ti=ti, _d=tbs[ti]: g.dma_start(
            out=_d.rearrange("p (a b) -> p a b", a=NKB),
            in_=rotab.rearrange("q (a p) b -> q p a b", p=128)[ti]),
            inc=sW, inc_amt=16)
    prog.emit("gpsimd", lambda g, _d=masksF: g.dma_start(
        out=_d.rearrange("p (a b) -> p a b", a=6),
        in_=masks_in.transpose([1, 0, 2])), inc=sW, inc_amt=16)
    prog.emit("sync", lambda s, _w=qs_in, _d=qsrow: s.dma_start(
        out=_d[0:1, :], in_=_w), inc=sU, inc_amt=16)
    prog.emit("sync", lambda s, _w=ks_in, _d=ksrow: s.dma_start(
        out=_d[0:1, :], in_=_w), inc=sU, inc_amt=16)

    prog.wait("gpsimd", sOut)
    prog.drain("gpsimd")
    ln_out_gp = prog.emit("gpsimd", lambda g: g.nop(), inc=sGP)
    P = dict(nc=nc, prog=prog, sPE=sPE, sDV=sDV, sSC=sSC, sGP=sGP, sW=sW,
             sU=sU, sSp=sSp, sOut=sOut, PS_A=PS_A, PS_B=PS_B, PS_S=PS_S, PS_T=PS_T,
             PS_U=PS_U, src=xn, xout=x1, wqkv_sb=wqkv_sb, wo_sb=wo_sb,
             tb=tbs, masks=masksF, qsrow=qsrow, ksrow=ksrow, kfm=kfm, Vb=Vb,
             Oring=Oring, qsl=qsl, kprep=kprep, qprep=qprep, rexp=rexpF,
             rot1=rot1F, sqb=sqbF, nrm=nrmF, Apar=AparF, csrow=csrowF,
             cslab=cslabF, out_d=out_d, oneb=oneb, onef=onef,
             idb=idb, osb=osbF)
    P["_acc_gp"] = ln_out_gp
    import os as _os
    P["stage"] = _os.environ.get("F1A_STAGE", "all")
    emit_attn(P)
    if DEV_DEBUG or FUSED_DEBUG:
        dbg_x1 = dt_out("dbg_x1", [D, TOK + 128], BF16)
        dbg_kfm = dt_out("dbg_kfm", [128, 4 * KT], BF16)
        dbg_q = dt_out("dbg_q", [128, 1024], BF16)
        prog.emit("sync", lambda s: s.dma_start(
            out=dbg_x1.rearrange("(a p) t -> p a t", p=128),
            in_=x1.rearrange("p (a b) -> p a b", a=8)),
            waits=[(sDV, None)], inc=sSp, inc_amt=16)
        prog.wait("vector", sSp)
        prog.emit("sync", lambda s: s.dma_start(out=dbg_kfm, in_=kfm[:, :]),
                  inc=sSp, inc_amt=16)
        prog.emit("sync", lambda s: s.dma_start(out=dbg_q, in_=qsl[:, :]),
                  inc=sSp, inc_amt=16)

    LM2 = Layout(pool)
    LM2.cur = LA.map["wqkv_sb"][0]
    hlnF = LM2.add("hlnF", 128, 8 * 512, BF16)
    sq2F = LM2.add("sq2F", 128, 8 * 512, BF16)
    geluF = LM2.add("geluF", 128, 512, F32)
    mrowF = LM2.add("mrowF", 1, 512, F32)
    mrow2F = LM2.add("mrow2F", 1, 512, F32)
    b1cF = LM2.add("b1cF", 128, 44, F32)
    b2cF = LM2.add("b2cF", 128, 8, F32)
    lngF = LM2.add("lngF", 128, 8, F32)
    lnbF = LM2.add("lnbF", 128, 8, F32)
    actF = LM2.add("actF", 128, 22 * 512, BF16)
    assert LM2.cur <= X1_AT, LM2.cur
    w1p = pv(pool, RB_AT, 128, 8 * 5632, BF16)
    w2p = pv(pool, RB_AT, 128, 22 * 1024, BF16)

    for kk in range(8):
        prog.emit("gpsimd", lambda g, kk=kk, _w=w1_w, _d=w1p: g.dma_start(
            out=_d[:, kk * 5632:(kk + 1) * 5632],
            in_=_w[128 * kk:128 * (kk + 1), :]),
            waits=([(sPE, None), (sSC, None), (sDV, None), (sSp, None)]
                   if kk == 0 else ()), inc=sW, inc_amt=16)
    for (dst, srcd) in ((b1cF, b1_w), (b2cF, b2_w), (lngF, lng_w),
                        (lnbF, lnb_w)):
        prog.emit("sync", lambda s, d=dst, r=srcd: s.dma_start(
            out=d[:, :], in_=r),
            waits=[(sPE, None), (sSC, None), (sDV, None)],
            inc=sU, inc_amt=16)
    P2 = dict(P)
    P2.update(xio=x1, w1p=w1p, w2p=w2p, b1c=b1cF, b2c=b2cF, lngc=lngF,
              lnbc=lnbF, hln=hlnF, sq2=sq2F, geluS=geluF, mrow=mrowF,
              mrow2=mrow2F, act=actF, act_d=act_d, cslab=cslabF,
              w2_w=w2_w, skip_x=False)
    emit_mlp(P2)

    prog.emit("sync", lambda s: s.dma_start(
        out=x1t_d.rearrange("p (a b) -> p a b", a=8),
        in_=x1.rearrange("p (a b) -> p a b", a=8)
        [:, :, 128 + TOK - 520:128 + TOK]),
        waits=[(sDV, None)], inc=sSp, inc_amt=16)
    prog.emit("gpsimd", lambda g: g.collective_compute(
        "AllGather", ALU.bypass, replica_groups=[list(range(8))],
        ins=[x1t_d], outs=[x1ta_d]), waits=[(sSp, None)], inc=sCC)
    if FUSED_DEBUG:
        dbg_o2 = dt_out("dbg_o2", [D, TOK])
        prog.wait("sync", sGP)
        prog.wait("sync", sOut)
        prog.emit("sync", lambda s: s.dma_start(out=dbg_o2, in_=out_d),
                  inc=sSp, inc_amt=16)
        prog.wait("gpsimd", sSp)
        dbg_x1f = dt_out("dbg_x1f", [128, 8 * (TOK + 128)], BF16)
        prog.emit("sync", lambda s: s.dma_start(
            out=dbg_x1f, in_=x1[:, :]), waits=[(sDV, None)],
            inc=sSp, inc_amt=16)
        prog.wait("vector", sSp)
    wqkv_w = wqkv2_w; wo_w = wo2_w; w1_w = w12_w; b1_w = b12_w
    w2_w = w22_w; b2_w = b22_w; lng_w = lng2_w; lnb_w = lnb2_w
    qs_in = qs2_in; ks_in = ks2_in
    KT = TOK + 640
    x1full = pv(pool, XN_AT, 128, 8 * XT, BF16)
    for e in ("gpsimd", "sync"):
        for sm in (sPE, sSC, sDV, sGP):
            prog.wait(e, sm)
    prog.wait("sync", sOut)
    prog.emit("sync", lambda s: s.dma_start(
        out=x1full.rearrange("p (a b) -> p a b", a=8)[:, :, HALO - 128:XT],
        in_=x1.rearrange("p (a b) -> p a b", a=8)), inc=sU, inc_amt=16)
    tl0 = pv(pool, RB_AT, 128, 8 * 520, BF16)
    tl1 = pv(pool, RB_AT + 8 * 520 * 2, 128, 8 * 520, BF16)
    selvc = pv(pool, RB_AT + 2 * (8 * 520 * 2), 128, 8, F32)
    prog.emit("sync", lambda s: s.dma_start(out=selvc[:, :], in_=selv_in),
              waits=[(sPE, None), (sSC, None), (sDV, None)],
              inc=sU, inc_amt=16)
    ms_dv = 0
    for j in range(8):
        tl = tl0 if j % 2 == 0 else tl1
        tj = prog.emit("sync", lambda s, j=j, tl=tl: s.dma_start(
            out=tl.rearrange("p (a b) -> p a b", a=8), in_=x1ta_d[j]),
            waits=[(sCC, 3), (sDV, ms_dv)], inc=sU, inc_amt=16)
        prog.wait("vector", sU, tj)
        if j == 0:
            ms_dv = prog.emit("vector", lambda v, tl=tl: v.tensor_scalar(
                x1full.rearrange("p (a b) -> p a b", a=8)[:, :, 0:520],
                tl.rearrange("p (a b) -> p a b", a=8),
                selvc[:, 0:1], None, ALU.mult), inc=sDV)
        else:
            ms_dv = prog.emit("vector", lambda v, j=j, tl=tl:
                              v.scalar_tensor_tensor(
                x1full.rearrange("p (a b) -> p a b", a=8)[:, :, 0:520],
                tl.rearrange("p (a b) -> p a b", a=8),
                selvc[:, j:j + 1],
                x1full.rearrange("p (a b) -> p a b", a=8)[:, :, 0:520],
                ALU.mult, ALU.add), inc=sDV)
    LA = Layout(pool)
    LA.cur = HDR
    cslabF = LA.add("cslabF", 128, 8 * 512, F32)
    wqkv_sb = LA.add("wqkv_sb", 128, 8 * 1536, BF16)
    wo_sb = LA.add("wo_sb", 128, 4 * 1024, BF16)
    tbs = [LA.add(f"tb{i}", 128, NKB * 64, BF16) for i in range(4)]
    masksF = LA.add("masksF", 128, 6 * 640, BF16)
    qsrow = LA.add("qsrow", 1, 64, F32)
    ksrow = LA.add("ksrow", 1, 64, F32)
    csrowF = LA.add("csrowF", 1, 256, F32)
    assert LA.cur <= X1_AT, LA.cur
    x2 = pv(pool, X1_AT, 128, 8 * (TOK + 128), BF16)
    LB = Layout(pool)
    LB.cur = RB_AT
    kfm = LB.add("kfm", 128, 4 * KT, BF16)
    Vb = LB.add("Vb", 128, 6 * 512, BF16)
    Oring = LB.add("Oring", 128, 6 * 512, BF16)
    qsl = LB.add("qsl", 128, 2 * 512, BF16)
    kprep = LB.add("kprep", 128, 512, F32)
    qprep = LB.add("qprep", 128, 512, F32)
    rexpF = LB.add("rexpF", 128, 512, F32)
    rot1F = LB.add("rot1F", 128, 512, F32)
    sqbF = LB.add("sqbF", 128, 512, BF16)
    nrmF = LB.add("nrmF", 128, 16, F32)
    AparF = LB.add("AparF", 128, 2 * 640, BF16)
    osbF = LB.add("osbF", 128, 256, F32)
    assert LB.cur <= XN_AT, LB.cur

    for kk in range(8):
        prog.emit("gpsimd", lambda g, kk=kk, _w=wqkv_w, _d=wqkv_sb:
                  g.dma_start(
            out=_d[:, kk * 1536:(kk + 1) * 1536],
            in_=_w[128 * kk:128 * (kk + 1), :]), inc=sW, inc_amt=16)
    for kk in range(4):
        prog.emit("gpsimd", lambda g, kk=kk, _w=wo_w, _d=wo_sb:
                  g.dma_start(
            out=_d[:, kk * 1024:(kk + 1) * 1024],
            in_=_w[128 * kk:128 * (kk + 1), :]), inc=sW, inc_amt=16)
    for ti in range(4):
        prog.emit("gpsimd", lambda g, ti=ti, _d=tbs[ti]: g.dma_start(
            out=_d.rearrange("p (a b) -> p a b", a=NKB),
            in_=rotab.rearrange("q (a p) b -> q p a b", p=128)[ti]),
            inc=sW, inc_amt=16)
    prog.emit("gpsimd", lambda g, _d=masksF: g.dma_start(
        out=_d.rearrange("p (a b) -> p a b", a=6),
        in_=masks_in.transpose([1, 0, 2])), inc=sW, inc_amt=16)
    prog.emit("sync", lambda s, _w=qs_in, _d=qsrow: s.dma_start(
        out=_d[0:1, :], in_=_w), inc=sU, inc_amt=16)
    prog.emit("sync", lambda s, _w=ks_in, _d=ksrow: s.dma_start(
        out=_d[0:1, :], in_=_w), inc=sU, inc_amt=16)

    P = dict(nc=nc, prog=prog, sPE=sPE, sDV=sDV, sSC=sSC, sGP=sGP, sW=sW,
             sU=sU, sSp=sSp, sOut=sOut, PS_A=PS_A, PS_B=PS_B, PS_S=PS_S,
             PS_T=PS_T, PS_U=PS_U, src=x1full, xout=x2, wqkv_sb=wqkv_sb,
             wo_sb=wo_sb, tb=tbs, masks=masksF, qsrow=qsrow, ksrow=ksrow,
             kfm=kfm, Vb=Vb, Oring=Oring, qsl=qsl, kprep=kprep, qprep=qprep,
             rexp=rexpF, rot1=rot1F, sqb=sqbF, nrm=nrmF, Apar=AparF,
             csrow=csrowF, cslab=cslabF, out_d=out_d, oneb=oneb, onef=onef,
             idb=idb, osb=osbF, tag="2")
    P["_acc_gp"] = acc_gp
    import os as _os
    P["stage"] = _os.environ.get("F1A_STAGE", "all")
    emit_attn(P)

    LM2 = Layout(pool)
    LM2.cur = LA.map["wqkv_sb"][0]
    hlnF = LM2.add("hlnF", 128, 8 * 512, BF16)
    sq2F = LM2.add("sq2F", 128, 8 * 512, BF16)
    geluF = LM2.add("geluF", 128, 512, F32)
    mrowF = LM2.add("mrowF", 1, 512, F32)
    mrow2F = LM2.add("mrow2F", 1, 512, F32)
    b1cF = LM2.add("b1cF", 128, 44, F32)
    b2cF = LM2.add("b2cF", 128, 8, F32)
    lngF = LM2.add("lngF", 128, 8, F32)
    lnbF = LM2.add("lnbF", 128, 8, F32)
    actF = LM2.add("actF", 128, 22 * 512, BF16)
    assert LM2.cur <= X1_AT, LM2.cur
    w1p = pv(pool, RB_AT, 128, 8 * 5632, BF16)
    w2p = pv(pool, RB_AT, 128, 22 * 1024, BF16)
    for kk in range(8):
        prog.emit("gpsimd", lambda g, kk=kk, _w=w1_w, _d=w1p: g.dma_start(
            out=_d[:, kk * 5632:(kk + 1) * 5632],
            in_=_w[128 * kk:128 * (kk + 1), :]),
            waits=([(sPE, None), (sSC, None), (sDV, None), (sSp, None)]
                   if kk == 0 else ()), inc=sW, inc_amt=16)
    for (dst, srcd) in ((b1cF, b1_w), (b2cF, b2_w), (lngF, lng_w),
                        (lnbF, lnb_w)):
        prog.emit("sync", lambda s, d=dst, r=srcd: s.dma_start(
            out=d[:, :], in_=r),
            waits=[(sPE, None), (sSC, None), (sDV, None)],
            inc=sU, inc_amt=16)
    P2 = dict(P)
    P2.update(xio=x2, w1p=w1p, w2p=w2p, b1c=b1cF, b2c=b2cF, lngc=lngF,
              lnbc=lnbF, hln=hlnF, sq2=sq2F, geluS=geluF, mrow=mrowF,
              mrow2=mrow2F, act=actF, act_d=act_d, cslab=cslabF,
              w2_w=w2_w, skip_x=True)
    emit_mlp(P2)

    if FUSED_DEBUG:
        dbg_xf = dt_out("dbg_xf", [128, 8 * XT], BF16)
        for _s in (sPE, sSC, sDV, sGP):
            prog.wait("sync", _s)
        prog.emit("sync", lambda s: s.dma_start(
            out=dbg_xf, in_=x1full[:, :]), inc=sSp, inc_amt=16)
    prog.drain("gpsimd")
    fin_gp = prog.emit("gpsimd", lambda g: g.nop(), inc=sGP)
    prog.wait("sync", sGP, fin_gp)
    for sm in (sW, sU, sSp, sOut):
        prog.wait("sync", sm)
    with nc.Block() as blk:
        prog.run(blk)
    ctx.close()
    return nc

_IOTA2 = None


def _core_inputs_l1(inputs, c):
    b, p = c // 4, c % 4
    S = TOK * p
    u = inputs["L0_attn_u"][b]
    useg = np.zeros((XT, D), np.float32)
    lo = max(0, S - HALO)
    useg[HALO - (S - lo):HALO + TOK] = u[lo:S + TOK]
    w1p, b1p = _pad_w1(inputs["L0_ffn_w1"][0], inputs["L0_ffn_b1"][0])
    cw = np.zeros((128, 64), np.float32)
    cwi = inputs["L0_attn_conv_w"]
    for fc in range(16):
        for k in range(4):
            cw[:, 4 * fc + k] = cwi[128 * fc:128 * (fc + 1), k]
    mtiles = _attn_mask_tiles(p)
    return {
        "u_t": np.ascontiguousarray(useg.T),
        "W_in": np.asarray(inputs["L0_attn_W_in"], np.float32),
        "conv_w": cw,
        "conv_b": _cols(inputs["L0_attn_conv_b"]),
        "delta_p": np.asarray(inputs["L0_attn_delta_param"],
                              np.float32).reshape(16, 1),
        "ident_f": np.eye(128, dtype=np.float32),
        "wqkv": np.asarray(inputs["L0_ffn_wqkv"][0], np.float32),
        "wo_w": np.asarray(inputs["L0_ffn_wo"][0], np.float32),
        "qs": np.asarray(inputs["L0_ffn_q_scale"][0],
                         np.float32).reshape(1, 64),
        "ks": np.asarray(inputs["L0_ffn_k_scale"][0],
                         np.float32).reshape(1, 64),
        "rotab": np.ascontiguousarray(_rotary_tables(S - 640, TOK + 640)),
        "masks": mtiles,
        "w1_w": w1p,
        "b1_w": _cols(b1p),
        "w2_w": _pad_w2(inputs["L0_ffn_w2"][0]),
        "b2_w": _cols(inputs["L0_ffn_b2"][0]),
        "lng": _cols(inputs["L0_ffn_ln_g"][0]),
        "lnb": _cols(inputs["L0_ffn_ln_b"][0]),
    }


def _core_inputs_l2(inputs, c, r1, m1):
    b, p = c // 4, c % 4
    S = TOK * p
    Gin = np.zeros((3, DS, NH * DH), np.float32)
    lamf = np.zeros((1, 48), np.float32)
    for j in range(p):
        pred = 4 * b + j
        Gin[3 - p + j] = r1[pred]["G_sp"].reshape(DS, NH * DH)
        lamf[0, 16 * (3 - p + j):16 * (3 - p + j) + 16] = \
            r1[pred]["lam_sp"].reshape(16, TOK + 1)[:, TOK]
    if p == 0:
        x1h = np.zeros((D, HALO), dtype=bf16)
    else:
        x1h = np.ascontiguousarray(
            r1[c - 1]["x1_sp"].reshape(D, TOK)[:, TOK - HALO:])
    d = {
        "out_init": r1[c]["out_part"].reshape(D, TOK),
        "xn_sp": r1[c]["xn_sp"].reshape(D, XT),
        "x1_sp": r1[c]["x1_sp"].reshape(D, TOK),
        "x1_halo": x1h,
        "lam_sp": r1[c]["lam_sp"].reshape(16, TOK + 1),
        "Xtm_sp": r1[c]["Xtm_sp"].reshape(NCH, 128, NH * DH),
        "Btm_sp": r1[c]["Btm_sp"].reshape(NCH, 128, NH * DS),
        "Bfm_sp": r1[c]["Bfm_sp"].reshape(8, 128, TOK),
        "Cfm_sp": r1[c]["Cfm_sp"].reshape(8, 128, TOK),
        "Gin": Gin,
        "lamf": lamf,
        "W_gate": np.asarray(inputs["L0_attn_W_gate"], np.float32),
        "W_out": np.asarray(inputs["L0_attn_W_out"], np.float32),
        "norm_w": _cols(inputs["L0_attn_norm_w"]),
        "wqkv": np.asarray(inputs["L0_ffn_wqkv"][1], np.float32),
        "wo_w": np.asarray(inputs["L0_ffn_wo"][1], np.float32),
        "qs": np.asarray(inputs["L0_ffn_q_scale"][1],
                         np.float32).reshape(1, 64),
        "ks": np.asarray(inputs["L0_ffn_k_scale"][1],
                         np.float32).reshape(1, 64),
        "w1_w": m1["w1p2"],
        "b1_w": m1["b1p2"],
        "w2_w": m1["w2p2"],
        "b2_w": _cols(inputs["L0_ffn_b2"][1]),
        "lng": _cols(inputs["L0_ffn_ln_g"][1]),
        "lnb": _cols(inputs["L0_ffn_ln_b"][1]),
        "rotab": m1["rotab"][p],
        "masks": m1["masks"][p],
        "ident_f": np.eye(128, dtype=np.float32),
        "iota2": np.stack([np.arange(128, dtype=np.float32),
                           np.full(128, 128.0, np.float32)], 1),
        "dmask": np.where(np.arange(128)[None, :] >= np.arange(128)[:, None],
                          0.0, -1.0e30).astype(np.float32),
    }
    return d


def _core_inputs_fused(inputs, c):
    b, p = c // 4, c % 4
    d = _core_inputs_l1(inputs, c)
    w1p2, b1p2 = _pad_w1(inputs["L0_ffn_w1"][1], inputs["L0_ffn_b1"][1])
    ohv = np.zeros((64, 8), np.float32)
    selv = np.zeros((128, 8), np.float32)
    if p > 0:
        ohv[:, c - 1] = 1.0
        selv[:, c - 1] = 1.0
    d.update({
        "W_gate": np.asarray(inputs["L0_attn_W_gate"], np.float32),
        "W_out": np.asarray(inputs["L0_attn_W_out"], np.float32),
        "norm_w": _cols(inputs["L0_attn_norm_w"]),
        "wqkv2": np.asarray(inputs["L0_ffn_wqkv"][1], np.float32),
        "wo2_w": np.asarray(inputs["L0_ffn_wo"][1], np.float32),
        "qs2": np.asarray(inputs["L0_ffn_q_scale"][1],
                          np.float32).reshape(1, 64),
        "ks2": np.asarray(inputs["L0_ffn_k_scale"][1],
                          np.float32).reshape(1, 64),
        "w1_w2": w1p2,
        "b1_w2": _cols(b1p2),
        "w2_w2": _pad_w2(inputs["L0_ffn_w2"][1]),
        "b2_w2": _cols(inputs["L0_ffn_b2"][1]),
        "lng2": _cols(inputs["L0_ffn_ln_g"][1]),
        "lnb2": _cols(inputs["L0_ffn_ln_b"][1]),
        "dmask": np.where(np.arange(128)[None, :] >= np.arange(128)[:, None],
                          0.0, -1.0e30).astype(np.float32),
        "ohv": ohv,
        "selv": selv,
    })
    return d


def kernel(**inputs):
    ins = [_core_inputs_fused(inputs, c) for c in range(NCORES)]
    ncp = build_fused()
    res = run_bass_kernel_spmd(ncp, ins, core_ids=list(range(NCORES)))
    out = np.zeros((B, L, D), np.float32)
    for c in range(NCORES):
        b, p = c // 4, c % 4
        out[b, TOK * p: TOK * (p + 1)] = res.results[c]["out"].reshape(
            D, TOK).T
    return out
